# version 18
# baseline (speedup 1.0000x reference)
import math
from contextlib import ExitStack

import numpy as np
import concourse.bass as bass
import concourse.mybir as mybir
from concourse.bass import IndirectOffsetOnAxis
from concourse.bass_utils import run_bass_kernel_spmd

F32 = mybir.dt.float32
BF16 = mybir.dt.bfloat16
FP8 = mybir.dt.float8e4
I32 = mybir.dt.int32
AF = mybir.ActivationFunctionType
ALU = mybir.AluOpType
AX = mybir.AxisListType

S, T, H = 2048, 1024, 2048
DOWN, UP, NH, RD, VD, FI = 512, 1024, 16, 64, 64, 5632
SC_ATT = 1.0 / math.sqrt(192.0)
EPS = 1e-5
NHC, NKC, NTC, NIC, NDC = H // 128, S // 128, T // 128, FI // 128, DOWN // 128
VSTR = NH * 65

ENGS = ("sync", "vector", "scalar", "tensor", "gpsimd")
DEBUG = False
PH_LIM = 99


class _Cut(Exception):
    pass


class Sem:
    def __init__(self, es, nc, name):
        self.h = es.enter_context(nc.semaphore(name))
        self.n = 0


class KB:

    def __init__(self, nc, es):
        self.nc = nc
        self.es = es
        self.q = {k: [] for k in ENGS}
        self._semid = 0
        self.bank = {}
        self.wset = {e: set() for e in ENGS}
        self.rset = {e: set() for e in ENGS}

    def sem(self, name=None):
        self._semid += 1
        return Sem(self.es, self.nc, name or f"sm{self._semid}")

    def emit(self, eng, fn):
        self.q[eng].append(fn)

    def wait(self, eng, sem, val=None):
        v = sem.n if val is None else val
        if v > 0:
            self.emit(eng, lambda e: e.wait_ge(sem.h, v))

    def wait_bank(self, eng, pp):
        ent = self.bank.get(id(pp))
        if ent is not None:
            self.wait(eng, ent[0], ent[1])

    def set_bank(self, pp, sem):
        self.bank[id(pp)] = (sem, sem.n)

    def dma(self, eng, out, in_, sem, drain=False, **kw):
        sem.n += 16
        if eng in ("vector", "scalar") and not drain:
            sid = self._tid(in_)
            if sid is not None and sid in self.wset[eng]:
                drain = True
        if drain:
            if eng in ("vector", "scalar"):
                self.wset[eng].clear()
                self.rset[eng].clear()
            self.emit(eng, lambda e: e.drain())
        self.emit(eng, lambda e: e.dma_start(out=out, in_=in_, **kw).then_inc(sem.h, 16))
        return sem.n

    def dmaT(self, eng, out, in_, sem):
        sem.n += 16
        self.emit(eng, lambda e: e.dma_start_transpose(out=out, in_=in_).then_inc(sem.h, 16))
        return sem.n

    def gather(self, out, in_, idx_ap, sem):
        sem.n += 16
        self.emit("gpsimd", lambda e: e.indirect_dma_start(
            out=out, out_offset=None, in_=in_,
            in_offset=IndirectOffsetOnAxis(idx_ap, axis=0)).then_inc(sem.h, 16))
        return sem.n

    @staticmethod
    def _tid(x):
        t = getattr(x, "tensor", None)
        return id(t) if t is not None else None

    def _hazard(self, eng, outs, ins):
        w, r = self.wset[eng], self.rset[eng]
        need = (not ins.isdisjoint(w)) or (not outs.isdisjoint(w)) \
            or (not outs.isdisjoint(r))
        if need:
            w.clear()
            r.clear()
        w |= outs
        r |= ins
        return need

    def op(self, eng, meth, *a, sem=None, **kw):
        pre = False
        if eng in ("vector", "scalar"):
            outs = {self._tid(a[0])} - {None}
            ins = {self._tid(x) for x in a[1:]} | {self._tid(x) for x in kw.values()}
            ins -= {None}
            pre = self._hazard(eng, outs, ins)
        if sem is None:
            if pre:
                self.emit(eng, lambda e: (e.drain(), getattr(e, meth)(*a, **kw)))
            else:
                self.emit(eng, lambda e: getattr(e, meth)(*a, **kw))
            return None
        sem.n += 1
        if pre:
            self.emit(eng, lambda e: (e.drain(), getattr(e, meth)(*a, **kw).then_inc(sem.h, 1)))
        else:
            self.emit(eng, lambda e: getattr(e, meth)(*a, **kw).then_inc(sem.h, 1))
        return sem.n

    def flush(self, blk):
        for name in ENGS:
            fns = self.q[name]
            if not fns:
                continue
            section = getattr(blk, name)

            def body(e, fns=fns):
                for f in fns:
                    f(e)
            section(body)


def build_nc():
    nc = bass.Bass("TRN2", target_bir_lowering=False, debug=False)
    D = lambda name, shape, dt, kind="ExternalInput": nc.dram_tensor(name, shape, dt, kind=kind).ap()

    d_h_own = D("h_own", [T, H], F32)
    d_h_oth = D("h_oth", [T, H], F32)
    d_wdkvT = D("wdkvT", [H, DOWN], F32)
    d_wdqT = D("wdqT", [H, DOWN], F32)
    d_wupkT = D("wupkT", [DOWN, UP], F32)
    d_wupvT = D("wupvT", [DOWN, UP], F32)
    d_wqcomb = D("wqcomb", [DOWN, 2048], F32)
    d_wkrT = D("wkrT", [H, RD], F32)
    d_wfcT = D("wfcT", [UP, H], F32)
    d_bdkv = D("bdkv", [DOWN], F32)
    d_bupk = D("bupk", [UP], F32)
    d_bupv = D("bupv", [1, UP], F32)
    d_bdq = D("bdq", [DOWN], F32)
    d_bqcomb = D("bqcomb", [2048], F32)
    d_bkr = D("bkr", [RD], F32)
    d_bfc = D("bfc", [1, H], F32)
    d_wg = D("wg_nat", [FI, H], F32)
    d_wu = D("wu_nat", [FI, H], F32)
    d_wd = D("wd_nat", [H, FI], F32)
    d_didx = D("didx", [FI], I32)
    d_ts = D("tabS", [128, S], F32)
    d_tcs = D("tabCsgn", [128, S], F32)
    d_out = D("y", [T, H], F32, "ExternalOutput")

    d_wg_sc = D("wg_sc", [FI, H], BF16, "Internal")
    d_wu_sc = D("wu_sc", [FI, H], BF16, "Internal")
    d_wd_sc = D("wd_sc", [H, FI], BF16, "Internal")
    d_t_sc = D("t_sc", [FI, T], F32, "Internal")
    d_qxf_sc = D("qxf_sc", [NHC * 128, T], FP8, "Internal")
    d_fc_sc = D("fc_sc", [T, H], F32, "Internal")
    d_rs1 = D("rs1", [128, 16], F32, "Internal")
    if DEBUG:
        d_dbg_attn = D("dbg_attn", [128, 8 * T], BF16, "Internal")
        d_dbg_q = D("dbg_q", [128, NH * T], BF16, "Internal")
        d_dbg_k = D("dbg_k", [128, 8 * S], BF16, "Internal")
        d_dbg_kr = D("dbg_kr", [128, S], BF16, "Internal")
        d_dbg_v = D("dbg_v", [128, NKC * VSTR], BF16, "Internal")
        d_dbg_ckv = D("dbg_ckv", [128, NDC * S], BF16, "Internal")
        d_dbg_cq = D("dbg_cq", [128, NDC * T], BF16, "Internal")

    es = ExitStack()
    sb = lambda n, shape, dt, **kw: es.enter_context(nc.sbuf_tensor(n, shape, dt, **kw))
    kb = KB(nc, es)

    t_id = sb("ident", [128, 128], F32)
    t_idb = sb("identb", [128, 128], BF16)
    t_ones = sb("onesr", [1, 128], BF16)
    t_onesF = sb("onesrF", [1, 128], F32)
    t_bvrow = sb("bvrow", [1, UP], BF16)
    t_b1 = sb("b1", [128, 20], F32)
    t_bq = sb("bq", [128, 16], F32)
    t_didx = sb("didxs", [128, NIC], I32)
    t_xam = sb("xam", [128, NTC], F32)
    t_gfac = sb("gfac", [128, NIC], F32)
    t_ufac = sb("ufac", [128, NIC], F32)
    t_dfac = sb("dfac", [128, NHC], F32)
    t_tamr = sb("tamr", [128, NTC], F32)
    t_row1 = sb("row1", [1, 512], F32)
    t_rtmp = sb("rtmp", [128, 6], F32)
    t_pwf = [sb("pwf0", [128, 1024], F32), sb("pwf1", [128, 1024], F32)]
    t_pw8 = [sb("pw8a", [128, 1024], FP8)] * 2
    t_pwb = sb("pwba", [128, 1024], BF16)
    t_pwt = sb("pwt", [128, 6], F32)

    p_es = ExitStack()
    p_a = p_es.enter_context(nc.psum_tensor("p_a", [128, 512], F32))
    p_b = p_es.enter_context(nc.psum_tensor("p_b", [128, 512], F32))
    p_c = p_es.enter_context(nc.psum_tensor("p_c", [128, 512], F32))
    p_d = p_es.enter_context(nc.psum_tensor("p_d", [128, 512], F32))
    p_e = p_es.enter_context(nc.psum_tensor("p_e", [128, 1024], F32))
    p_f = p_es.enter_context(nc.psum_tensor("p_f", [128, 1024], F32))

    s_tab = kb.sem("s_tab")
    s_h = [kb.sem("s_h0"), kb.sem("s_h1")]
    s_xw = [kb.sem("s_xw0"), kb.sem("s_xw1")]
    s_xt = kb.sem("s_xt")
    s_wa = kb.sem("s_wa")
    s_tb = kb.sem("s_tb")
    s_wf = [kb.sem("s_wf0"), kb.sem("s_wf1")]
    s_pwv = kb.sem("s_pwv")
    s_pwg = kb.sem("s_pwg")
    s_wqw = kb.sem("s_wqw")
    s_wt = [kb.sem(f"s_wt{i}") for i in range(8)]
    s_wdt = [kb.sem("s_wdt0"), kb.sem("s_wdt1")]
    s_qtw = kb.sem("s_qtw")
    s_tw = kb.sem("s_tw")
    s_tr = [kb.sem("s_tr0"), kb.sem("s_tr1")]
    s_qxw = kb.sem("s_qxw")
    s_qxl = kb.sem("s_qxl")
    s_qxst = [kb.sem("s_qxst0"), kb.sem("s_qxst1")]
    s_qtg = [kb.sem("s_qtg0"), kb.sem("s_qtg1")]
    s_fcw = kb.sem("s_fcw")
    s_fcr = kb.sem("s_fcr")
    s_hor = kb.sem("s_hor")
    s_out = kb.sem("s_out")
    s_mv = kb.sem("s_mv")
    s_rsw = kb.sem("s_rsw")
    s_kr2 = kb.sem("s_kr2")
    s_ash = kb.sem("s_ash")
    s_v = kb.sem("s_v")
    s_s = kb.sem("s_s")
    s_t = kb.sem("s_t")
    s_g = kb.sem("s_g")

    W = kb.wait
    OP = kb.op

    _stacks = []

    def reg(st):
        _stacks.append(st)
        return st

    def unreg(st):
        st.close()
        if st in _stacks:
            _stacks.remove(st)

    def cut(k):
        if PH_LIM <= k:
            raise _Cut()

    try:
        WD_SEG = (2048, 2048, 1536)
        pw_state = {"q": 0, "done": False, "wd_scale_v": 0}
        pw_wq_gate = {}
        pw_wq_up = {}
        pw_wq_wd = {}
        pw_wqw = [kb.sem("s_pww0"), kb.sem("s_pww1")]
        pw_fac_v_slot = [0, 0]

        NQ_PW = 184
        pw_rd = [0, 0]
        pw_rdv = [0, 0]
        pw_ld = [0, 0]
        pw_tpwt_rd = [0]
        pw_store = [None, 0]

        def pw_geom(q):
            if q < 88:
                c = q // 2
                dram_in = d_wg if q % 2 == 0 else d_wu
                return [(dram_in[c * 128:(c + 1) * 128, 0:1024], 1024),
                        (dram_in[c * 128:(c + 1) * 128, 1024:2048], 1024)]
            qq = q - 88
            c = qq // 6
            step = qq % 6
            o0 = sum(WD_SEG[:step % 3])
            w = WD_SEG[step % 3]
            return [(d_wd[c * 128:(c + 1) * 128, o0:o0 + 1024], 1024),
                    (d_wd[c * 128:(c + 1) * 128, o0 + 1024:o0 + w], w - 1024)]

        def pw_issue(q):
            for sl, (src, w) in enumerate(pw_geom(q)):
                if pw_rd[sl]:
                    kb.wait("scalar", s_pwg, pw_rd[sl])
                if pw_rdv[sl]:
                    kb.wait("scalar", s_pwv, pw_rdv[sl])
                kb.dma("scalar", t_pwf[sl][:, 0:w], src, s_wf[sl])
                pw_ld[sl] = s_wf[sl].n

        def pw_cast_store(hf, w, dst_ap, scale_col, wsl):
            OP("scalar", "activation", t_pw8[0][:, 0:w], t_pwf[hf][:, 0:w],
               AF.Copy, scale=scale_col, sem=s_pwg)
            kb.wait("vector", s_pwg, s_pwg.n)
            if pw_store[0] is not None:
                kb.wait("vector", pw_store[0], pw_store[1])
            OP("vector", "tensor_copy", t_pwb[:, 0:w], t_pw8[0][:, 0:w], sem=s_pwv)
            kb.wait("scalar", s_pwv, s_pwv.n)
            kb.dma("scalar", dst_ap, t_pwb[:, 0:w], pw_wqw[wsl])
            pw_store[0] = pw_wqw[wsl]
            pw_store[1] = pw_wqw[wsl].n

        def pw_quantum():
            q = pw_state["q"]
            if q >= NQ_PW:
                pw_state["done"] = True
                return
            pw_state["q"] += 1
            wsl = q % 2
            if q == 0:
                pw_issue(0)
            halves = pw_geom(q)
            if q < 88:
                mat = "g" if q % 2 == 0 else "u"
                c = q // 2
                dram_sc = d_wg_sc if mat == "g" else d_wu_sc
                fac = t_gfac if mat == "g" else t_ufac
                kb.wait("vector", s_wf[0], pw_ld[0])
                if pw_tpwt_rd[0]:
                    kb.wait("vector", s_pwg, pw_tpwt_rd[0])
                OP("vector", "tensor_reduce", t_pwt[:, 0:1], t_pwf[0][:, :], AX.X, ALU.max,
                   apply_absolute_value=True)
                kb.wait("vector", s_wf[1], pw_ld[1])
                OP("vector", "tensor_reduce", t_pwt[:, 5:6], t_pwf[1][:, :], AX.X, ALU.max,
                   apply_absolute_value=True)
                OP("vector", "tensor_tensor", t_pwt[:, 0:1], t_pwt[:, 0:1], t_pwt[:, 5:6],
                   ALU.max)
                OP("vector", "tensor_scalar", fac[:, c:c + 1], t_pwt[:, 0:1], 1.0 / 448.0,
                   None, ALU.mult)
                OP("vector", "reciprocal", t_pwt[:, 1:2], t_pwt[:, 0:1])
                OP("vector", "tensor_scalar", t_pwt[:, 2:3], t_pwt[:, 1:2], 224.0,
                   None, ALU.mult, sem=s_pwv)
                pw_rdv[0] = pw_rdv[1] = s_pwv.n
                kb.wait("scalar", s_pwv, s_pwv.n)
                for hf in range(2):
                    pw_cast_store(hf, 1024,
                                  dram_sc[c * 128:(c + 1) * 128, hf * 1024:(hf + 1) * 1024],
                                  t_pwt[:, 2:3], wsl)
                pw_rd[0] = pw_rd[1] = s_pwg.n
                pw_tpwt_rd[0] = s_pwg.n
                (pw_wq_gate if mat == "g" else pw_wq_up)[c] = (wsl, pw_wqw[wsl].n)
            else:
                qq = q - 88
                c = qq // 6
                step = qq % 6
                o0 = sum(WD_SEG[:step % 3])
                w = WD_SEG[step % 3]
                if step < 3:
                    kb.wait("vector", s_wf[0], pw_ld[0])
                    if pw_tpwt_rd[0]:
                        kb.wait("vector", s_pwg, pw_tpwt_rd[0])
                    OP("vector", "tensor_reduce", t_pwt[:, 0:1], t_pwf[0][:, :], AX.X,
                       ALU.max, apply_absolute_value=True)
                    kb.wait("vector", s_wf[1], pw_ld[1])
                    OP("vector", "tensor_reduce", t_pwt[:, 5:6], t_pwf[1][:, 0:w - 1024],
                       AX.X, ALU.max, apply_absolute_value=True)
                    OP("vector", "tensor_tensor", t_pwt[:, 0:1], t_pwt[:, 0:1],
                       t_pwt[:, 5:6], ALU.max)
                    if step == 0:
                        OP("vector", "tensor_copy", t_pwt[:, 1:2], t_pwt[:, 0:1], sem=s_pwv)
                    else:
                        OP("vector", "tensor_tensor", t_pwt[:, 1:2], t_pwt[:, 1:2],
                           t_pwt[:, 0:1], ALU.max, sem=s_pwv)
                    if step == 2:
                        OP("vector", "tensor_scalar", t_dfac[:, c:c + 1], t_pwt[:, 1:2],
                           1.0 / 448.0, None, ALU.mult)
                        OP("vector", "reciprocal", t_pwt[:, 1:2], t_pwt[:, 1:2])
                        OP("vector", "tensor_scalar", t_pwt[:, 4:5], t_pwt[:, 1:2], 224.0,
                           None, ALU.mult, sem=s_pwv)
                        pw_state["wd_scale_v"] = s_pwv.n
                    pw_rdv[0] = pw_rdv[1] = s_pwv.n
                else:
                    kb.wait("scalar", s_wf[0], pw_ld[0])
                    kb.wait("scalar", s_wf[1], pw_ld[1])
                    kb.wait("scalar", s_pwv, pw_state["wd_scale_v"])
                    for hf, hw_ in ((0, 1024), (1, w - 1024)):
                        pw_cast_store(hf, hw_,
                                      d_wd_sc[c * 128:(c + 1) * 128,
                                              o0 + hf * 1024:o0 + hf * 1024 + hw_],
                                      t_pwt[:, 4:5], wsl)
                    pw_rd[0] = pw_rd[1] = s_pwg.n
                    pw_tpwt_rd[0] = s_pwg.n
                    if step == 5:
                        pw_wq_wd[c] = (wsl, pw_wqw[wsl].n)
            if q + 1 < NQ_PW:
                pw_issue(q + 1)

        def pw_steps(n):
            for _ in range(n):
                if not pw_state["done"]:
                    pw_quantum()

        kb.dma("sync", t_b1[:, 0:4], d_bdkv.rearrange("(c p) -> p c", p=128), s_tab, allow_slow_non_contiguous=True)
        kb.dma("sync", t_b1[:, 4:8], d_bdq.rearrange("(c p) -> p c", p=128), s_tab, allow_slow_non_contiguous=True)
        kb.dma("sync", t_b1[:, 8:16], d_bupk.rearrange("(c p) -> p c", p=128), s_tab, allow_slow_non_contiguous=True)
        kb.dma("sync", t_b1[0:64, 16:17], d_bkr.rearrange("(p c) -> p c", c=1), s_tab)
        kb.dma("sync", t_bq[:, :], d_bqcomb.rearrange("(c p) -> p c", p=128), s_tab, allow_slow_non_contiguous=True)
        kb.dma("sync", t_didx[:, :], d_didx.rearrange("(c p) -> p c", p=128), s_tab, allow_slow_non_contiguous=True)
        kb.dma("gpsimd", t_bvrow[0:1, :], d_bupv[0:1, :], s_tb)
        TAB0 = s_tab.n
        OP("vector", "memset", t_id[:, :], 1.0, sem=s_v)
        OP("vector", "memset", t_ones[:, :], 1.0, sem=s_v)
        OP("vector", "memset", t_onesF[:, :], 1.0, sem=s_v)
        OP("vector", "memset", t_rtmp[:, 5:6], SC_ATT, sem=s_v)
        W("gpsimd", s_v)
        OP("gpsimd", "affine_select", t_id[:, :], t_id[:, :], pattern=[[1, 128]],
           compare_op=ALU.is_equal, fill=0.0, base=0, channel_multiplier=-1, sem=s_g)
        G_ID = s_g.n
        W("vector", s_g, G_ID)
        OP("vector", "tensor_copy", t_idb[:, :], t_id[:, :], sem=s_v)
        IDB_V = s_v.n


        ph_qx = reg(ExitStack())
        ph2 = reg(ExitStack())
        sb2 = lambda n, shape, dt, **kw: ph2.enter_context(nc.sbuf_tensor(n, shape, dt, **kw))
        t_attn = sb2("attn_fm", [128, 8 * T], BF16)
        t_aren = sb2("arena", [128, 4 * 2048], BF16)
        t_ckv = sb2("ckv_fm", [128, NDC * S], BF16)
        t_cq = sb2("cq_fm", [128, NDC * T], BF16)
        t_tabS = sb2("tabSs", [128, S], BF16)
        t_tabC = sb2("tabCs", [128, S], BF16)
        t_rop = sb2("ropeA", [128, S], BF16)
        t_rsw = sb2("ropeB", [128, T], BF16)
        ph_xn = reg(ExitStack())
        t_xnfm = ph_xn.enter_context(nc.sbuf_tensor("xn_fm", [128, NHC * S], BF16))
        ph1 = reg(ExitStack())
        sb1 = lambda n, shape, dt, **kw: ph1.enter_context(nc.sbuf_tensor(n, shape, dt, **kw))
        t_h = [sb1(f"h{i}", [128, H], F32) for i in range(2)]
        t_h2 = sb1("h2", [128, H], F32)
        t_xns = [sb1(f"xnb{i}", [128, H], BF16) for i in range(2)]
        t_q8tm = [sb1("q8tm0", [128, H], FP8)]
        t_qbtm = [sb1(f"qbtm{i}", [128, H], BF16) for i in range(4)]
        t_qxs = [sb1(f"qxs{i}", [128, 512], FP8) for i in range(2)]
        t_ss = sb1("lnss", [128, 2 * NKC], F32)
        t_rstd = sb1("rstd", [128, NKC], F32)
        t_nb = sb1("nbb", [128, NKC], F32)
        t_nm = sb1("negm", [128, NKC], F32)
        t_lt = sb1("lntmp", [128, NKC], F32)
        t_xmx = sb1("xmx", [128, NTC], F32)
        t_xmn = sb1("xmn", [128, NTC], F32)
        t_qsc = sb1("qsc", [128, NTC], F32)
        t_qbi = sb1("qbi", [128, NTC], F32)

        kb.dma("gpsimd", t_tabS[:, :], d_ts[:, :], s_tb)
        kb.dma("gpsimd", t_tabC[:, :], d_tcs[:, :], s_tb)
        TAB_ROPE = s_tb.n

        def h_src(kc):
            return (d_h_own if kc < NTC else d_h_oth)[(kc % NTC) * 128:(kc % NTC) * 128 + 128, :]

        vdone = {}
        for kc in range(NKC):
            sl = kc % 2
            if kc >= 2:
                W("sync", s_v, vdone[kc - 2])
            kb.dma("sync", t_h[sl][:, :], h_src(kc), s_h[sl])
            W("vector", s_h[sl], s_h[sl].n)
            OP("vector", "tensor_reduce", t_ss[:, 2 * kc:2 * kc + 1], t_h[sl][:, :], AX.X, ALU.add)
            OP("vector", "tensor_tensor", t_h2[:, :], t_h[sl][:, :], t_h[sl][:, :], ALU.mult)
            if kc < NTC:
                OP("vector", "tensor_reduce", t_xmx[:, kc:kc + 1], t_h[sl][:, :], AX.X, ALU.max)
                OP("vector", "tensor_reduce", t_xmn[:, kc:kc + 1], t_h[sl][:, :], AX.X, ALU.min)
            OP("vector", "tensor_reduce", t_ss[:, 2 * kc + 1:2 * kc + 2], t_h2[:, :], AX.X, ALU.add,
               sem=s_v)
            vdone[kc] = s_v.n

        OP("vector", "tensor_scalar", t_nm[:, :], t_ss[:, 0:2 * NKC:2], 1.0 / H, None, ALU.mult)
        OP("vector", "tensor_scalar", t_rstd[:, :], t_ss[:, 1:2 * NKC:2], 1.0 / H, None, ALU.mult)
        OP("vector", "tensor_tensor", t_lt[:, :], t_nm[:, :], t_nm[:, :], ALU.mult)
        OP("vector", "tensor_tensor", t_rstd[:, :], t_rstd[:, :], t_lt[:, :], ALU.subtract)
        OP("vector", "tensor_scalar", t_rstd[:, :], t_rstd[:, :], EPS, None, ALU.add)
        OP("vector", "tensor_copy", t_ss[:, 0:NKC], t_rstd[:, :])
        OP("vector", "reciprocal", t_rstd[:, :], t_rstd[:, :], sem=s_v)
        W("scalar", s_v)
        OP("scalar", "activation", t_rstd[:, :], t_rstd[:, :], AF.Sqrt, sem=s_s)
        W("vector", s_s)
        OP("vector", "tensor_tensor", t_lt[:, :], t_rstd[:, :], t_rstd[:, :], ALU.mult)
        OP("vector", "tensor_tensor", t_lt[:, :], t_lt[:, :], t_ss[:, 0:NKC], ALU.mult)
        OP("vector", "tensor_scalar", t_lt[:, :], t_lt[:, :], -0.5, 1.5, ALU.mult, ALU.add)
        OP("vector", "tensor_tensor", t_rstd[:, :], t_rstd[:, :], t_lt[:, :], ALU.mult)
        OP("vector", "tensor_tensor", t_nb[:, :], t_nm[:, :], t_rstd[:, :], ALU.mult)
        OP("vector", "tensor_scalar", t_nb[:, :], t_nb[:, :], -1.0, None, ALU.mult)
        OP("vector", "tensor_scalar", t_nm[:, :], t_nm[:, :], -1.0, None, ALU.mult, sem=s_v)
        OP("vector", "tensor_tensor", t_xmx[:, :], t_xmx[:, :], t_nm[:, 0:NTC], ALU.add)
        OP("vector", "tensor_tensor", t_xmn[:, :], t_xmn[:, :], t_nm[:, 0:NTC], ALU.add)
        OP("vector", "tensor_scalar", t_xmn[:, :], t_xmn[:, :], -1.0, None, ALU.mult)
        OP("vector", "tensor_tensor", t_xmx[:, :], t_xmx[:, :], t_xmn[:, :], ALU.max)
        OP("vector", "tensor_tensor", t_xam[:, 0:NTC], t_xmx[:, :], t_rstd[:, 0:NTC], ALU.mult)
        OP("vector", "reciprocal", t_xmn[:, :], t_xam[:, 0:NTC])
        OP("vector", "tensor_scalar", t_xmn[:, :], t_xmn[:, :], 224.0, None, ALU.mult)
        OP("vector", "tensor_tensor", t_qsc[:, :], t_rstd[:, 0:NTC], t_xmn[:, :], ALU.mult)
        OP("vector", "tensor_tensor", t_qbi[:, :], t_nb[:, 0:NTC], t_xmn[:, :], ALU.mult,
           sem=s_v)
        V1B = s_v.n

        psum_ro = [p_a, p_b, p_c, p_d]
        pidx = [0]
        W("scalar", s_v, V1B)
        W("tensor", s_v, IDB_V)
        sdone = {}
        xn_done = {}
        qb_copy = {}
        trp_free = [0, 0]
        qtrp_free = [0, 0, 0, 0]
        qxs_st = [0, 0]
        for kc in range(NKC):
            sl = kc % 2
            sx = kc % 4
            if kc >= 2:
                W("sync", s_s, sdone[kc - 2])
            else:
                W("sync", s_v, vdone[NKC - 2 + sl])
            kb.dma("sync", t_h[sl][:, :], h_src(kc), s_h[sl])
            W("scalar", s_h[sl], s_h[sl].n)
            if trp_free[sl]:
                W("scalar", s_t, trp_free[sl])
            OP("scalar", "activation", t_xns[sl][:, :], t_h[sl][:, :], AF.Identity,
               bias=t_nb[:, kc:kc + 1], scale=t_rstd[:, kc:kc + 1], sem=s_s)
            xn_done[kc] = s_s.n
            if kc < NTC:
                if kc >= 1:
                    W("scalar", s_v, qb_copy[kc - 1])
                OP("scalar", "activation", t_q8tm[0][:, :], t_h[sl][:, :], AF.Identity,
                   bias=t_qbi[:, kc:kc + 1], scale=t_qsc[:, kc:kc + 1], sem=s_s)
                W("vector", s_s, s_s.n)
                if qtrp_free[sx]:
                    W("vector", s_t, qtrp_free[sx])
                OP("vector", "tensor_copy", t_qbtm[sx][:, :], t_q8tm[0][:, :], sem=s_v)
                qb_copy[kc] = s_v.n
            sdone[kc] = s_s.n
            if sx % 2 == 1:
                g0 = kc - 1
                W("tensor", s_s, xn_done[kc])
                for f in range(NHC):
                    pg = psum_ro[pidx[0] % 4]
                    pidx[0] += 1
                    kb.wait_bank("tensor", pg)
                    for j in range(2):
                        OP("tensor", "matmul", pg[:, j * 128:(j + 1) * 128],
                           t_xns[j][:, f * 128:(f + 1) * 128], t_idb[:, :],
                           start=True, stop=True, sem=(s_t if j == 1 else None))
                    W("scalar", s_t, s_t.n)
                    OP("scalar", "copy", t_xnfm[:, f * S + g0 * 128:f * S + g0 * 128 + 256],
                       pg[:, 0:256], sem=s_s)
                    kb.set_bank(pg, s_s)
                trp_free[0] = trp_free[1] = s_t.n
            if sx == 3 and kc < NTC:
                g0 = kc - 3
                W("tensor", s_v, qb_copy[kc])
                for f in range(NHC):
                    sq = f % 2
                    pg = psum_ro[pidx[0] % 4]
                    pidx[0] += 1
                    kb.wait_bank("tensor", pg)
                    for j in range(4):
                        OP("tensor", "matmul", pg[:, j * 128:(j + 1) * 128],
                           t_qbtm[j][:, f * 128:(f + 1) * 128], t_idb[:, :],
                           start=True, stop=True, sem=(s_t if j == 3 else None))
                    W("vector", s_t, s_t.n)
                    if qxs_st[sq]:
                        W("vector", s_qxst[sq], qxs_st[sq])
                    OP("vector", "tensor_copy", t_qxs[sq][:, :], pg[:, :], sem=s_v)
                    kb.set_bank(pg, s_v)
                    W("scalar", s_v, s_v.n)
                    kb.dma("scalar", d_qxf_sc[f * 128:(f + 1) * 128, g0 * 128:g0 * 128 + 512],
                           t_qxs[sq][:, :], s_qxst[sq])
                    qxs_st[sq] = s_qxst[sq].n
                for j in range(4):
                    qtrp_free[j] = s_t.n
        XNFM_S = s_s.n
        V_QX = s_v.n
        QXW_ALL = [s.n for s in s_qxst]
        V_XAM = V1B

        cut(1)
        unreg(ph1)
        W("sync", s_v, V_XAM)
        kb.dma("sync", d_rs1[:, 0:NTC], t_xam[:, 0:NTC], s_mv)
        RS1_ST = s_mv.n

        cut(2)

        def arena_load(dram_ap, nchunks, ncols, after_t):
            W("gpsimd", s_t, after_t)
            kb.dma("gpsimd", t_aren[:, 0:nchunks * ncols].rearrange("p (f d) -> p f d", d=ncols),
                   dram_ap.rearrange("(f p) d -> p f d", p=128), s_wa)
            return s_wa.n

        def mm_group(dst_ap_fn, lhs_list, bias_col, out_width=512):
            pp = psum_ro[pidx[0] % 4]
            pidx[0] += 1
            kb.wait_bank("tensor", pp)
            n = len(lhs_list)
            for i, (lhsT, rhs) in enumerate(lhs_list):
                OP("tensor", "matmul", pp[:, 0:out_width] if out_width != 512 else pp[:, :],
                   lhsT, rhs, start=(i == 0), stop=(i == n - 1),
                   sem=(s_t if i == n - 1 else None))
            W("scalar", s_t, s_t.n)
            dst = dst_ap_fn(pp)
            OP("scalar", "activation", dst, pp[:, :] if out_width == 512 else pp[:, 0:out_width],
               AF.Identity, bias=bias_col, scale=1.0, sem=s_s)
            kb.set_bank(pp, s_s)
            pw_steps(1)
            return s_s.n

        W("scalar", s_tab, TAB0)
        W("tensor", s_tab, TAB0)

        W("gpsimd", s_xw[0], s_xw[0].n)
        W("gpsimd", s_xw[1], s_xw[1].n)
        W("gpsimd", s_qxw, s_qxw.n)
        W("gpsimd", s_s, s_s.n)
        W("gpsimd", s_v, s_v.n)
        AW = arena_load(d_wdkvT, NHC, DOWN, after_t=0)
        W("tensor", s_wa, AW)
        W("tensor", s_s, XNFM_S)
        for m in range(NDC):
            for blk in range(4):
                mm_group(
                    lambda pp, m=m, blk=blk: t_ckv[:, m * S + blk * 512:m * S + (blk + 1) * 512],
                    [(t_aren[:, f * DOWN + m * 128:f * DOWN + (m + 1) * 128],
                      t_xnfm[:, f * S + blk * 512:f * S + (blk + 1) * 512]) for f in range(NHC)],
                    t_b1[:, m:m + 1])
        T_CKV = s_t.n
        S_CKV = s_s.n

        AW = arena_load(d_wdqT, NHC, DOWN, after_t=T_CKV)
        W("tensor", s_wa, AW)
        for m in range(NDC):
            for blk in range(2):
                mm_group(
                    lambda pp, m=m, blk=blk: t_cq[:, m * T + blk * 512:m * T + (blk + 1) * 512],
                    [(t_aren[:, f * DOWN + m * 128:f * DOWN + (m + 1) * 128],
                      t_xnfm[:, f * S + blk * 512:f * S + (blk + 1) * 512]) for f in range(NHC)],
                    t_b1[:, 4 + m:5 + m])
        T_CQ = s_t.n
        S_CQ = s_s.n

        AW = arena_load(d_wkrT, NHC, RD, after_t=T_CQ)
        W("tensor", s_wa, AW)
        for blk in range(4):
            pw_steps(1)
            pp = psum_ro[pidx[0] % 4]
            pidx[0] += 1
            kb.wait_bank("tensor", pp)
            for f in range(NHC):
                OP("tensor", "matmul", pp[0:64, :],
                   t_aren[:, f * RD:(f + 1) * RD],
                   t_xnfm[:, f * S + blk * 512:f * S + (blk + 1) * 512],
                   start=(f == 0), stop=(f == NHC - 1), sem=(s_t if f == NHC - 1 else None))
            W("scalar", s_t, s_t.n)
            OP("scalar", "activation", t_rop[0:64, blk * 512:(blk + 1) * 512], pp[0:64, :],
               AF.Identity, bias=t_b1[0:64, 16:17], scale=1.0, sem=s_s)
            kb.set_bank(pp, s_s)
        S_KRP = s_s.n
        T_KRP = s_t.n
        unreg(ph_xn)
        phB = reg(ExitStack())
        sbB = lambda n, shape, dt, **kw: phB.enter_context(nc.sbuf_tensor(n, shape, dt, **kw))
        t_kall = sbB("K_all", [128, 8 * S], BF16)
        t_qall = sbB("Q_all", [128, NH * T], BF16)
        t_vaug = sbB("V_aug", [128, NKC * VSTR], BF16)
        t_kr2 = sbB("Kr2", [128, S], BF16)

        W("vector", s_t, T_KRP)
        OP("vector", "memset", t_vaug[:, :], 1.0, sem=s_v)
        V_VAUG1 = s_v.n

        rope_done = {0: 0, 64: 0}

        def rope_apply(src, dst, qr_base, width, tab0=0):
            W("sync", s_s, s_s.n)
            W("sync", s_v, s_v.n)
            kb.dma("sync", t_rsw[qr_base:qr_base + 32, 0:width],
                   src[qr_base + 32:qr_base + 64, 0:width], s_rsw)
            kb.dma("sync", t_rsw[qr_base + 32:qr_base + 64, 0:width],
                   src[qr_base:qr_base + 32, 0:width], s_rsw)
            W("vector", s_rsw, s_rsw.n)
            W("vector", s_tb, TAB_ROPE)
            OP("vector", "tensor_tensor", src[qr_base:qr_base + 64, 0:width],
               src[qr_base:qr_base + 64, 0:width],
               t_tabS[qr_base:qr_base + 64, tab0:tab0 + width], ALU.mult)
            OP("vector", "tensor_tensor", t_rsw[qr_base:qr_base + 64, 0:width],
               t_rsw[qr_base:qr_base + 64, 0:width],
               t_tabC[qr_base:qr_base + 64, tab0:tab0 + width], ALU.mult)
            OP("vector", "tensor_tensor", dst,
               src[qr_base:qr_base + 64, 0:width], t_rsw[qr_base:qr_base + 64, 0:width], ALU.add,
               sem=s_v)
            rope_done[qr_base] = s_v.n
            return s_v.n

        W("vector", s_s, S_KRP)
        rope_apply(t_rop[:, 0:T], t_kr2[0:64, 0:T], 0, T, tab0=0)
        V_KR = rope_apply(t_rop[:, T:S], t_kr2[0:64, T:S], 0, T, tab0=T)
        W("sync", s_v, V_KR)
        kb.dma("sync", t_kr2[64:128, :], t_kr2[0:64, :], s_kr2)
        MV_KR = s_kr2.n

        AW = arena_load(d_wupkT, NDC, UP, after_t=T_KRP)
        W("tensor", s_wa, AW)
        W("tensor", s_s, S_CKV)
        W("scalar", s_v, V_QX)
        for _i in range(2):
            W("scalar", s_qxst[_i], QXW_ALL[_i])
        for kd in range(8):
            for blk in range(4):
                mm_group(
                    lambda pp, kd=kd, blk=blk: t_kall[:, kd * S + blk * 512:kd * S + (blk + 1) * 512],
                    [(t_aren[:, dc * UP + kd * 128:dc * UP + (kd + 1) * 128],
                      t_ckv[:, dc * S + blk * 512:dc * S + (blk + 1) * 512]) for dc in range(NDC)],
                    t_b1[:, 8 + kd:9 + kd])
        T_KC = s_t.n

        AW = arena_load(d_wqcomb, NDC, 2048, after_t=T_KC)
        W("tensor", s_wa, AW)
        W("tensor", s_s, S_CQ)
        for hh in range(NH):
            par = hh % 2
            qc_base = 64 * par
            qr_base = 64 * (1 - par)
            for blk in range(2):
                pw_steps(1)
                pp = psum_ro[pidx[0] % 4]
                pidx[0] += 1
                kb.wait_bank("tensor", pp)
                for dc in range(NDC):
                    OP("tensor", "matmul", pp[:, :],
                       t_aren[:, dc * 2048 + hh * 128:dc * 2048 + (hh + 1) * 128],
                       t_cq[:, dc * T + blk * 512:dc * T + (blk + 1) * 512],
                       start=(dc == 0), stop=(dc == NDC - 1), sem=(s_t if dc == NDC - 1 else None))
                W("scalar", s_t, s_t.n)
                if rope_done[qr_base]:
                    W("scalar", s_v, rope_done[qr_base])
                OP("scalar", "activation",
                   t_qall[qc_base:qc_base + 64, hh * T + blk * 512:hh * T + (blk + 1) * 512],
                   pp[qc_base:qc_base + 64, :], AF.Identity, bias=t_bq[qc_base:qc_base + 64, hh:hh + 1],
                   scale=1.0)
                OP("scalar", "activation",
                   t_rop[qr_base:qr_base + 64, blk * 512:(blk + 1) * 512],
                   pp[qr_base:qr_base + 64, :], AF.Identity,
                   bias=t_bq[qr_base:qr_base + 64, hh:hh + 1], scale=1.0, sem=s_s)
                kb.set_bank(pp, s_s)
            W("vector", s_s, s_s.n)
            rope_apply(t_rop, t_qall[qr_base:qr_base + 64, hh * T:(hh + 1) * T], qr_base, T)
        T_Q = s_t.n
        V_Q = s_v.n

        AW = arena_load(d_wupvT, NDC, UP, after_t=T_Q)
        W("tensor", s_wa, AW)
        W("tensor", s_tb, s_tb.n)
        W("tensor", s_v, V_VAUG1)
        for kc in range(NKC):
            pw_steps(1)
            pp = p_e
            kb.wait_bank("tensor", pp)
            for half in range(2):
                for dc in range(NDC):
                    OP("tensor", "matmul", pp[:, half * 512:(half + 1) * 512],
                       t_ckv[:, dc * S + kc * 128:dc * S + (kc + 1) * 128],
                       t_aren[:, dc * UP + half * 512:dc * UP + (half + 1) * 512],
                       start=(dc == 0), stop=False)
                OP("tensor", "matmul", pp[:, half * 512:(half + 1) * 512],
                   t_ones[0:1, 0:128], t_bvrow[0:1, half * 512:(half + 1) * 512],
                   start=False, stop=True, sem=s_t)
            W("scalar", s_t, s_t.n)
            for hh in range(NH):
                OP("scalar", "copy",
                   t_vaug[:, kc * VSTR + hh * 65:kc * VSTR + hh * 65 + 64],
                   pp[:, hh * 64:(hh + 1) * 64], sem=(s_s if hh == NH - 1 else None))
            kb.set_bank(pp, s_s)
        T_V = s_t.n
        S_VE = s_s.n

        cut(3)
        ph3 = reg(ExitStack())
        t_kst = t_rop
        t_exp2 = [t_ckv[:, 1024:2048], t_ckv[:, 2048:3072]]
        t_nrm = t_ckv[:, 0:512]
        t_atmp = t_ckv[:, 512:1024]

        W("tensor", s_v, V_Q)
        W("tensor", s_s, S_VE)
        W("tensor", s_kr2, MV_KR)
        exp_done = [kb.bank.get(id(p_e), (s_s, 0))[1], kb.bank.get(id(p_f), (s_s, 0))[1]]
        nrm_used = 0
        expmm_done = [0, 0]
        attn_ev = kb.bank.get(id(p_c), (s_v, 0))
        nrm_ev = 0
        W("vector", s_kr2, MV_KR)
        W("vector", s_s, S_VE)
        kst_free = 0
        kst_ready = 0
        for hh in range(NH):
            par = hh % 2
            kd = hh // 2
            qc_base = 64 * par
            qr_base = 64 * (1 - par)
            if kst_free:
                W("vector", s_t, kst_free)
            OP("vector", "tensor_copy", t_kst[qc_base:qc_base + 64, :],
               t_kall[qc_base:qc_base + 64, kd * S:(kd + 1) * S])
            OP("vector", "tensor_copy", t_kst[qr_base:qr_base + 64, :],
               t_kr2[qr_base:qr_base + 64, :], sem=s_v)
            kst_ready = s_v.n
            W("tensor", s_v, kst_ready)
            for qb in range(2):
                pv = [p_c, p_d][(hh * 2 + qb) % 2]
                for kk in range(NKC // 2):
                    if kk in (2, 5):
                        pw_steps(1)
                    sl = kk % 2
                    pp = [p_e, p_f][sl]
                    W("tensor", s_s, exp_done[sl])
                    for j in range(2):
                        kc = 2 * kk + j
                        OP("tensor", "matmul", pp[:, j * 512:(j + 1) * 512],
                           t_kst[:, kc * 128:(kc + 1) * 128],
                           t_qall[:, hh * T + qb * 512:hh * T + (qb + 1) * 512],
                           start=True, stop=True, sem=(s_t if j == 1 else None))
                    sc_val = s_t.n
                    W("scalar", s_t, sc_val)
                    if expmm_done[sl]:
                        W("scalar", s_t, expmm_done[sl])
                    OP("scalar", "activation", t_exp2[sl], pp[:, :], AF.Exp,
                       scale=t_rtmp[:, 5:6], sem=s_s)
                    exp_done[sl] = s_s.n
                    kb.set_bank(pp, s_s)
                    W("tensor", s_s, exp_done[sl])
                    if kk == 0:
                        kb.wait_bank("tensor", pv)
                    for j in range(2):
                        kc = 2 * kk + j
                        OP("tensor", "matmul", pv[0:65, :],
                           t_vaug[:, kc * VSTR + hh * 65:kc * VSTR + (hh + 1) * 65],
                           t_exp2[sl][:, j * 512:(j + 1) * 512],
                           start=(kc == 0), stop=(kc == NKC - 1),
                           sem=(s_t if kc == NKC - 1 else None))
                    expmm_done[sl] = s_t.n
                T_ATT = s_t.n
                kst_free = sc_val
                pw_steps(1)
                W("vector", s_t, T_ATT)
                OP("vector", "reciprocal", t_row1[0:1, 0:512], pv[64:65, :], sem=s_v)
                W("tensor", s_v, s_v.n)
                if nrm_used:
                    W("tensor", s_s, nrm_used)
                else:
                    kb.wait_bank("tensor", p_a)
                OP("tensor", "matmul", p_a[0:64, 0:512], t_onesF[0:1, 0:64],
                   t_row1[0:1, 0:512], start=True, stop=True, sem=s_t)
                W("scalar", s_t, s_t.n)
                if nrm_ev:
                    W("scalar", s_v, nrm_ev)
                OP("scalar", "copy", t_nrm[0:64, 0:512], p_a[0:64, 0:512], sem=s_s)
                nrm_used = s_s.n
                W("vector", s_s, s_s.n)
                if par == 0:
                    OP("vector", "tensor_tensor",
                       t_attn[0:64, kd * T + qb * 512:kd * T + (qb + 1) * 512],
                       pv[0:64, :], t_nrm[0:64, 0:512], ALU.mult, sem=s_v)
                else:
                    W("vector", s_ash, s_ash.n)
                    OP("vector", "tensor_tensor", t_atmp[0:64, 0:512],
                       pv[0:64, :], t_nrm[0:64, 0:512], ALU.mult, sem=s_v)
                    W("sync", s_v, s_v.n)
                    kb.dma("sync", t_attn[64:128, kd * T + qb * 512:kd * T + (qb + 1) * 512],
                           t_atmp[0:64, 0:512], s_ash)
                nrm_ev = s_v.n
                kb.set_bank(pv, s_v)
        T_ATT_ALL = s_t.n
        V_ATT_ALL = s_v.n
        MV_ATT = s_ash.n
        kb.set_bank(p_a, s_s)

        cut(4)
        if DEBUG:
            W("sync", s_v, V_ATT_ALL)
            W("sync", s_ash, MV_ATT)
            W("sync", s_t, T_ATT_ALL)
            kb.dma("sync", d_dbg_attn[:, :], t_attn[:, :], s_mv)
            kb.dma("sync", d_dbg_q[:, :], t_qall[:, :], s_mv)
            kb.dma("sync", d_dbg_k[:, :], t_kall[:, :], s_mv)
            kb.dma("sync", d_dbg_kr[:, :], t_kr2[:, :], s_mv)
            kb.dma("sync", d_dbg_v[:, :], t_vaug[:, :], s_mv)
            kb.dma("sync", d_dbg_ckv[:, :], t_ckv[:, :], s_mv)
            kb.dma("sync", d_dbg_cq[:, :], t_cq[:, :], s_mv)
            W("tensor", s_mv, s_mv.n)
            W("gpsimd", s_mv, s_mv.n)
            W("scalar", s_mv, s_mv.n)
            W("vector", s_mv, s_mv.n)

        unreg(ph3)
        unreg(phB)
        ph35 = reg(ExitStack())
        t_wfc = ph35.enter_context(nc.sbuf_tensor("wfcA", [128, 8 * 2048], BF16))
        t_fcst = [ph35.enter_context(nc.sbuf_tensor(f"fcst{i}", [128, 512], F32)) for i in range(2)]
        t_bfc = ph35.enter_context(nc.sbuf_tensor("bfcr", [1, H], BF16))
        W("gpsimd", s_t, T_ATT_ALL)
        kb.dma("gpsimd", t_bfc[0:1, :], d_bfc[0:1, :], s_tb)
        W("tensor", s_tb, s_tb.n)
        AW = kb.dma("gpsimd", t_wfc[:, :].rearrange("p (f d) -> p f d", d=2048),
                    d_wfcT.rearrange("(f p) d -> p f d", p=128), s_wa)
        W("tensor", s_wa, AW)
        W("tensor", s_v, V_ATT_ALL)
        W("tensor", s_ash, MV_ATT)
        for tc in range(NTC):
            for nb in range(4):
                pp = [p_a, p_b][nb % 2]
                kb.wait_bank("tensor", pp)
                for vc in range(8):
                    OP("tensor", "matmul", pp[:, :],
                       t_attn[:, vc * T + tc * 128:vc * T + (tc + 1) * 128],
                       t_wfc[:, vc * 2048 + nb * 512:vc * 2048 + (nb + 1) * 512],
                       start=(vc == 0), stop=False)
                OP("tensor", "matmul", pp[:, :], t_ones[0:1, 0:128],
                   t_bfc[0:1, nb * 512:(nb + 1) * 512], start=False, stop=True, sem=s_t)
                W("scalar", s_t, s_t.n)
                fsl = nb % 2
                pw_steps(1)
                if tc * 4 + nb >= 2:
                    W("scalar", s_fcw, s_fcw.n)
                OP("scalar", "copy", t_fcst[fsl][:, :], pp[:, :], sem=s_s)
                kb.set_bank(pp, s_s)
                W("scalar", s_s, s_s.n)
                kb.dma("scalar", d_fc_sc[tc * 128:(tc + 1) * 128, nb * 512:(nb + 1) * 512],
                       t_fcst[fsl][:, :], s_fcw, drain=True)
        FCW_ALL = s_fcw.n
        S_FC = s_s.n

        unreg(ph35)
        unreg(ph2)

        cut(5)
        pw_steps(400)
        WQW_ALL = 0


        cut(6)
        ph45 = reg(ExitStack())
        t_wd8 = ph45.enter_context(nc.sbuf_tensor("wd8", [128, NIC * 2048], FP8))
        t_wdT = [ph45.enter_context(nc.sbuf_tensor("wdT0", [128, H], BF16,
                                                   is_transpose_dest=True))] * 2
        ph4 = reg(ExitStack())
        sb4 = lambda n, shape, dt, **kw: ph4.enter_context(nc.sbuf_tensor(n, shape, dt, **kw))
        ISB = 4
        NSB = NIC // ISB
        t_wsbB = [sb4(f"wsbB{i}", [128, 512], BF16, is_transpose_dest=True) for i in range(4)]
        t_w8g = [sb4(f"w8g{i}", [128, NHC * 512], FP8) for i in range(3)]
        t_w8u = [sb4(f"w8u{i}", [128, NHC * 512], FP8) for i in range(3)]
        t_sg = sb4("sgb", [128, T], F32)
        t_su = sb4("sub", [128, T], F32)
        t_tb = [sb4(f"t_st{i}", [128, T], F32) for i in range(2)]
        t_tmx = sb4("tmx", [128, T], F32)
        t_tmn = sb4("tmn", [128, T], F32)
        t_qxfm = sb4("qx_fm", [128, NHC * T], FP8)
        t_tfacb = sb4("tfacb", [128, T], F32)

        W("gpsimd", s_s, S_FC)
        for _i in range(2):
            W("gpsimd", s_qxst[_i], QXW_ALL[_i])
        kb.dma("gpsimd", t_qxfm[:, :].rearrange("p (f d) -> p f d", d=T),
               d_qxf_sc.rearrange("(f p) d -> p f d", p=128), s_qxl)
        QXL = s_qxl.n

        W("sync", s_t, T_ATT_ALL)
        W("sync", s_mv, RS1_ST)
        for half in range(2):
            kb.dma("sync", t_row1[0:1, :],
                   d_rs1[:, half * 4:half * 4 + 4].rearrange("p c -> c p"), s_mv,
                   allow_slow_non_contiguous=True)
            W("vector", s_mv, s_mv.n)
            W("vector", s_t, T_ATT_ALL)
            OP("vector", "tensor_scalar", t_row1[0:1, :], t_row1[0:1, :], 4.0 / 448.0, None,
               ALU.mult, sem=s_v)
            W("tensor", s_v, s_v.n)
            pp = psum_ro[pidx[0] % 4]
            pidx[0] += 1
            kb.wait_bank("tensor", pp)
            OP("tensor", "matmul", pp[:, :], t_onesF[0:1, 0:128],
               t_row1[0:1, :], start=True, stop=True, sem=s_t)
            W("scalar", s_t, s_t.n)
            OP("scalar", "copy", t_tfacb[:, half * 512:(half + 1) * 512], pp[:, :], sem=s_s)
            kb.set_bank(pp, s_s)
            W("sync", s_t, s_t.n)
        V_TFACB = s_s.n

        W("sync", s_fcw, FCW_ALL)
        W("gpsimd", s_fcw, FCW_ALL)
        W("vector", s_fcw, FCW_ALL)
        W("tensor", s_qxl, QXL)
        W("tensor", s_s, V_TFACB)

        vqt_done = {}
        wdT_done = [0, 0]
        cast_done = [0] * 8
        w8_free = [0, 0, 0]
        tmax_init = [False]
        w8_ready = [0, 0, 0]

        def stage_w8(sbi):
            sl = sbi % 3
            need = [0, 0]
            for cc in range(sbi * ISB, (sbi + 1) * ISB):
                for tbl in (pw_wq_gate, pw_wq_up):
                    wsl, wv = tbl[cc]
                    need[wsl] = max(need[wsl], wv)
            for wsl in range(2):
                if need[wsl]:
                    W("sync", pw_wqw[wsl], need[wsl])
            if w8_free[sl]:
                W("scalar", s_t, w8_free[sl])
            for mat, w8 in ((0, t_w8g), (1, t_w8u)):
                dsc = (d_wg_sc, d_wu_sc)[mat]
                for hc in range(NHC):
                    tsl = (mat * NHC + hc) % 4
                    if cast_done[tsl]:
                        W("sync", s_s, cast_done[tsl])
                    kb.dmaT("sync", t_wsbB[tsl][:, :],
                            dsc[sbi * 512:(sbi + 1) * 512, hc * 128:(hc + 1) * 128], s_wt[tsl])
                    W("scalar", s_wt[tsl], s_wt[tsl].n)
                    OP("scalar", "copy", w8[sl][:, hc * 512:(hc + 1) * 512],
                       t_wsbB[tsl][:, :], sem=s_s)
                    cast_done[tsl] = s_s.n
            w8_ready[sl] = s_s.n

        stage_w8(0)
        stage_w8(1)
        for sbi in range(NSB):
            sl = sbi % 3
            if sbi + 2 < NSB:
                stage_w8(sbi + 2)
            W("tensor", s_s, w8_ready[sl])
            for ci in range(ISB):
                c = sbi * ISB + ci
                for pp, w8 in ((p_e, t_w8g), (p_f, t_w8u)):
                    kb.wait_bank("tensor", pp)
                    for half in range(2):
                        for hc in range(NHC):
                            OP("tensor", "matmul", pp[:, half * 512:(half + 1) * 512],
                               w8[sl][:, hc * 512 + ci * 128:hc * 512 + (ci + 1) * 128],
                               t_qxfm[:, hc * T + half * 512:hc * T + (half + 1) * 512],
                               start=(hc == 0), stop=(hc == NHC - 1),
                               sem=(s_t if (hc == NHC - 1 and half == 1) else None))
                T_GU = s_t.n
                W("vector", s_t, T_GU)
                OP("vector", "tensor_tensor", t_sg[:, :], p_e[:, :], t_tfacb[:, :], ALU.mult,
                   sem=s_v)
                W("scalar", s_v, s_v.n)
                OP("scalar", "activation", t_sg[:, :], t_sg[:, :], AF.Silu,
                   scale=t_gfac[:, c:c + 1], sem=s_s)
                W("vector", s_s, s_s.n)
                tsl = c % 2
                if c >= 2:
                    W("vector", s_tw, s_tw.n)
                OP("vector", "tensor_tensor", t_su[:, :], p_f[:, :], t_tfacb[:, :], ALU.mult)
                OP("vector", "scalar_tensor_tensor", t_tb[tsl][:, :],
                   t_su[:, :], t_ufac[:, c:c + 1], t_sg[:, :], ALU.mult, ALU.mult)
                if not tmax_init[0]:
                    OP("vector", "tensor_copy", t_tmx[:, :], t_tb[tsl][:, :])
                    OP("vector", "tensor_copy", t_tmn[:, :], t_tb[tsl][:, :], sem=s_v)
                    tmax_init[0] = True
                else:
                    OP("vector", "tensor_tensor", t_tmx[:, :], t_tmx[:, :],
                       t_tb[tsl][:, :], ALU.max)
                    OP("vector", "tensor_tensor", t_tmn[:, :], t_tmn[:, :],
                       t_tb[tsl][:, :], ALU.min, sem=s_v)
                kb.set_bank(p_e, s_v)
                kb.set_bank(p_f, s_v)
                W("scalar", s_v, s_v.n)
                kb.dma("scalar", d_t_sc[c * 128:(c + 1) * 128, :], t_tb[tsl][:, :], s_tw)
            w8_free[sl] = s_t.n
            if sbi == 0:
                need_wd = [0, 0]
                for cc in range(NHC):
                    wsl_c, wv_c = pw_wq_wd[cc]
                    need_wd[wsl_c] = max(need_wd[wsl_c], wv_c)
                for wsl_c in range(2):
                    if need_wd[wsl_c]:
                        W("sync", pw_wqw[wsl_c], need_wd[wsl_c])
                W("sync", s_s, S_FC)
            for c in range(sbi * 4, sbi * 4 + 4):
                dsl = 0
                if wdT_done[dsl]:
                    W("sync", s_s, wdT_done[dsl])
                kb.dmaT("sync", t_wdT[dsl][:, :], d_wd_sc[:, c * 128:(c + 1) * 128], s_wdt[dsl])
                W("scalar", s_wdt[dsl], s_wdt[dsl].n)
                OP("scalar", "copy", t_wd8[:, c * 2048:(c + 1) * 2048], t_wdT[dsl][:, :],
                   sem=s_s)
                wdT_done[dsl] = s_s.n
        G_WD8 = s_s.n
        V_T_ALL = s_v.n
        TW_ALL = s_tw.n

        W("tensor", s_g, G_ID)
        OP("vector", "tensor_scalar", t_tmn[:, :], t_tmn[:, :], -1.0, None, ALU.mult)
        OP("vector", "tensor_tensor", t_tmx[:, :], t_tmx[:, :], t_tmn[:, :], ALU.max, sem=s_v)
        V_TMX = s_v.n
        W("tensor", s_v, V_TMX)
        for tb in range(NTC):
            pp = [p_a, p_b][tb % 2]
            kb.wait_bank("tensor", pp)
            OP("tensor", "transpose", pp[:, 0:128], t_tmx[:, tb * 128:(tb + 1) * 128], t_id[:, :],
               sem=s_t)
            W("vector", s_t, s_t.n)
            OP("vector", "tensor_reduce", t_tamr[:, tb:tb + 1], pp[:, 0:128], AX.X, ALU.max, sem=s_v)
            kb.set_bank(pp, s_v)
        V_TAMR = s_v.n
        T_AMX = s_t.n
        W("sync", s_v, V_TAMR)
        kb.dma("sync", d_rs1[:, 0:NTC], t_tamr[:, 0:NTC], s_mv)
        W("sync", s_mv, s_mv.n)
        unreg(ph4)
        unreg(ph_qx)

        cut(7)
        ph5 = reg(ExitStack())
        sb5 = lambda n, shape, dt, **kw: ph5.enter_context(nc.sbuf_tensor(n, shape, dt, **kw))
        t_qsclb = sb5("qsclb", [128, T], F32)
        t_qgf = [sb5(f"qgf{i}", [128, T], F32) for i in range(2)]
        t_qperm = sb5("qperm", [128, NIC * T], FP8)
        t_swdb = sb5("swdb", [128, H], F32)
        t_ho = sb5("hor", [128, H], F32)
        t_fcb = sb5("fcbr", [128, H], F32)
        t_outb = sb5("outbr", [128, H], F32)

        for half in range(2):
            kb.dma("sync", t_row1[0:1, :],
                   d_rs1[:, half * 4:half * 4 + 4].rearrange("p c -> c p"), s_mv,
                   allow_slow_non_contiguous=True)
            W("vector", s_mv, s_mv.n)
            OP("vector", "reciprocal", t_row1[0:1, :], t_row1[0:1, :])
            OP("vector", "tensor_scalar", t_row1[0:1, :], t_row1[0:1, :], 224.0, None,
               ALU.mult, sem=s_v)
            W("tensor", s_v, s_v.n)
            pp = [p_a, p_b][half]
            kb.wait_bank("tensor", pp)
            OP("tensor", "matmul", pp[:, :], t_onesF[0:1, 0:128],
               t_row1[0:1, :], start=True, stop=True, sem=s_t)
            W("vector", s_t, s_t.n)
            OP("vector", "tensor_copy", t_qsclb[:, half * 512:(half + 1) * 512], pp[:, :], sem=s_v)
            kb.set_bank(pp, s_v)
            W("sync", s_t, s_t.n)
        V_QSCLB = s_v.n

        kb.dma("sync", d_rs1[:, 0:NHC], t_dfac[:, 0:NHC], s_mv)
        W("sync", s_mv, s_mv.n)
        for q4 in range(4):
            kb.dma("sync", t_row1[0:1, :],
                   d_rs1[:, q4 * 4:q4 * 4 + 4].rearrange("p c -> c p"), s_mv,
                   allow_slow_non_contiguous=True)
            W("tensor", s_mv, s_mv.n)
            pp = [p_a, p_b][q4 % 2]
            kb.wait_bank("tensor", pp)
            OP("tensor", "matmul", pp[:, :], t_onesF[0:1, 0:128],
               t_row1[0:1, :], start=True, stop=True, sem=s_t)
            W("vector", s_t, s_t.n)
            OP("vector", "tensor_copy", t_swdb[:, q4 * 512:(q4 + 1) * 512], pp[:, :], sem=s_v)
            kb.set_bank(pp, s_v)
            W("sync", s_t, s_t.n)
        V_SWDB = s_v.n

        W("gpsimd", s_tw, TW_ALL)
        W("gpsimd", s_t, T_AMX)
        W("gpsimd", s_v, V_TAMR)
        W("vector", s_tw, TW_ALL)
        W("vector", s_t, T_AMX)
        vqt = {}
        gat = {}
        for c in range(NIC):
            sl = c % 2
            if c >= 2:
                W("gpsimd", s_v, vqt[c - 2])
            kb.gather(t_qgf[sl][:, :], d_t_sc[:, :], t_didx[:, c:c + 1], s_qtg[sl])
            gat[c] = s_qtg[sl].n
            W("vector", s_qtg[sl], gat[c])
            OP("vector", "tensor_tensor", t_qperm[:, c * T:(c + 1) * T],
               t_qgf[sl][:, :], t_qsclb[:, :], ALU.mult, sem=s_v)
            vqt[c] = s_v.n
        V_QTT_ALL = s_v.n

        W("tensor", s_s, G_WD8)
        W("tensor", s_v, V_SWDB)
        W("scalar", s_v, V_QTT_ALL)
        for tc in range(NTC):
            if tc >= 1:
                W("scalar", s_out, s_out.n)
            W("scalar", s_fcw, FCW_ALL)
            kb.dma("scalar", t_ho[:, :], d_h_own[tc * 128:(tc + 1) * 128, :], s_hor)
            kb.dma("scalar", t_fcb[:, :], d_fc_sc[tc * 128:(tc + 1) * 128, :], s_fcr)
            for nb in range(4):
                pp = [p_c, p_d][nb % 2]
                kb.wait_bank("tensor", pp)
                for c in range(NIC):
                    if tc == 0 and nb == 0:
                        W("tensor", s_v, vqt[c])
                    OP("tensor", "matmul", pp[:, :],
                       t_qperm[:, c * T + tc * 128:c * T + (tc + 1) * 128],
                       t_wd8[:, c * 2048 + nb * 512:c * 2048 + (nb + 1) * 512],
                       start=(c == 0), stop=(c == NIC - 1), sem=(s_t if c == NIC - 1 else None))
                T_DN = s_t.n
                W("vector", s_t, T_DN)
                W("vector", s_hor, s_hor.n)
                W("vector", s_fcr, s_fcr.n)
                if nb == 0:
                    OP("vector", "tensor_scalar", t_rtmp[:, 2:3], t_tamr[:, tc:tc + 1],
                       4.0 / 448.0, None, ALU.mult)
                OP("vector", "tensor_scalar", t_outb[:, nb * 512:(nb + 1) * 512], pp[:, :],
                   t_rtmp[:, 2:3], None, ALU.mult)
                OP("vector", "tensor_tensor", t_outb[:, nb * 512:(nb + 1) * 512],
                   t_outb[:, nb * 512:(nb + 1) * 512], t_swdb[:, nb * 512:(nb + 1) * 512], ALU.mult)
                OP("vector", "tensor_tensor", t_outb[:, nb * 512:(nb + 1) * 512],
                   t_outb[:, nb * 512:(nb + 1) * 512], t_fcb[:, nb * 512:(nb + 1) * 512], ALU.add)
                OP("vector", "tensor_tensor", t_outb[:, nb * 512:(nb + 1) * 512],
                   t_outb[:, nb * 512:(nb + 1) * 512], t_ho[:, nb * 512:(nb + 1) * 512], ALU.add,
                   sem=s_v)
                kb.set_bank(pp, s_v)
            W("scalar", s_v, s_v.n)
            kb.dma("scalar", d_out[tc * 128:(tc + 1) * 128, :], t_outb[:, :], s_out)
        OUT_ALL = s_out.n
        unreg(ph5)
        unreg(ph45)

    except _Cut:
        pass
    for _st in reversed(_stacks):
        _st.close()
    if s_out.n:
        W("gpsimd", s_out, s_out.n)

    blk = nc.Block()
    with blk:
        kb.flush(blk)
    es.close()
    p_es.close()
    return nc


_NC_CACHE = {}


def _get_nc():
    if "nc" not in _NC_CACHE:
        _NC_CACHE["nc"] = build_nc()
    return _NC_CACHE["nc"]


def _host_prep(inputs):
    x = np.ascontiguousarray(np.asarray(inputs["L0_attn_h"], dtype=np.float32))

    wupq = np.asarray(inputs["L0_attn_W_up_q"], dtype=np.float32)
    wqr = np.asarray(inputs["L0_attn_W_qr"], dtype=np.float32)
    bupq = np.asarray(inputs["L0_attn_b_up_q"], dtype=np.float32)
    bqr = np.asarray(inputs["L0_attn_b_qr"], dtype=np.float32)

    deint = np.concatenate([np.arange(0, RD, 2), np.arange(1, RD, 2)])

    wq_cols, bq_cols = [], []
    for h in range(NH):
        qc = wupq[h * 64:(h + 1) * 64, :].T
        qr = wqr[h * 64 + deint, :].T
        bc = bupq[h * 64:(h + 1) * 64]
        br = bqr[h * 64 + deint]
        if h % 2 == 0:
            wq_cols += [qc, qr]
            bq_cols += [bc, br]
        else:
            wq_cols += [qr, qc]
            bq_cols += [br, bc]
    wqcomb = np.ascontiguousarray(np.concatenate(wq_cols, axis=1), dtype=np.float32)
    bqcomb = np.ascontiguousarray(np.concatenate(bq_cols), dtype=np.float32)

    wkr = np.asarray(inputs["L0_attn_W_kr"], dtype=np.float32)
    wkrT = np.ascontiguousarray(wkr[deint, :].T)
    bkr = np.ascontiguousarray(np.asarray(inputs["L0_attn_b_kr"], dtype=np.float32)[deint])

    common = {
        "wdkvT": np.ascontiguousarray(np.asarray(inputs["L0_attn_W_down_kv"], np.float32).T),
        "wdqT": np.ascontiguousarray(np.asarray(inputs["L0_attn_W_down_q"], np.float32).T),
        "wupkT": np.ascontiguousarray(np.asarray(inputs["L0_attn_W_up_k"], np.float32).T),
        "wupvT": np.ascontiguousarray(np.asarray(inputs["L0_attn_W_up_v"], np.float32).T),
        "wqcomb": wqcomb,
        "wkrT": wkrT,
        "wfcT": np.ascontiguousarray(np.asarray(inputs["L0_attn_W_fc"], np.float32).T),
        "bdkv": np.ascontiguousarray(np.asarray(inputs["L0_attn_b_down_kv"], np.float32)),
        "bupk": np.ascontiguousarray(np.asarray(inputs["L0_attn_b_up_k"], np.float32)),
        "bupv": np.ascontiguousarray(np.asarray(inputs["L0_attn_b_up_v"], np.float32)[None, :]),
        "bdq": np.ascontiguousarray(np.asarray(inputs["L0_attn_b_down_q"], np.float32)),
        "bqcomb": bqcomb,
        "bkr": bkr,
        "bfc": np.ascontiguousarray(np.asarray(inputs["L0_attn_b_fc"], np.float32)[None, :]),
        "wg_nat": np.ascontiguousarray(np.asarray(inputs["L0_ffn_w_gate"], np.float32)),
        "wu_nat": np.ascontiguousarray(np.asarray(inputs["L0_ffn_w_up"], np.float32)),
        "wd_nat": np.ascontiguousarray(np.asarray(inputs["L0_ffn_w_down"], np.float32)),
        "didx": np.ascontiguousarray(np.asarray(inputs["L0_ffn_down_reorder_index"], np.int32)),
    }

    theta = (1.0 / (10000.0 ** (np.arange(0, RD, 2, dtype=np.float32) / np.float32(RD)))).astype(np.float32)

    in_maps = []
    for core in range(8):
        b, half = divmod(core, 2)
        own = slice(half * T, (half + 1) * T)
        oth = slice((1 - half) * T, (2 - half) * T)
        pos = np.concatenate([np.arange(half * T, (half + 1) * T, dtype=np.float32),
                              np.arange((1 - half) * T, (2 - half) * T, dtype=np.float32)])
        ang = (pos[:, None] * theta[None, :]).astype(np.float32)
        sin_t = np.sin(ang).astype(np.float32).T
        cos_t = np.cos(ang).astype(np.float32).T
        tabS = np.ascontiguousarray(np.tile(sin_t, (4, 1)).astype(np.float32))
        tabC = np.ascontiguousarray(
            np.concatenate([-cos_t, cos_t, -cos_t, cos_t], axis=0).astype(np.float32))
        m = dict(common)
        m["h_own"] = np.ascontiguousarray(x[b, own, :])
        m["h_oth"] = np.ascontiguousarray(x[b, oth, :])
        m["tabS"] = tabS
        m["tabCsgn"] = tabC
        in_maps.append(m)
    return in_maps


def kernel(**inputs):
    in_maps = _host_prep(inputs)
    nc = _get_nc()
    res = run_bass_kernel_spmd(nc, in_maps, core_ids=list(range(8)))
    out = np.empty((4, 2048, 2048), dtype=np.float32)
    for core in range(8):
        b, half = divmod(core, 2)
        out[b, half * T:(half + 1) * T, :] = res.results[core]["y"]
    return out


# version 19
# speedup vs baseline: 1.0274x; 1.0274x over previous
import math
from contextlib import ExitStack

import numpy as np
import concourse.bass as bass
import concourse.mybir as mybir
from concourse.bass import IndirectOffsetOnAxis
from concourse.bass_utils import run_bass_kernel_spmd

F32 = mybir.dt.float32
BF16 = mybir.dt.bfloat16
FP8 = mybir.dt.float8e4
I32 = mybir.dt.int32
AF = mybir.ActivationFunctionType
ALU = mybir.AluOpType
AX = mybir.AxisListType

S, T, H = 2048, 1024, 2048
DOWN, UP, NH, RD, VD, FI = 512, 1024, 16, 64, 64, 5632
SC_ATT = 1.0 / math.sqrt(192.0)
EPS = 1e-5
NHC, NKC, NTC, NIC, NDC = H // 128, S // 128, T // 128, FI // 128, DOWN // 128
VSTR = NH * 65

ENGS = ("sync", "vector", "scalar", "tensor", "gpsimd")
DEBUG = False
PH_LIM = 99


class _Cut(Exception):
    pass


class Sem:
    def __init__(self, es, nc, name):
        self.h = es.enter_context(nc.semaphore(name))
        self.n = 0


class KB:

    def __init__(self, nc, es):
        self.nc = nc
        self.es = es
        self.q = {k: [] for k in ENGS}
        self._semid = 0
        self.bank = {}
        self.wset = {e: set() for e in ENGS}
        self.rset = {e: set() for e in ENGS}

    def sem(self, name=None):
        self._semid += 1
        return Sem(self.es, self.nc, name or f"sm{self._semid}")

    def emit(self, eng, fn):
        self.q[eng].append(fn)

    def wait(self, eng, sem, val=None):
        v = sem.n if val is None else val
        if v > 0:
            self.emit(eng, lambda e: e.wait_ge(sem.h, v))

    def wait_bank(self, eng, pp):
        ent = self.bank.get(id(pp))
        if ent is not None:
            self.wait(eng, ent[0], ent[1])

    def set_bank(self, pp, sem):
        self.bank[id(pp)] = (sem, sem.n)

    def dma(self, eng, out, in_, sem, drain=False, **kw):
        sem.n += 16
        if eng in ("vector", "scalar") and not drain:
            sid = self._tid(in_)
            if sid is not None and sid in self.wset[eng]:
                drain = True
        if drain:
            if eng in ("vector", "scalar"):
                self.wset[eng].clear()
                self.rset[eng].clear()
            self.emit(eng, lambda e: e.drain())
        self.emit(eng, lambda e: e.dma_start(out=out, in_=in_, **kw).then_inc(sem.h, 16))
        return sem.n

    def dmaT(self, eng, out, in_, sem):
        sem.n += 16
        self.emit(eng, lambda e: e.dma_start_transpose(out=out, in_=in_).then_inc(sem.h, 16))
        return sem.n

    def gather(self, out, in_, idx_ap, sem):
        sem.n += 16
        self.emit("gpsimd", lambda e: e.indirect_dma_start(
            out=out, out_offset=None, in_=in_,
            in_offset=IndirectOffsetOnAxis(idx_ap, axis=0)).then_inc(sem.h, 16))
        return sem.n

    @staticmethod
    def _tid(x):
        t = getattr(x, "tensor", None)
        return id(t) if t is not None else None

    def _hazard(self, eng, outs, ins):
        w, r = self.wset[eng], self.rset[eng]
        need = (not ins.isdisjoint(w)) or (not outs.isdisjoint(w)) \
            or (not outs.isdisjoint(r))
        if need:
            w.clear()
            r.clear()
        w |= outs
        r |= ins
        return need

    def op(self, eng, meth, *a, sem=None, **kw):
        pre = False
        if eng in ("vector", "scalar"):
            outs = {self._tid(a[0])} - {None}
            ins = {self._tid(x) for x in a[1:]} | {self._tid(x) for x in kw.values()}
            ins -= {None}
            pre = self._hazard(eng, outs, ins)
        if sem is None:
            if pre:
                self.emit(eng, lambda e: (e.drain(), getattr(e, meth)(*a, **kw)))
            else:
                self.emit(eng, lambda e: getattr(e, meth)(*a, **kw))
            return None
        sem.n += 1
        if pre:
            self.emit(eng, lambda e: (e.drain(), getattr(e, meth)(*a, **kw).then_inc(sem.h, 1)))
        else:
            self.emit(eng, lambda e: getattr(e, meth)(*a, **kw).then_inc(sem.h, 1))
        return sem.n

    def flush(self, blk):
        for name in ENGS:
            fns = self.q[name]
            if not fns:
                continue
            section = getattr(blk, name)

            def body(e, fns=fns):
                for f in fns:
                    f(e)
            section(body)


def build_nc():
    nc = bass.Bass("TRN2", target_bir_lowering=False, debug=False)
    D = lambda name, shape, dt, kind="ExternalInput": nc.dram_tensor(name, shape, dt, kind=kind).ap()

    d_h_own = D("h_own", [T, H], F32)
    d_h_oth = D("h_oth", [T, H], F32)
    d_wdkvT = D("wdkvT", [H, DOWN], F32)
    d_wdqT = D("wdqT", [H, DOWN], F32)
    d_wupkT = D("wupkT", [DOWN, UP], F32)
    d_wupvT = D("wupvT", [DOWN, UP], F32)
    d_wqcomb = D("wqcomb", [DOWN, 2048], F32)
    d_wkrT = D("wkrT", [H, RD], F32)
    d_wfcT = D("wfcT", [UP, H], F32)
    d_bdkv = D("bdkv", [DOWN], F32)
    d_bupk = D("bupk", [UP], F32)
    d_bupv = D("bupv", [1, UP], F32)
    d_bdq = D("bdq", [DOWN], F32)
    d_bqcomb = D("bqcomb", [2048], F32)
    d_bkr = D("bkr", [RD], F32)
    d_bfc = D("bfc", [1, H], F32)
    d_wg = D("wg_nat", [FI, H], F32)
    d_wu = D("wu_nat", [FI, H], F32)
    d_wd = D("wd_nat", [H, FI], F32)
    d_didx = D("didx", [FI], I32)
    d_ts = D("tabS", [128, S], F32)
    d_tcs = D("tabCsgn", [128, S], F32)
    d_out = D("y", [T, H], F32, "ExternalOutput")

    d_wg_sc = D("wg_sc", [FI, H], BF16, "Internal")
    d_wu_sc = D("wu_sc", [FI, H], BF16, "Internal")
    d_wd_sc = D("wd_sc", [H, FI], BF16, "Internal")
    d_t_sc = D("t_sc", [FI, T], F32, "Internal")
    d_qxf_sc = D("qxf_sc", [NHC * 128, T], FP8, "Internal")
    d_fc_sc = D("fc_sc", [T, H], F32, "Internal")
    d_rs1 = D("rs1", [128, 16], F32, "Internal")
    if DEBUG:
        d_dbg_attn = D("dbg_attn", [128, 8 * T], BF16, "Internal")
        d_dbg_q = D("dbg_q", [128, NH * T], BF16, "Internal")
        d_dbg_k = D("dbg_k", [128, 8 * S], BF16, "Internal")
        d_dbg_kr = D("dbg_kr", [128, S], BF16, "Internal")
        d_dbg_v = D("dbg_v", [128, NKC * VSTR], BF16, "Internal")
        d_dbg_ckv = D("dbg_ckv", [128, NDC * S], BF16, "Internal")
        d_dbg_cq = D("dbg_cq", [128, NDC * T], BF16, "Internal")

    es = ExitStack()
    sb = lambda n, shape, dt, **kw: es.enter_context(nc.sbuf_tensor(n, shape, dt, **kw))
    kb = KB(nc, es)

    t_id = sb("ident", [128, 128], F32)
    t_idb = sb("identb", [128, 128], BF16)
    t_ones = sb("onesr", [1, 128], BF16)
    t_onesF = sb("onesrF", [1, 128], F32)
    t_bvrow = sb("bvrow", [1, UP], BF16)
    t_b1 = sb("b1", [128, 20], F32)
    t_bq = sb("bq", [128, 16], F32)
    t_didx = sb("didxs", [128, NIC], I32)
    t_xam = sb("xam", [128, NTC], F32)
    t_gfac = sb("gfac", [128, NIC], F32)
    t_ufac = sb("ufac", [128, NIC], F32)
    t_dfac = sb("dfac", [128, NHC], F32)
    t_tamr = sb("tamr", [128, NTC], F32)
    t_row1 = sb("row1", [1, 512], F32)
    t_rtmp = sb("rtmp", [128, 6], F32)
    t_pwf = [sb("pwf0", [128, 1024], F32), sb("pwf1", [128, 1024], F32)]
    t_pw8 = [sb("pw8a", [128, 1024], FP8)] * 2
    t_pwb = sb("pwba", [128, 1024], BF16)
    t_pwt = sb("pwt", [128, 6], F32)

    p_es = ExitStack()
    p_a = p_es.enter_context(nc.psum_tensor("p_a", [128, 512], F32))
    p_b = p_es.enter_context(nc.psum_tensor("p_b", [128, 512], F32))
    p_c = p_es.enter_context(nc.psum_tensor("p_c", [128, 512], F32))
    p_d = p_es.enter_context(nc.psum_tensor("p_d", [128, 512], F32))
    p_e = p_es.enter_context(nc.psum_tensor("p_e", [128, 1024], F32))
    p_f = p_es.enter_context(nc.psum_tensor("p_f", [128, 1024], F32))

    s_tab = kb.sem("s_tab")
    s_h = [kb.sem("s_h0"), kb.sem("s_h1")]
    s_xw = [kb.sem("s_xw0"), kb.sem("s_xw1")]
    s_xt = kb.sem("s_xt")
    s_wa = kb.sem("s_wa")
    s_tb = kb.sem("s_tb")
    s_wf = [kb.sem("s_wf0"), kb.sem("s_wf1")]
    s_pwv = kb.sem("s_pwv")
    s_pwg = kb.sem("s_pwg")
    s_wqw = kb.sem("s_wqw")
    s_wt = [kb.sem(f"s_wt{i}") for i in range(4)]
    s_wdt = [kb.sem("s_wdt0"), kb.sem("s_wdt1")]
    s_qtw = kb.sem("s_qtw")
    s_tw = kb.sem("s_tw")
    s_tr = [kb.sem("s_tr0"), kb.sem("s_tr1")]
    s_qxw = kb.sem("s_qxw")
    s_qxl = kb.sem("s_qxl")
    s_qxst = [kb.sem("s_qxst0"), kb.sem("s_qxst1")]
    s_qtg = [kb.sem("s_qtg0"), kb.sem("s_qtg1")]
    s_fcw = kb.sem("s_fcw")
    s_fcr = kb.sem("s_fcr")
    s_hor = kb.sem("s_hor")
    s_out = kb.sem("s_out")
    s_mv = kb.sem("s_mv")
    s_rsw = kb.sem("s_rsw")
    s_kr2 = kb.sem("s_kr2")
    s_ash = kb.sem("s_ash")
    s_v = kb.sem("s_v")
    s_s = kb.sem("s_s")
    s_t = kb.sem("s_t")
    s_g = kb.sem("s_g")

    W = kb.wait
    OP = kb.op

    _stacks = []

    def reg(st):
        _stacks.append(st)
        return st

    def unreg(st):
        st.close()
        if st in _stacks:
            _stacks.remove(st)

    def cut(k):
        if PH_LIM <= k:
            raise _Cut()

    try:
        WD_SEG = (2048, 2048, 1536)
        pw_state = {"q": 0, "done": False, "wd_scale_v": 0}
        pw_wq_gate = {}
        pw_wq_up = {}
        pw_wq_wd = {}
        pw_wqw = [kb.sem("s_pww0"), kb.sem("s_pww1")]
        pw_fac_v_slot = [0, 0]

        NQ_PW = 184
        pw_rd = [0, 0]
        pw_rdv = [0, 0]
        pw_ld = [0, 0]
        pw_tpwt_rd = [0]
        pw_store = [None, 0]

        def pw_geom(q):
            if q < 88:
                c = q // 2
                dram_in = d_wg if q % 2 == 0 else d_wu
                return [(dram_in[c * 128:(c + 1) * 128, 0:1024], 1024),
                        (dram_in[c * 128:(c + 1) * 128, 1024:2048], 1024)]
            qq = q - 88
            c = qq // 6
            step = qq % 6
            o0 = sum(WD_SEG[:step % 3])
            w = WD_SEG[step % 3]
            return [(d_wd[c * 128:(c + 1) * 128, o0:o0 + 1024], 1024),
                    (d_wd[c * 128:(c + 1) * 128, o0 + 1024:o0 + w], w - 1024)]

        def pw_issue(q):
            for sl, (src, w) in enumerate(pw_geom(q)):
                if pw_rd[sl]:
                    kb.wait("scalar", s_pwg, pw_rd[sl])
                if pw_rdv[sl]:
                    kb.wait("scalar", s_pwv, pw_rdv[sl])
                kb.dma("scalar", t_pwf[sl][:, 0:w], src, s_wf[sl])
                pw_ld[sl] = s_wf[sl].n

        def pw_cast_store(hf, w, dst_ap, scale_col, wsl):
            OP("scalar", "activation", t_pw8[0][:, 0:w], t_pwf[hf][:, 0:w],
               AF.Copy, scale=scale_col, sem=s_pwg)
            kb.wait("vector", s_pwg, s_pwg.n)
            if pw_store[0] is not None:
                kb.wait("vector", pw_store[0], pw_store[1])
            OP("vector", "tensor_copy", t_pwb[:, 0:w], t_pw8[0][:, 0:w], sem=s_pwv)
            kb.wait("scalar", s_pwv, s_pwv.n)
            kb.dma("scalar", dst_ap, t_pwb[:, 0:w], pw_wqw[wsl])
            pw_store[0] = pw_wqw[wsl]
            pw_store[1] = pw_wqw[wsl].n

        def pw_quantum():
            q = pw_state["q"]
            if q >= NQ_PW:
                pw_state["done"] = True
                return
            pw_state["q"] += 1
            wsl = q % 2
            if q == 0:
                pw_issue(0)
            halves = pw_geom(q)
            if q < 88:
                mat = "g" if q % 2 == 0 else "u"
                c = q // 2
                dram_sc = d_wg_sc if mat == "g" else d_wu_sc
                fac = t_gfac if mat == "g" else t_ufac
                kb.wait("vector", s_wf[0], pw_ld[0])
                if pw_tpwt_rd[0]:
                    kb.wait("vector", s_pwg, pw_tpwt_rd[0])
                OP("vector", "tensor_reduce", t_pwt[:, 0:1], t_pwf[0][:, :], AX.X, ALU.max,
                   apply_absolute_value=True)
                kb.wait("vector", s_wf[1], pw_ld[1])
                OP("vector", "tensor_reduce", t_pwt[:, 5:6], t_pwf[1][:, :], AX.X, ALU.max,
                   apply_absolute_value=True)
                OP("vector", "tensor_tensor", t_pwt[:, 0:1], t_pwt[:, 0:1], t_pwt[:, 5:6],
                   ALU.max)
                OP("vector", "tensor_scalar", fac[:, c:c + 1], t_pwt[:, 0:1], 1.0 / 448.0,
                   None, ALU.mult)
                OP("vector", "reciprocal", t_pwt[:, 1:2], t_pwt[:, 0:1])
                OP("vector", "tensor_scalar", t_pwt[:, 2:3], t_pwt[:, 1:2], 224.0,
                   None, ALU.mult, sem=s_pwv)
                pw_rdv[0] = pw_rdv[1] = s_pwv.n
                kb.wait("scalar", s_pwv, s_pwv.n)
                for hf in range(2):
                    pw_cast_store(hf, 1024,
                                  dram_sc[c * 128:(c + 1) * 128, hf * 1024:(hf + 1) * 1024],
                                  t_pwt[:, 2:3], wsl)
                pw_rd[0] = pw_rd[1] = s_pwg.n
                pw_tpwt_rd[0] = s_pwg.n
                (pw_wq_gate if mat == "g" else pw_wq_up)[c] = (wsl, pw_wqw[wsl].n)
            else:
                qq = q - 88
                c = qq // 6
                step = qq % 6
                o0 = sum(WD_SEG[:step % 3])
                w = WD_SEG[step % 3]
                if step < 3:
                    kb.wait("vector", s_wf[0], pw_ld[0])
                    if pw_tpwt_rd[0]:
                        kb.wait("vector", s_pwg, pw_tpwt_rd[0])
                    OP("vector", "tensor_reduce", t_pwt[:, 0:1], t_pwf[0][:, :], AX.X,
                       ALU.max, apply_absolute_value=True)
                    kb.wait("vector", s_wf[1], pw_ld[1])
                    OP("vector", "tensor_reduce", t_pwt[:, 5:6], t_pwf[1][:, 0:w - 1024],
                       AX.X, ALU.max, apply_absolute_value=True)
                    OP("vector", "tensor_tensor", t_pwt[:, 0:1], t_pwt[:, 0:1],
                       t_pwt[:, 5:6], ALU.max)
                    if step == 0:
                        OP("vector", "tensor_copy", t_pwt[:, 1:2], t_pwt[:, 0:1], sem=s_pwv)
                    else:
                        OP("vector", "tensor_tensor", t_pwt[:, 1:2], t_pwt[:, 1:2],
                           t_pwt[:, 0:1], ALU.max, sem=s_pwv)
                    if step == 2:
                        OP("vector", "tensor_scalar", t_dfac[:, c:c + 1], t_pwt[:, 1:2],
                           1.0 / 448.0, None, ALU.mult)
                        OP("vector", "reciprocal", t_pwt[:, 1:2], t_pwt[:, 1:2])
                        OP("vector", "tensor_scalar", t_pwt[:, 4:5], t_pwt[:, 1:2], 224.0,
                           None, ALU.mult, sem=s_pwv)
                        pw_state["wd_scale_v"] = s_pwv.n
                    pw_rdv[0] = pw_rdv[1] = s_pwv.n
                else:
                    kb.wait("scalar", s_wf[0], pw_ld[0])
                    kb.wait("scalar", s_wf[1], pw_ld[1])
                    kb.wait("scalar", s_pwv, pw_state["wd_scale_v"])
                    for hf, hw_ in ((0, 1024), (1, w - 1024)):
                        pw_cast_store(hf, hw_,
                                      d_wd_sc[c * 128:(c + 1) * 128,
                                              o0 + hf * 1024:o0 + hf * 1024 + hw_],
                                      t_pwt[:, 4:5], wsl)
                    pw_rd[0] = pw_rd[1] = s_pwg.n
                    pw_tpwt_rd[0] = s_pwg.n
                    if step == 5:
                        pw_wq_wd[c] = (wsl, pw_wqw[wsl].n)
            if q + 1 < NQ_PW:
                pw_issue(q + 1)

        def pw_steps(n):
            for _ in range(n):
                if not pw_state["done"]:
                    pw_quantum()

        kb.dma("sync", t_b1[:, 0:4], d_bdkv.rearrange("(c p) -> p c", p=128), s_tab, allow_slow_non_contiguous=True)
        kb.dma("sync", t_b1[:, 4:8], d_bdq.rearrange("(c p) -> p c", p=128), s_tab, allow_slow_non_contiguous=True)
        kb.dma("sync", t_b1[:, 8:16], d_bupk.rearrange("(c p) -> p c", p=128), s_tab, allow_slow_non_contiguous=True)
        kb.dma("sync", t_b1[0:64, 16:17], d_bkr.rearrange("(p c) -> p c", c=1), s_tab)
        kb.dma("sync", t_bq[:, :], d_bqcomb.rearrange("(c p) -> p c", p=128), s_tab, allow_slow_non_contiguous=True)
        kb.dma("sync", t_didx[:, :], d_didx.rearrange("(c p) -> p c", p=128), s_tab, allow_slow_non_contiguous=True)
        kb.dma("gpsimd", t_bvrow[0:1, :], d_bupv[0:1, :], s_tb)
        TAB0 = s_tab.n
        OP("vector", "memset", t_id[:, :], 1.0, sem=s_v)
        OP("vector", "memset", t_ones[:, :], 1.0, sem=s_v)
        OP("vector", "memset", t_onesF[:, :], 1.0, sem=s_v)
        OP("vector", "memset", t_rtmp[:, 5:6], SC_ATT, sem=s_v)
        W("gpsimd", s_v)
        OP("gpsimd", "affine_select", t_id[:, :], t_id[:, :], pattern=[[1, 128]],
           compare_op=ALU.is_equal, fill=0.0, base=0, channel_multiplier=-1, sem=s_g)
        G_ID = s_g.n
        W("vector", s_g, G_ID)
        OP("vector", "tensor_copy", t_idb[:, :], t_id[:, :], sem=s_v)
        IDB_V = s_v.n


        ph_qx = reg(ExitStack())
        ph2 = reg(ExitStack())
        sb2 = lambda n, shape, dt, **kw: ph2.enter_context(nc.sbuf_tensor(n, shape, dt, **kw))
        t_attn = sb2("attn_fm", [128, 8 * T], BF16)
        t_aren = sb2("arena", [128, 4 * 2048], BF16)
        t_ckv = sb2("ckv_fm", [128, NDC * S], BF16)
        t_cq = sb2("cq_fm", [128, NDC * T], BF16)
        t_tabS = sb2("tabSs", [128, S], BF16)
        t_tabC = sb2("tabCs", [128, S], BF16)
        t_rop = sb2("ropeA", [128, S], BF16)
        t_rsw = sb2("ropeB", [128, T], BF16)
        ph_xn = reg(ExitStack())
        t_xnfm = ph_xn.enter_context(nc.sbuf_tensor("xn_fm", [128, NHC * S], BF16))
        ph1 = reg(ExitStack())
        sb1 = lambda n, shape, dt, **kw: ph1.enter_context(nc.sbuf_tensor(n, shape, dt, **kw))
        t_h = [sb1(f"h{i}", [128, H], F32) for i in range(2)]
        t_h2 = sb1("h2", [128, H], F32)
        t_xns = [sb1(f"xnb{i}", [128, H], BF16) for i in range(2)]
        t_q8tm = [sb1("q8tm0", [128, H], FP8)]
        t_qbtm = [sb1(f"qbtm{i}", [128, H], BF16) for i in range(4)]
        t_qxs = [sb1(f"qxs{i}", [128, 512], FP8) for i in range(2)]
        t_ss = sb1("lnss", [128, 2 * NKC], F32)
        t_rstd = sb1("rstd", [128, NKC], F32)
        t_nb = sb1("nbb", [128, NKC], F32)
        t_nm = sb1("negm", [128, NKC], F32)
        t_lt = sb1("lntmp", [128, NKC], F32)
        t_xmx = sb1("xmx", [128, NTC], F32)
        t_xmn = sb1("xmn", [128, NTC], F32)
        t_qsc = sb1("qsc", [128, NTC], F32)
        t_qbi = sb1("qbi", [128, NTC], F32)

        kb.dma("gpsimd", t_tabS[:, :], d_ts[:, :], s_tb)
        kb.dma("gpsimd", t_tabC[:, :], d_tcs[:, :], s_tb)
        TAB_ROPE = s_tb.n

        def h_src(kc):
            return (d_h_own if kc < NTC else d_h_oth)[(kc % NTC) * 128:(kc % NTC) * 128 + 128, :]

        vdone = {}
        for kc in range(NKC):
            sl = kc % 2
            if kc >= 2:
                W("sync", s_v, vdone[kc - 2])
            kb.dma("sync", t_h[sl][:, :], h_src(kc), s_h[sl])
            W("vector", s_h[sl], s_h[sl].n)
            OP("vector", "tensor_reduce", t_ss[:, 2 * kc:2 * kc + 1], t_h[sl][:, :], AX.X, ALU.add)
            OP("vector", "tensor_tensor", t_h2[:, :], t_h[sl][:, :], t_h[sl][:, :], ALU.mult)
            if kc < NTC:
                OP("vector", "tensor_reduce", t_xmx[:, kc:kc + 1], t_h[sl][:, :], AX.X, ALU.max)
                OP("vector", "tensor_reduce", t_xmn[:, kc:kc + 1], t_h[sl][:, :], AX.X, ALU.min)
            OP("vector", "tensor_reduce", t_ss[:, 2 * kc + 1:2 * kc + 2], t_h2[:, :], AX.X, ALU.add,
               sem=s_v)
            vdone[kc] = s_v.n

        OP("vector", "tensor_scalar", t_nm[:, :], t_ss[:, 0:2 * NKC:2], 1.0 / H, None, ALU.mult)
        OP("vector", "tensor_scalar", t_rstd[:, :], t_ss[:, 1:2 * NKC:2], 1.0 / H, None, ALU.mult)
        OP("vector", "tensor_tensor", t_lt[:, :], t_nm[:, :], t_nm[:, :], ALU.mult)
        OP("vector", "tensor_tensor", t_rstd[:, :], t_rstd[:, :], t_lt[:, :], ALU.subtract)
        OP("vector", "tensor_scalar", t_rstd[:, :], t_rstd[:, :], EPS, None, ALU.add)
        OP("vector", "tensor_copy", t_ss[:, 0:NKC], t_rstd[:, :])
        OP("vector", "reciprocal", t_rstd[:, :], t_rstd[:, :], sem=s_v)
        W("scalar", s_v)
        OP("scalar", "activation", t_rstd[:, :], t_rstd[:, :], AF.Sqrt, sem=s_s)
        W("vector", s_s)
        OP("vector", "tensor_tensor", t_lt[:, :], t_rstd[:, :], t_rstd[:, :], ALU.mult)
        OP("vector", "tensor_tensor", t_lt[:, :], t_lt[:, :], t_ss[:, 0:NKC], ALU.mult)
        OP("vector", "tensor_scalar", t_lt[:, :], t_lt[:, :], -0.5, 1.5, ALU.mult, ALU.add)
        OP("vector", "tensor_tensor", t_rstd[:, :], t_rstd[:, :], t_lt[:, :], ALU.mult)
        OP("vector", "tensor_tensor", t_nb[:, :], t_nm[:, :], t_rstd[:, :], ALU.mult)
        OP("vector", "tensor_scalar", t_nb[:, :], t_nb[:, :], -1.0, None, ALU.mult)
        OP("vector", "tensor_scalar", t_nm[:, :], t_nm[:, :], -1.0, None, ALU.mult, sem=s_v)
        OP("vector", "tensor_tensor", t_xmx[:, :], t_xmx[:, :], t_nm[:, 0:NTC], ALU.add)
        OP("vector", "tensor_tensor", t_xmn[:, :], t_xmn[:, :], t_nm[:, 0:NTC], ALU.add)
        OP("vector", "tensor_scalar", t_xmn[:, :], t_xmn[:, :], -1.0, None, ALU.mult)
        OP("vector", "tensor_tensor", t_xmx[:, :], t_xmx[:, :], t_xmn[:, :], ALU.max)
        OP("vector", "tensor_tensor", t_xam[:, 0:NTC], t_xmx[:, :], t_rstd[:, 0:NTC], ALU.mult)
        OP("vector", "reciprocal", t_xmn[:, :], t_xam[:, 0:NTC])
        OP("vector", "tensor_scalar", t_xmn[:, :], t_xmn[:, :], 224.0, None, ALU.mult)
        OP("vector", "tensor_tensor", t_qsc[:, :], t_rstd[:, 0:NTC], t_xmn[:, :], ALU.mult)
        OP("vector", "tensor_tensor", t_qbi[:, :], t_nb[:, 0:NTC], t_xmn[:, :], ALU.mult,
           sem=s_v)
        V1B = s_v.n

        psum_ro = [p_a, p_b, p_c, p_d]
        pidx = [0]
        W("scalar", s_v, V1B)
        W("tensor", s_v, IDB_V)
        sdone = {}
        xn_done = {}
        qb_copy = {}
        trp_free = [0, 0]
        qtrp_free = [0, 0, 0, 0]
        qxs_st = [0, 0]
        for kc in range(NKC):
            sl = kc % 2
            sx = kc % 4
            if kc >= 2:
                W("sync", s_s, sdone[kc - 2])
            else:
                W("sync", s_v, vdone[NKC - 2 + sl])
            kb.dma("sync", t_h[sl][:, :], h_src(kc), s_h[sl])
            W("scalar", s_h[sl], s_h[sl].n)
            if trp_free[sl]:
                W("scalar", s_t, trp_free[sl])
            OP("scalar", "activation", t_xns[sl][:, :], t_h[sl][:, :], AF.Identity,
               bias=t_nb[:, kc:kc + 1], scale=t_rstd[:, kc:kc + 1], sem=s_s)
            xn_done[kc] = s_s.n
            if kc < NTC:
                if kc >= 1:
                    W("scalar", s_v, qb_copy[kc - 1])
                OP("scalar", "activation", t_q8tm[0][:, :], t_h[sl][:, :], AF.Identity,
                   bias=t_qbi[:, kc:kc + 1], scale=t_qsc[:, kc:kc + 1], sem=s_s)
                W("vector", s_s, s_s.n)
                if qtrp_free[sx]:
                    W("vector", s_t, qtrp_free[sx])
                OP("vector", "tensor_copy", t_qbtm[sx][:, :], t_q8tm[0][:, :], sem=s_v)
                qb_copy[kc] = s_v.n
            sdone[kc] = s_s.n
            if sx % 2 == 1:
                g0 = kc - 1
                W("tensor", s_s, xn_done[kc])
                for f in range(NHC):
                    pg = psum_ro[pidx[0] % 4]
                    pidx[0] += 1
                    kb.wait_bank("tensor", pg)
                    for j in range(2):
                        OP("tensor", "matmul", pg[:, j * 128:(j + 1) * 128],
                           t_xns[j][:, f * 128:(f + 1) * 128], t_idb[:, :],
                           start=True, stop=True, sem=(s_t if j == 1 else None))
                    W("scalar", s_t, s_t.n)
                    OP("scalar", "copy", t_xnfm[:, f * S + g0 * 128:f * S + g0 * 128 + 256],
                       pg[:, 0:256], sem=s_s)
                    kb.set_bank(pg, s_s)
                trp_free[0] = trp_free[1] = s_t.n
            if sx == 3 and kc < NTC:
                g0 = kc - 3
                W("tensor", s_v, qb_copy[kc])
                for f in range(NHC):
                    sq = f % 2
                    pg = psum_ro[pidx[0] % 4]
                    pidx[0] += 1
                    kb.wait_bank("tensor", pg)
                    for j in range(4):
                        OP("tensor", "matmul", pg[:, j * 128:(j + 1) * 128],
                           t_qbtm[j][:, f * 128:(f + 1) * 128], t_idb[:, :],
                           start=True, stop=True, sem=(s_t if j == 3 else None))
                    W("vector", s_t, s_t.n)
                    if qxs_st[sq]:
                        W("vector", s_qxst[sq], qxs_st[sq])
                    OP("vector", "tensor_copy", t_qxs[sq][:, :], pg[:, :], sem=s_v)
                    kb.set_bank(pg, s_v)
                    W("scalar", s_v, s_v.n)
                    kb.dma("scalar", d_qxf_sc[f * 128:(f + 1) * 128, g0 * 128:g0 * 128 + 512],
                           t_qxs[sq][:, :], s_qxst[sq])
                    qxs_st[sq] = s_qxst[sq].n
                for j in range(4):
                    qtrp_free[j] = s_t.n
        XNFM_S = s_s.n
        V_QX = s_v.n
        QXW_ALL = [s.n for s in s_qxst]
        V_XAM = V1B

        cut(1)
        unreg(ph1)
        W("sync", s_v, V_XAM)
        kb.dma("sync", d_rs1[:, 0:NTC], t_xam[:, 0:NTC], s_mv)
        RS1_ST = s_mv.n

        cut(2)

        def arena_load(dram_ap, nchunks, ncols, after_t):
            W("gpsimd", s_t, after_t)
            kb.dma("gpsimd", t_aren[:, 0:nchunks * ncols].rearrange("p (f d) -> p f d", d=ncols),
                   dram_ap.rearrange("(f p) d -> p f d", p=128), s_wa)
            return s_wa.n

        def mm_group(dst_ap_fn, lhs_list, bias_col, out_width=512):
            pp = psum_ro[pidx[0] % 4]
            pidx[0] += 1
            kb.wait_bank("tensor", pp)
            n = len(lhs_list)
            for i, (lhsT, rhs) in enumerate(lhs_list):
                OP("tensor", "matmul", pp[:, 0:out_width] if out_width != 512 else pp[:, :],
                   lhsT, rhs, start=(i == 0), stop=(i == n - 1),
                   sem=(s_t if i == n - 1 else None))
            W("scalar", s_t, s_t.n)
            dst = dst_ap_fn(pp)
            OP("scalar", "activation", dst, pp[:, :] if out_width == 512 else pp[:, 0:out_width],
               AF.Identity, bias=bias_col, scale=1.0, sem=s_s)
            kb.set_bank(pp, s_s)
            pw_steps(1)
            return s_s.n

        W("scalar", s_tab, TAB0)
        W("tensor", s_tab, TAB0)

        W("gpsimd", s_xw[0], s_xw[0].n)
        W("gpsimd", s_xw[1], s_xw[1].n)
        W("gpsimd", s_qxw, s_qxw.n)
        W("gpsimd", s_s, s_s.n)
        W("gpsimd", s_v, s_v.n)
        AW = arena_load(d_wdkvT, NHC, DOWN, after_t=0)
        W("tensor", s_wa, AW)
        W("tensor", s_s, XNFM_S)
        for m in range(NDC):
            for blk in range(4):
                mm_group(
                    lambda pp, m=m, blk=blk: t_ckv[:, m * S + blk * 512:m * S + (blk + 1) * 512],
                    [(t_aren[:, f * DOWN + m * 128:f * DOWN + (m + 1) * 128],
                      t_xnfm[:, f * S + blk * 512:f * S + (blk + 1) * 512]) for f in range(NHC)],
                    t_b1[:, m:m + 1])
        T_CKV = s_t.n
        S_CKV = s_s.n

        AW = arena_load(d_wdqT, NHC, DOWN, after_t=T_CKV)
        W("tensor", s_wa, AW)
        for m in range(NDC):
            for blk in range(2):
                mm_group(
                    lambda pp, m=m, blk=blk: t_cq[:, m * T + blk * 512:m * T + (blk + 1) * 512],
                    [(t_aren[:, f * DOWN + m * 128:f * DOWN + (m + 1) * 128],
                      t_xnfm[:, f * S + blk * 512:f * S + (blk + 1) * 512]) for f in range(NHC)],
                    t_b1[:, 4 + m:5 + m])
        T_CQ = s_t.n
        S_CQ = s_s.n

        AW = arena_load(d_wkrT, NHC, RD, after_t=T_CQ)
        W("tensor", s_wa, AW)
        for blk in range(4):
            pw_steps(1)
            pp = psum_ro[pidx[0] % 4]
            pidx[0] += 1
            kb.wait_bank("tensor", pp)
            for f in range(NHC):
                OP("tensor", "matmul", pp[0:64, :],
                   t_aren[:, f * RD:(f + 1) * RD],
                   t_xnfm[:, f * S + blk * 512:f * S + (blk + 1) * 512],
                   start=(f == 0), stop=(f == NHC - 1), sem=(s_t if f == NHC - 1 else None))
            W("scalar", s_t, s_t.n)
            OP("scalar", "activation", t_rop[0:64, blk * 512:(blk + 1) * 512], pp[0:64, :],
               AF.Identity, bias=t_b1[0:64, 16:17], scale=1.0, sem=s_s)
            kb.set_bank(pp, s_s)
        S_KRP = s_s.n
        T_KRP = s_t.n
        unreg(ph_xn)
        phB = reg(ExitStack())
        sbB = lambda n, shape, dt, **kw: phB.enter_context(nc.sbuf_tensor(n, shape, dt, **kw))
        t_kall = sbB("K_all", [128, 8 * S], BF16)
        t_qall = sbB("Q_all", [128, NH * T], BF16)
        t_vaug = sbB("V_aug", [128, NKC * VSTR], BF16)
        t_kr2 = sbB("Kr2", [128, S], BF16)

        W("vector", s_t, T_KRP)
        OP("vector", "memset", t_vaug[:, :], 1.0, sem=s_v)
        V_VAUG1 = s_v.n

        rope_done = {0: 0, 64: 0}

        def rope_apply(src, dst, qr_base, width, tab0=0):
            W("sync", s_s, s_s.n)
            W("sync", s_v, s_v.n)
            kb.dma("sync", t_rsw[qr_base:qr_base + 32, 0:width],
                   src[qr_base + 32:qr_base + 64, 0:width], s_rsw)
            kb.dma("sync", t_rsw[qr_base + 32:qr_base + 64, 0:width],
                   src[qr_base:qr_base + 32, 0:width], s_rsw)
            W("vector", s_rsw, s_rsw.n)
            W("vector", s_tb, TAB_ROPE)
            OP("vector", "tensor_tensor", src[qr_base:qr_base + 64, 0:width],
               src[qr_base:qr_base + 64, 0:width],
               t_tabS[qr_base:qr_base + 64, tab0:tab0 + width], ALU.mult)
            OP("vector", "tensor_tensor", t_rsw[qr_base:qr_base + 64, 0:width],
               t_rsw[qr_base:qr_base + 64, 0:width],
               t_tabC[qr_base:qr_base + 64, tab0:tab0 + width], ALU.mult)
            OP("vector", "tensor_tensor", dst,
               src[qr_base:qr_base + 64, 0:width], t_rsw[qr_base:qr_base + 64, 0:width], ALU.add,
               sem=s_v)
            rope_done[qr_base] = s_v.n
            return s_v.n

        W("vector", s_s, S_KRP)
        rope_apply(t_rop[:, 0:T], t_kr2[0:64, 0:T], 0, T, tab0=0)
        V_KR = rope_apply(t_rop[:, T:S], t_kr2[0:64, T:S], 0, T, tab0=T)
        W("sync", s_v, V_KR)
        kb.dma("sync", t_kr2[64:128, :], t_kr2[0:64, :], s_kr2)
        MV_KR = s_kr2.n

        AW = arena_load(d_wupkT, NDC, UP, after_t=T_KRP)
        W("tensor", s_wa, AW)
        W("tensor", s_s, S_CKV)
        W("scalar", s_v, V_QX)
        for _i in range(2):
            W("scalar", s_qxst[_i], QXW_ALL[_i])
        for kd in range(8):
            for blk in range(4):
                mm_group(
                    lambda pp, kd=kd, blk=blk: t_kall[:, kd * S + blk * 512:kd * S + (blk + 1) * 512],
                    [(t_aren[:, dc * UP + kd * 128:dc * UP + (kd + 1) * 128],
                      t_ckv[:, dc * S + blk * 512:dc * S + (blk + 1) * 512]) for dc in range(NDC)],
                    t_b1[:, 8 + kd:9 + kd])
        T_KC = s_t.n

        AW = arena_load(d_wqcomb, NDC, 2048, after_t=T_KC)
        W("tensor", s_wa, AW)
        W("tensor", s_s, S_CQ)
        for hh in range(NH):
            par = hh % 2
            qc_base = 64 * par
            qr_base = 64 * (1 - par)
            for blk in range(2):
                pw_steps(1)
                pp = psum_ro[pidx[0] % 4]
                pidx[0] += 1
                kb.wait_bank("tensor", pp)
                for dc in range(NDC):
                    OP("tensor", "matmul", pp[:, :],
                       t_aren[:, dc * 2048 + hh * 128:dc * 2048 + (hh + 1) * 128],
                       t_cq[:, dc * T + blk * 512:dc * T + (blk + 1) * 512],
                       start=(dc == 0), stop=(dc == NDC - 1), sem=(s_t if dc == NDC - 1 else None))
                W("scalar", s_t, s_t.n)
                if rope_done[qr_base]:
                    W("scalar", s_v, rope_done[qr_base])
                OP("scalar", "activation",
                   t_qall[qc_base:qc_base + 64, hh * T + blk * 512:hh * T + (blk + 1) * 512],
                   pp[qc_base:qc_base + 64, :], AF.Identity, bias=t_bq[qc_base:qc_base + 64, hh:hh + 1],
                   scale=1.0)
                OP("scalar", "activation",
                   t_rop[qr_base:qr_base + 64, blk * 512:(blk + 1) * 512],
                   pp[qr_base:qr_base + 64, :], AF.Identity,
                   bias=t_bq[qr_base:qr_base + 64, hh:hh + 1], scale=1.0, sem=s_s)
                kb.set_bank(pp, s_s)
            W("vector", s_s, s_s.n)
            rope_apply(t_rop, t_qall[qr_base:qr_base + 64, hh * T:(hh + 1) * T], qr_base, T)
        T_Q = s_t.n
        V_Q = s_v.n

        AW = arena_load(d_wupvT, NDC, UP, after_t=T_Q)
        W("tensor", s_wa, AW)
        W("tensor", s_tb, s_tb.n)
        W("tensor", s_v, V_VAUG1)
        for kc in range(NKC):
            pw_steps(1)
            pp = p_e
            kb.wait_bank("tensor", pp)
            for half in range(2):
                for dc in range(NDC):
                    OP("tensor", "matmul", pp[:, half * 512:(half + 1) * 512],
                       t_ckv[:, dc * S + kc * 128:dc * S + (kc + 1) * 128],
                       t_aren[:, dc * UP + half * 512:dc * UP + (half + 1) * 512],
                       start=(dc == 0), stop=False)
                OP("tensor", "matmul", pp[:, half * 512:(half + 1) * 512],
                   t_ones[0:1, 0:128], t_bvrow[0:1, half * 512:(half + 1) * 512],
                   start=False, stop=True, sem=s_t)
            W("scalar", s_t, s_t.n)
            for hh in range(NH):
                OP("scalar", "copy",
                   t_vaug[:, kc * VSTR + hh * 65:kc * VSTR + hh * 65 + 64],
                   pp[:, hh * 64:(hh + 1) * 64], sem=(s_s if hh == NH - 1 else None))
            kb.set_bank(pp, s_s)
        T_V = s_t.n
        S_VE = s_s.n

        cut(3)
        ph3 = reg(ExitStack())
        t_kst = t_rop
        t_kst2 = t_ckv[:, 3072:5120]
        kstb = [t_kst, t_kst2]
        t_exp2 = [t_ckv[:, 1024:2048], t_ckv[:, 2048:3072]]
        t_nrm = t_ckv[:, 0:512]
        t_atmp = t_ckv[:, 512:1024]

        W("tensor", s_v, V_Q)
        W("tensor", s_s, S_VE)
        W("tensor", s_kr2, MV_KR)
        exp_done = [kb.bank.get(id(p_e), (s_s, 0))[1], kb.bank.get(id(p_f), (s_s, 0))[1]]
        nrm_used = 0
        expmm_done = [0, 0]
        attn_ev = kb.bank.get(id(p_c), (s_v, 0))
        nrm_ev = 0
        W("vector", s_kr2, MV_KR)
        W("vector", s_s, S_VE)
        kst_free = [0, 0]
        kst_ready = [0, 0]

        def stage_kst(hh):
            par = hh % 2
            kd = hh // 2
            qc_base = 64 * par
            qr_base = 64 * (1 - par)
            kb_ = kstb[hh % 2]
            if kst_free[hh % 2]:
                W("vector", s_t, kst_free[hh % 2])
            OP("vector", "tensor_copy", kb_[qc_base:qc_base + 64, :],
               t_kall[qc_base:qc_base + 64, kd * S:(kd + 1) * S])
            OP("vector", "tensor_copy", kb_[qr_base:qr_base + 64, :],
               t_kr2[qr_base:qr_base + 64, :], sem=s_v)
            kst_ready[hh % 2] = s_v.n

        stage_kst(0)
        for hh in range(NH):
            par = hh % 2
            kd = hh // 2
            qc_base = 64 * par
            qr_base = 64 * (1 - par)
            if hh + 1 < NH:
                stage_kst(hh + 1)
            t_ksc = kstb[hh % 2]
            W("tensor", s_v, kst_ready[hh % 2])
            for qb in range(2):
                pv = [p_c, p_d][(hh * 2 + qb) % 2]
                for kk in range(NKC // 2):
                    if kk in (2, 5):
                        pw_steps(1)
                    sl = kk % 2
                    pp = [p_e, p_f][sl]
                    W("tensor", s_s, exp_done[sl])
                    for j in range(2):
                        kc = 2 * kk + j
                        OP("tensor", "matmul", pp[:, j * 512:(j + 1) * 512],
                           t_ksc[:, kc * 128:(kc + 1) * 128],
                           t_qall[:, hh * T + qb * 512:hh * T + (qb + 1) * 512],
                           start=True, stop=True, sem=(s_t if j == 1 else None))
                    sc_val = s_t.n
                    W("scalar", s_t, sc_val)
                    if expmm_done[sl]:
                        W("scalar", s_t, expmm_done[sl])
                    OP("scalar", "activation", t_exp2[sl], pp[:, :], AF.Exp,
                       scale=t_rtmp[:, 5:6], sem=s_s)
                    exp_done[sl] = s_s.n
                    kb.set_bank(pp, s_s)
                    W("tensor", s_s, exp_done[sl])
                    if kk == 0:
                        kb.wait_bank("tensor", pv)
                    for j in range(2):
                        kc = 2 * kk + j
                        OP("tensor", "matmul", pv[0:65, :],
                           t_vaug[:, kc * VSTR + hh * 65:kc * VSTR + (hh + 1) * 65],
                           t_exp2[sl][:, j * 512:(j + 1) * 512],
                           start=(kc == 0), stop=(kc == NKC - 1),
                           sem=(s_t if kc == NKC - 1 else None))
                    expmm_done[sl] = s_t.n
                T_ATT = s_t.n
                kst_free[hh % 2] = sc_val
                pw_steps(1)
                W("vector", s_t, T_ATT)
                OP("vector", "reciprocal", t_row1[0:1, 0:512], pv[64:65, :], sem=s_v)
                W("tensor", s_v, s_v.n)
                if nrm_used:
                    W("tensor", s_s, nrm_used)
                else:
                    kb.wait_bank("tensor", p_a)
                OP("tensor", "matmul", p_a[0:64, 0:512], t_onesF[0:1, 0:64],
                   t_row1[0:1, 0:512], start=True, stop=True, sem=s_t)
                W("scalar", s_t, s_t.n)
                if nrm_ev:
                    W("scalar", s_v, nrm_ev)
                OP("scalar", "copy", t_nrm[0:64, 0:512], p_a[0:64, 0:512], sem=s_s)
                nrm_used = s_s.n
                W("vector", s_s, s_s.n)
                if par == 0:
                    OP("vector", "tensor_tensor",
                       t_attn[0:64, kd * T + qb * 512:kd * T + (qb + 1) * 512],
                       pv[0:64, :], t_nrm[0:64, 0:512], ALU.mult, sem=s_v)
                else:
                    W("vector", s_ash, s_ash.n)
                    OP("vector", "tensor_tensor", t_atmp[0:64, 0:512],
                       pv[0:64, :], t_nrm[0:64, 0:512], ALU.mult, sem=s_v)
                    W("sync", s_v, s_v.n)
                    kb.dma("sync", t_attn[64:128, kd * T + qb * 512:kd * T + (qb + 1) * 512],
                           t_atmp[0:64, 0:512], s_ash)
                nrm_ev = s_v.n
                kb.set_bank(pv, s_v)
        T_ATT_ALL = s_t.n
        V_ATT_ALL = s_v.n
        MV_ATT = s_ash.n
        kb.set_bank(p_a, s_s)

        cut(4)
        if DEBUG:
            W("sync", s_v, V_ATT_ALL)
            W("sync", s_ash, MV_ATT)
            W("sync", s_t, T_ATT_ALL)
            kb.dma("sync", d_dbg_attn[:, :], t_attn[:, :], s_mv)
            kb.dma("sync", d_dbg_q[:, :], t_qall[:, :], s_mv)
            kb.dma("sync", d_dbg_k[:, :], t_kall[:, :], s_mv)
            kb.dma("sync", d_dbg_kr[:, :], t_kr2[:, :], s_mv)
            kb.dma("sync", d_dbg_v[:, :], t_vaug[:, :], s_mv)
            kb.dma("sync", d_dbg_ckv[:, :], t_ckv[:, :], s_mv)
            kb.dma("sync", d_dbg_cq[:, :], t_cq[:, :], s_mv)
            W("tensor", s_mv, s_mv.n)
            W("gpsimd", s_mv, s_mv.n)
            W("scalar", s_mv, s_mv.n)
            W("vector", s_mv, s_mv.n)

        unreg(ph3)
        unreg(phB)
        ph35 = reg(ExitStack())
        t_wfc = ph35.enter_context(nc.sbuf_tensor("wfcA", [128, 8 * 2048], BF16))
        t_fcst = [ph35.enter_context(nc.sbuf_tensor(f"fcst{i}", [128, 512], F32)) for i in range(2)]
        t_bfc = ph35.enter_context(nc.sbuf_tensor("bfcr", [1, H], BF16))
        W("gpsimd", s_t, T_ATT_ALL)
        kb.dma("gpsimd", t_bfc[0:1, :], d_bfc[0:1, :], s_tb)
        W("tensor", s_tb, s_tb.n)
        AW = kb.dma("gpsimd", t_wfc[:, :].rearrange("p (f d) -> p f d", d=2048),
                    d_wfcT.rearrange("(f p) d -> p f d", p=128), s_wa)
        W("tensor", s_wa, AW)
        W("tensor", s_v, V_ATT_ALL)
        W("tensor", s_ash, MV_ATT)
        for tc in range(NTC):
            for nb in range(4):
                pp = [p_a, p_b][nb % 2]
                kb.wait_bank("tensor", pp)
                for vc in range(8):
                    OP("tensor", "matmul", pp[:, :],
                       t_attn[:, vc * T + tc * 128:vc * T + (tc + 1) * 128],
                       t_wfc[:, vc * 2048 + nb * 512:vc * 2048 + (nb + 1) * 512],
                       start=(vc == 0), stop=False)
                OP("tensor", "matmul", pp[:, :], t_ones[0:1, 0:128],
                   t_bfc[0:1, nb * 512:(nb + 1) * 512], start=False, stop=True, sem=s_t)
                W("scalar", s_t, s_t.n)
                fsl = nb % 2
                pw_steps(1)
                if tc * 4 + nb >= 2:
                    W("scalar", s_fcw, s_fcw.n)
                OP("scalar", "copy", t_fcst[fsl][:, :], pp[:, :], sem=s_s)
                kb.set_bank(pp, s_s)
                W("scalar", s_s, s_s.n)
                kb.dma("scalar", d_fc_sc[tc * 128:(tc + 1) * 128, nb * 512:(nb + 1) * 512],
                       t_fcst[fsl][:, :], s_fcw, drain=True)
        FCW_ALL = s_fcw.n
        S_FC = s_s.n

        unreg(ph35)
        unreg(ph2)

        cut(5)
        pw_steps(400)
        WQW_ALL = 0


        cut(6)
        ph45 = reg(ExitStack())
        t_wd8 = ph45.enter_context(nc.sbuf_tensor("wd8", [128, NIC * 2048], FP8))
        t_wdT = [ph45.enter_context(nc.sbuf_tensor(f"wdT{i}", [128, H], BF16,
                                                   is_transpose_dest=True)) for i in range(2)]
        ph4 = reg(ExitStack())
        sb4 = lambda n, shape, dt, **kw: ph4.enter_context(nc.sbuf_tensor(n, shape, dt, **kw))
        ISB = 4
        NSB = NIC // ISB
        t_wsbB = [sb4(f"wsbB{i}", [128, 512], BF16, is_transpose_dest=True) for i in range(4)]
        t_w8g = [sb4(f"w8g{i}", [128, NHC * 512], FP8) for i in range(2)]
        t_w8u = [sb4(f"w8u{i}", [128, NHC * 512], FP8) for i in range(2)]
        t_sg = sb4("sgb", [128, T], F32)
        t_su = sb4("sub", [128, T], F32)
        t_tb = [sb4(f"t_st{i}", [128, T], F32) for i in range(2)]
        t_tmx = sb4("tmx", [128, T], F32)
        t_tmn = sb4("tmn", [128, T], F32)
        t_qxfm = sb4("qx_fm", [128, NHC * T], FP8)
        t_tfacb = sb4("tfacb", [128, T], F32)

        W("gpsimd", s_s, S_FC)
        for _i in range(2):
            W("gpsimd", s_qxst[_i], QXW_ALL[_i])
        kb.dma("gpsimd", t_qxfm[:, :].rearrange("p (f d) -> p f d", d=T),
               d_qxf_sc.rearrange("(f p) d -> p f d", p=128), s_qxl)
        QXL = s_qxl.n

        W("sync", s_t, T_ATT_ALL)
        W("sync", s_mv, RS1_ST)
        for half in range(2):
            kb.dma("sync", t_row1[0:1, :],
                   d_rs1[:, half * 4:half * 4 + 4].rearrange("p c -> c p"), s_mv,
                   allow_slow_non_contiguous=True)
            W("vector", s_mv, s_mv.n)
            W("vector", s_t, T_ATT_ALL)
            OP("vector", "tensor_scalar", t_row1[0:1, :], t_row1[0:1, :], 4.0 / 448.0, None,
               ALU.mult, sem=s_v)
            W("tensor", s_v, s_v.n)
            pp = psum_ro[pidx[0] % 4]
            pidx[0] += 1
            kb.wait_bank("tensor", pp)
            OP("tensor", "matmul", pp[:, :], t_onesF[0:1, 0:128],
               t_row1[0:1, :], start=True, stop=True, sem=s_t)
            W("scalar", s_t, s_t.n)
            OP("scalar", "copy", t_tfacb[:, half * 512:(half + 1) * 512], pp[:, :], sem=s_s)
            kb.set_bank(pp, s_s)
            W("sync", s_t, s_t.n)
        V_TFACB = s_s.n

        W("sync", s_fcw, FCW_ALL)
        W("gpsimd", s_fcw, FCW_ALL)
        W("vector", s_fcw, FCW_ALL)
        W("tensor", s_qxl, QXL)
        W("tensor", s_s, V_TFACB)

        vqt_done = {}
        wdT_done = [0, 0]
        cast_done = [0, 0, 0, 0]
        w8_free = [0, 0]
        tmax_init = [False]
        w8_ready = [0, 0]

        def stage_w8(sbi):
            sl = sbi % 2
            need = [0, 0]
            for cc in range(sbi * ISB, (sbi + 1) * ISB):
                for tbl in (pw_wq_gate, pw_wq_up):
                    wsl, wv = tbl[cc]
                    need[wsl] = max(need[wsl], wv)
            for wsl in range(2):
                if need[wsl]:
                    W("sync", pw_wqw[wsl], need[wsl])
            if w8_free[sl]:
                W("scalar", s_t, w8_free[sl])
            for mat, w8 in ((0, t_w8g), (1, t_w8u)):
                dsc = (d_wg_sc, d_wu_sc)[mat]
                for hc in range(NHC):
                    tsl = (mat * NHC + hc) % 4
                    if cast_done[tsl]:
                        W("sync", s_s, cast_done[tsl])
                    kb.dmaT("sync", t_wsbB[tsl][:, :],
                            dsc[sbi * 512:(sbi + 1) * 512, hc * 128:(hc + 1) * 128], s_wt[tsl])
                    W("scalar", s_wt[tsl], s_wt[tsl].n)
                    OP("scalar", "copy", w8[sl][:, hc * 512:(hc + 1) * 512],
                       t_wsbB[tsl][:, :], sem=s_s)
                    cast_done[tsl] = s_s.n
            w8_ready[sl] = s_s.n

        stage_w8(0)
        for sbi in range(NSB):
            sl = sbi % 2
            if sbi + 1 < NSB:
                stage_w8(sbi + 1)
            W("tensor", s_s, w8_ready[sl])
            for ci in range(ISB):
                c = sbi * ISB + ci
                for pp, w8 in ((p_e, t_w8g), (p_f, t_w8u)):
                    kb.wait_bank("tensor", pp)
                    for half in range(2):
                        for hc in range(NHC):
                            OP("tensor", "matmul", pp[:, half * 512:(half + 1) * 512],
                               w8[sl][:, hc * 512 + ci * 128:hc * 512 + (ci + 1) * 128],
                               t_qxfm[:, hc * T + half * 512:hc * T + (half + 1) * 512],
                               start=(hc == 0), stop=(hc == NHC - 1),
                               sem=(s_t if (hc == NHC - 1 and half == 1) else None))
                T_GU = s_t.n
                W("vector", s_t, T_GU)
                OP("vector", "tensor_tensor", t_sg[:, :], p_e[:, :], t_tfacb[:, :], ALU.mult,
                   sem=s_v)
                W("scalar", s_v, s_v.n)
                OP("scalar", "activation", t_sg[:, :], t_sg[:, :], AF.Silu,
                   scale=t_gfac[:, c:c + 1], sem=s_s)
                W("vector", s_s, s_s.n)
                tsl = c % 2
                if c >= 2:
                    W("vector", s_tw, s_tw.n)
                OP("vector", "tensor_tensor", t_su[:, :], p_f[:, :], t_tfacb[:, :], ALU.mult)
                OP("vector", "scalar_tensor_tensor", t_tb[tsl][:, :],
                   t_su[:, :], t_ufac[:, c:c + 1], t_sg[:, :], ALU.mult, ALU.mult)
                if not tmax_init[0]:
                    OP("vector", "tensor_copy", t_tmx[:, :], t_tb[tsl][:, :])
                    OP("vector", "tensor_copy", t_tmn[:, :], t_tb[tsl][:, :], sem=s_v)
                    tmax_init[0] = True
                else:
                    OP("vector", "tensor_tensor", t_tmx[:, :], t_tmx[:, :],
                       t_tb[tsl][:, :], ALU.max)
                    OP("vector", "tensor_tensor", t_tmn[:, :], t_tmn[:, :],
                       t_tb[tsl][:, :], ALU.min, sem=s_v)
                kb.set_bank(p_e, s_v)
                kb.set_bank(p_f, s_v)
                W("scalar", s_v, s_v.n)
                kb.dma("scalar", d_t_sc[c * 128:(c + 1) * 128, :], t_tb[tsl][:, :], s_tw)
            w8_free[sl] = s_t.n
            if sbi == 0:
                need_wd = [0, 0]
                for cc in range(NHC):
                    wsl_c, wv_c = pw_wq_wd[cc]
                    need_wd[wsl_c] = max(need_wd[wsl_c], wv_c)
                for wsl_c in range(2):
                    if need_wd[wsl_c]:
                        W("sync", pw_wqw[wsl_c], need_wd[wsl_c])
                W("sync", s_s, S_FC)
            for c in range(sbi * 4, sbi * 4 + 4):
                dsl = c % 2
                if wdT_done[dsl]:
                    W("sync", s_s, wdT_done[dsl])
                kb.dmaT("sync", t_wdT[dsl][:, :], d_wd_sc[:, c * 128:(c + 1) * 128], s_wdt[dsl])
                W("scalar", s_wdt[dsl], s_wdt[dsl].n)
                OP("scalar", "copy", t_wd8[:, c * 2048:(c + 1) * 2048], t_wdT[dsl][:, :],
                   sem=s_s)
                wdT_done[dsl] = s_s.n
        G_WD8 = s_s.n
        V_T_ALL = s_v.n
        TW_ALL = s_tw.n

        W("tensor", s_g, G_ID)
        OP("vector", "tensor_scalar", t_tmn[:, :], t_tmn[:, :], -1.0, None, ALU.mult)
        OP("vector", "tensor_tensor", t_tmx[:, :], t_tmx[:, :], t_tmn[:, :], ALU.max, sem=s_v)
        V_TMX = s_v.n
        W("tensor", s_v, V_TMX)
        for tb in range(NTC):
            pp = [p_a, p_b][tb % 2]
            kb.wait_bank("tensor", pp)
            OP("tensor", "transpose", pp[:, 0:128], t_tmx[:, tb * 128:(tb + 1) * 128], t_id[:, :],
               sem=s_t)
            W("vector", s_t, s_t.n)
            OP("vector", "tensor_reduce", t_tamr[:, tb:tb + 1], pp[:, 0:128], AX.X, ALU.max, sem=s_v)
            kb.set_bank(pp, s_v)
        V_TAMR = s_v.n
        T_AMX = s_t.n
        W("sync", s_v, V_TAMR)
        kb.dma("sync", d_rs1[:, 0:NTC], t_tamr[:, 0:NTC], s_mv)
        W("sync", s_mv, s_mv.n)
        unreg(ph4)
        unreg(ph_qx)

        cut(7)
        ph5 = reg(ExitStack())
        sb5 = lambda n, shape, dt, **kw: ph5.enter_context(nc.sbuf_tensor(n, shape, dt, **kw))
        t_qsclb = sb5("qsclb", [128, T], F32)
        t_qgf = [sb5(f"qgf{i}", [128, T], F32) for i in range(2)]
        t_qperm = sb5("qperm", [128, NIC * T], FP8)
        t_swdb = sb5("swdb", [128, H], F32)
        t_ho = sb5("hor", [128, H], F32)
        t_fcb = sb5("fcbr", [128, H], F32)
        t_outb = sb5("outbr", [128, H], F32)

        for half in range(2):
            kb.dma("sync", t_row1[0:1, :],
                   d_rs1[:, half * 4:half * 4 + 4].rearrange("p c -> c p"), s_mv,
                   allow_slow_non_contiguous=True)
            W("vector", s_mv, s_mv.n)
            OP("vector", "reciprocal", t_row1[0:1, :], t_row1[0:1, :])
            OP("vector", "tensor_scalar", t_row1[0:1, :], t_row1[0:1, :], 224.0, None,
               ALU.mult, sem=s_v)
            W("tensor", s_v, s_v.n)
            pp = [p_a, p_b][half]
            kb.wait_bank("tensor", pp)
            OP("tensor", "matmul", pp[:, :], t_onesF[0:1, 0:128],
               t_row1[0:1, :], start=True, stop=True, sem=s_t)
            W("vector", s_t, s_t.n)
            OP("vector", "tensor_copy", t_qsclb[:, half * 512:(half + 1) * 512], pp[:, :], sem=s_v)
            kb.set_bank(pp, s_v)
            W("sync", s_t, s_t.n)
        V_QSCLB = s_v.n

        kb.dma("sync", d_rs1[:, 0:NHC], t_dfac[:, 0:NHC], s_mv)
        W("sync", s_mv, s_mv.n)
        for q4 in range(4):
            kb.dma("sync", t_row1[0:1, :],
                   d_rs1[:, q4 * 4:q4 * 4 + 4].rearrange("p c -> c p"), s_mv,
                   allow_slow_non_contiguous=True)
            W("tensor", s_mv, s_mv.n)
            pp = [p_a, p_b][q4 % 2]
            kb.wait_bank("tensor", pp)
            OP("tensor", "matmul", pp[:, :], t_onesF[0:1, 0:128],
               t_row1[0:1, :], start=True, stop=True, sem=s_t)
            W("vector", s_t, s_t.n)
            OP("vector", "tensor_copy", t_swdb[:, q4 * 512:(q4 + 1) * 512], pp[:, :], sem=s_v)
            kb.set_bank(pp, s_v)
            W("sync", s_t, s_t.n)
        V_SWDB = s_v.n

        W("gpsimd", s_tw, TW_ALL)
        W("gpsimd", s_t, T_AMX)
        W("gpsimd", s_v, V_TAMR)
        W("vector", s_tw, TW_ALL)
        W("vector", s_t, T_AMX)
        vqt = {}
        gat = {}
        for c in range(NIC):
            sl = c % 2
            if c >= 2:
                W("gpsimd", s_v, vqt[c - 2])
            kb.gather(t_qgf[sl][:, :], d_t_sc[:, :], t_didx[:, c:c + 1], s_qtg[sl])
            gat[c] = s_qtg[sl].n
            W("vector", s_qtg[sl], gat[c])
            OP("vector", "tensor_tensor", t_qperm[:, c * T:(c + 1) * T],
               t_qgf[sl][:, :], t_qsclb[:, :], ALU.mult, sem=s_v)
            vqt[c] = s_v.n
        V_QTT_ALL = s_v.n

        W("tensor", s_s, G_WD8)
        W("tensor", s_v, V_SWDB)
        W("scalar", s_v, V_QTT_ALL)
        for tc in range(NTC):
            if tc >= 1:
                W("scalar", s_out, s_out.n)
            W("scalar", s_fcw, FCW_ALL)
            kb.dma("scalar", t_ho[:, :], d_h_own[tc * 128:(tc + 1) * 128, :], s_hor)
            kb.dma("scalar", t_fcb[:, :], d_fc_sc[tc * 128:(tc + 1) * 128, :], s_fcr)
            for nb in range(4):
                pp = [p_c, p_d][nb % 2]
                kb.wait_bank("tensor", pp)
                for c in range(NIC):
                    if tc == 0 and nb == 0:
                        W("tensor", s_v, vqt[c])
                    OP("tensor", "matmul", pp[:, :],
                       t_qperm[:, c * T + tc * 128:c * T + (tc + 1) * 128],
                       t_wd8[:, c * 2048 + nb * 512:c * 2048 + (nb + 1) * 512],
                       start=(c == 0), stop=(c == NIC - 1), sem=(s_t if c == NIC - 1 else None))
                T_DN = s_t.n
                W("vector", s_t, T_DN)
                W("vector", s_hor, s_hor.n)
                W("vector", s_fcr, s_fcr.n)
                if nb == 0:
                    OP("vector", "tensor_scalar", t_rtmp[:, 2:3], t_tamr[:, tc:tc + 1],
                       4.0 / 448.0, None, ALU.mult)
                OP("vector", "tensor_scalar", t_outb[:, nb * 512:(nb + 1) * 512], pp[:, :],
                   t_rtmp[:, 2:3], None, ALU.mult)
                OP("vector", "tensor_tensor", t_outb[:, nb * 512:(nb + 1) * 512],
                   t_outb[:, nb * 512:(nb + 1) * 512], t_swdb[:, nb * 512:(nb + 1) * 512], ALU.mult)
                OP("vector", "tensor_tensor", t_outb[:, nb * 512:(nb + 1) * 512],
                   t_outb[:, nb * 512:(nb + 1) * 512], t_fcb[:, nb * 512:(nb + 1) * 512], ALU.add)
                OP("vector", "tensor_tensor", t_outb[:, nb * 512:(nb + 1) * 512],
                   t_outb[:, nb * 512:(nb + 1) * 512], t_ho[:, nb * 512:(nb + 1) * 512], ALU.add,
                   sem=s_v)
                kb.set_bank(pp, s_v)
            W("scalar", s_v, s_v.n)
            kb.dma("scalar", d_out[tc * 128:(tc + 1) * 128, :], t_outb[:, :], s_out)
        OUT_ALL = s_out.n
        unreg(ph5)
        unreg(ph45)

    except _Cut:
        pass
    for _st in reversed(_stacks):
        _st.close()
    if s_out.n:
        W("gpsimd", s_out, s_out.n)

    blk = nc.Block()
    with blk:
        kb.flush(blk)
    es.close()
    p_es.close()
    return nc


_NC_CACHE = {}


def _get_nc():
    if "nc" not in _NC_CACHE:
        _NC_CACHE["nc"] = build_nc()
    return _NC_CACHE["nc"]


def _host_prep(inputs):
    x = np.ascontiguousarray(np.asarray(inputs["L0_attn_h"], dtype=np.float32))

    wupq = np.asarray(inputs["L0_attn_W_up_q"], dtype=np.float32)
    wqr = np.asarray(inputs["L0_attn_W_qr"], dtype=np.float32)
    bupq = np.asarray(inputs["L0_attn_b_up_q"], dtype=np.float32)
    bqr = np.asarray(inputs["L0_attn_b_qr"], dtype=np.float32)

    deint = np.concatenate([np.arange(0, RD, 2), np.arange(1, RD, 2)])

    wq_cols, bq_cols = [], []
    for h in range(NH):
        qc = wupq[h * 64:(h + 1) * 64, :].T
        qr = wqr[h * 64 + deint, :].T
        bc = bupq[h * 64:(h + 1) * 64]
        br = bqr[h * 64 + deint]
        if h % 2 == 0:
            wq_cols += [qc, qr]
            bq_cols += [bc, br]
        else:
            wq_cols += [qr, qc]
            bq_cols += [br, bc]
    wqcomb = np.ascontiguousarray(np.concatenate(wq_cols, axis=1), dtype=np.float32)
    bqcomb = np.ascontiguousarray(np.concatenate(bq_cols), dtype=np.float32)

    wkr = np.asarray(inputs["L0_attn_W_kr"], dtype=np.float32)
    wkrT = np.ascontiguousarray(wkr[deint, :].T)
    bkr = np.ascontiguousarray(np.asarray(inputs["L0_attn_b_kr"], dtype=np.float32)[deint])

    common = {
        "wdkvT": np.ascontiguousarray(np.asarray(inputs["L0_attn_W_down_kv"], np.float32).T),
        "wdqT": np.ascontiguousarray(np.asarray(inputs["L0_attn_W_down_q"], np.float32).T),
        "wupkT": np.ascontiguousarray(np.asarray(inputs["L0_attn_W_up_k"], np.float32).T),
        "wupvT": np.ascontiguousarray(np.asarray(inputs["L0_attn_W_up_v"], np.float32).T),
        "wqcomb": wqcomb,
        "wkrT": wkrT,
        "wfcT": np.ascontiguousarray(np.asarray(inputs["L0_attn_W_fc"], np.float32).T),
        "bdkv": np.ascontiguousarray(np.asarray(inputs["L0_attn_b_down_kv"], np.float32)),
        "bupk": np.ascontiguousarray(np.asarray(inputs["L0_attn_b_up_k"], np.float32)),
        "bupv": np.ascontiguousarray(np.asarray(inputs["L0_attn_b_up_v"], np.float32)[None, :]),
        "bdq": np.ascontiguousarray(np.asarray(inputs["L0_attn_b_down_q"], np.float32)),
        "bqcomb": bqcomb,
        "bkr": bkr,
        "bfc": np.ascontiguousarray(np.asarray(inputs["L0_attn_b_fc"], np.float32)[None, :]),
        "wg_nat": np.ascontiguousarray(np.asarray(inputs["L0_ffn_w_gate"], np.float32)),
        "wu_nat": np.ascontiguousarray(np.asarray(inputs["L0_ffn_w_up"], np.float32)),
        "wd_nat": np.ascontiguousarray(np.asarray(inputs["L0_ffn_w_down"], np.float32)),
        "didx": np.ascontiguousarray(np.asarray(inputs["L0_ffn_down_reorder_index"], np.int32)),
    }

    theta = (1.0 / (10000.0 ** (np.arange(0, RD, 2, dtype=np.float32) / np.float32(RD)))).astype(np.float32)

    in_maps = []
    for core in range(8):
        b, half = divmod(core, 2)
        own = slice(half * T, (half + 1) * T)
        oth = slice((1 - half) * T, (2 - half) * T)
        pos = np.concatenate([np.arange(half * T, (half + 1) * T, dtype=np.float32),
                              np.arange((1 - half) * T, (2 - half) * T, dtype=np.float32)])
        ang = (pos[:, None] * theta[None, :]).astype(np.float32)
        sin_t = np.sin(ang).astype(np.float32).T
        cos_t = np.cos(ang).astype(np.float32).T
        tabS = np.ascontiguousarray(np.tile(sin_t, (4, 1)).astype(np.float32))
        tabC = np.ascontiguousarray(
            np.concatenate([-cos_t, cos_t, -cos_t, cos_t], axis=0).astype(np.float32))
        m = dict(common)
        m["h_own"] = np.ascontiguousarray(x[b, own, :])
        m["h_oth"] = np.ascontiguousarray(x[b, oth, :])
        m["tabS"] = tabS
        m["tabCsgn"] = tabC
        in_maps.append(m)
    return in_maps


def kernel(**inputs):
    in_maps = _host_prep(inputs)
    nc = _get_nc()
    res = run_bass_kernel_spmd(nc, in_maps, core_ids=list(range(8)))
    out = np.empty((4, 2048, 2048), dtype=np.float32)
    for core in range(8):
        b, half = divmod(core, 2)
        out[b, half * T:(half + 1) * T, :] = res.results[core]["y"]
    return out


# version 20 (final)
# speedup vs baseline: 1.0286x; 1.0012x over previous
import math
from contextlib import ExitStack

import numpy as np
import concourse.bass as bass
import concourse.mybir as mybir
from concourse.bass import IndirectOffsetOnAxis
from concourse.bass_utils import run_bass_kernel_spmd

F32 = mybir.dt.float32
BF16 = mybir.dt.bfloat16
FP8 = mybir.dt.float8e4
I32 = mybir.dt.int32
AF = mybir.ActivationFunctionType
ALU = mybir.AluOpType
AX = mybir.AxisListType

S, T, H = 2048, 1024, 2048
DOWN, UP, NH, RD, VD, FI = 512, 1024, 16, 64, 64, 5632
SC_ATT = 1.0 / math.sqrt(192.0)
EPS = 1e-5
NHC, NKC, NTC, NIC, NDC = H // 128, S // 128, T // 128, FI // 128, DOWN // 128
VSTR = NH * 65

ENGS = ("sync", "vector", "scalar", "tensor", "gpsimd")
DEBUG = False
PH_LIM = 99


class _Cut(Exception):
    pass


class Sem:
    def __init__(self, es, nc, name):
        self.h = es.enter_context(nc.semaphore(name))
        self.n = 0


class KB:

    def __init__(self, nc, es):
        self.nc = nc
        self.es = es
        self.q = {k: [] for k in ENGS}
        self._semid = 0
        self.bank = {}
        self.wset = {e: set() for e in ENGS}
        self.rset = {e: set() for e in ENGS}

    def sem(self, name=None):
        self._semid += 1
        return Sem(self.es, self.nc, name or f"sm{self._semid}")

    def emit(self, eng, fn):
        self.q[eng].append(fn)

    def wait(self, eng, sem, val=None):
        v = sem.n if val is None else val
        if v > 0:
            self.emit(eng, lambda e: e.wait_ge(sem.h, v))

    def wait_bank(self, eng, pp):
        ent = self.bank.get(id(pp))
        if ent is not None:
            self.wait(eng, ent[0], ent[1])

    def set_bank(self, pp, sem):
        self.bank[id(pp)] = (sem, sem.n)

    def dma(self, eng, out, in_, sem, drain=False, **kw):
        sem.n += 16
        if eng in ("vector", "scalar") and not drain:
            sid = self._tid(in_)
            if sid is not None and sid in self.wset[eng]:
                drain = True
        if drain:
            if eng in ("vector", "scalar"):
                self.wset[eng].clear()
                self.rset[eng].clear()
            self.emit(eng, lambda e: e.drain())
        self.emit(eng, lambda e: e.dma_start(out=out, in_=in_, **kw).then_inc(sem.h, 16))
        return sem.n

    def dmaT(self, eng, out, in_, sem):
        sem.n += 16
        self.emit(eng, lambda e: e.dma_start_transpose(out=out, in_=in_).then_inc(sem.h, 16))
        return sem.n

    def gather(self, out, in_, idx_ap, sem):
        sem.n += 16
        self.emit("gpsimd", lambda e: e.indirect_dma_start(
            out=out, out_offset=None, in_=in_,
            in_offset=IndirectOffsetOnAxis(idx_ap, axis=0)).then_inc(sem.h, 16))
        return sem.n

    @staticmethod
    def _tid(x):
        t = getattr(x, "tensor", None)
        return id(t) if t is not None else None

    def _hazard(self, eng, outs, ins):
        w, r = self.wset[eng], self.rset[eng]
        need = (not ins.isdisjoint(w)) or (not outs.isdisjoint(w)) \
            or (not outs.isdisjoint(r))
        if need:
            w.clear()
            r.clear()
        w |= outs
        r |= ins
        return need

    def op(self, eng, meth, *a, sem=None, **kw):
        pre = False
        if eng in ("vector", "scalar"):
            outs = {self._tid(a[0])} - {None}
            ins = {self._tid(x) for x in a[1:]} | {self._tid(x) for x in kw.values()}
            ins -= {None}
            pre = self._hazard(eng, outs, ins)
        if sem is None:
            if pre:
                self.emit(eng, lambda e: (e.drain(), getattr(e, meth)(*a, **kw)))
            else:
                self.emit(eng, lambda e: getattr(e, meth)(*a, **kw))
            return None
        sem.n += 1
        if pre:
            self.emit(eng, lambda e: (e.drain(), getattr(e, meth)(*a, **kw).then_inc(sem.h, 1)))
        else:
            self.emit(eng, lambda e: getattr(e, meth)(*a, **kw).then_inc(sem.h, 1))
        return sem.n

    def flush(self, blk):
        for name in ENGS:
            fns = self.q[name]
            if not fns:
                continue
            section = getattr(blk, name)

            def body(e, fns=fns):
                for f in fns:
                    f(e)
            section(body)


def build_nc():
    nc = bass.Bass("TRN2", target_bir_lowering=False, debug=False)
    D = lambda name, shape, dt, kind="ExternalInput": nc.dram_tensor(name, shape, dt, kind=kind).ap()

    d_h_own = D("h_own", [T, H], F32)
    d_h_oth = D("h_oth", [T, H], F32)
    d_wdkvT = D("wdkvT", [H, DOWN], F32)
    d_wdqT = D("wdqT", [H, DOWN], F32)
    d_wupkT = D("wupkT", [DOWN, UP], F32)
    d_wupvT = D("wupvT", [DOWN, UP], F32)
    d_wqcomb = D("wqcomb", [DOWN, 2048], F32)
    d_wkrT = D("wkrT", [H, RD], F32)
    d_wfcT = D("wfcT", [UP, H], F32)
    d_bdkv = D("bdkv", [DOWN], F32)
    d_bupk = D("bupk", [UP], F32)
    d_bupv = D("bupv", [1, UP], F32)
    d_bdq = D("bdq", [DOWN], F32)
    d_bqcomb = D("bqcomb", [2048], F32)
    d_bkr = D("bkr", [RD], F32)
    d_bfc = D("bfc", [1, H], F32)
    d_wg = D("wg_nat", [FI, H], F32)
    d_wu = D("wu_nat", [FI, H], F32)
    d_wd = D("wd_nat", [H, FI], F32)
    d_didx = D("didx", [FI], I32)
    d_ts = D("tabS", [128, S], F32)
    d_tcs = D("tabCsgn", [128, S], F32)
    d_out = D("y", [T, H], F32, "ExternalOutput")

    d_wg_sc = D("wg_sc", [FI, H], BF16, "Internal")
    d_wu_sc = D("wu_sc", [FI, H], BF16, "Internal")
    d_wd_sc = D("wd_sc", [H, FI], BF16, "Internal")
    d_t_sc = D("t_sc", [FI, T], F32, "Internal")
    d_qxf_sc = D("qxf_sc", [NHC * 128, T], FP8, "Internal")
    d_fc_sc = D("fc_sc", [T, H], F32, "Internal")
    d_rs1 = D("rs1", [128, 16], F32, "Internal")
    if DEBUG:
        d_dbg_attn = D("dbg_attn", [128, 8 * T], BF16, "Internal")
        d_dbg_q = D("dbg_q", [128, NH * T], BF16, "Internal")
        d_dbg_k = D("dbg_k", [128, 8 * S], BF16, "Internal")
        d_dbg_kr = D("dbg_kr", [128, S], BF16, "Internal")
        d_dbg_v = D("dbg_v", [128, NKC * VSTR], BF16, "Internal")
        d_dbg_ckv = D("dbg_ckv", [128, NDC * S], BF16, "Internal")
        d_dbg_cq = D("dbg_cq", [128, NDC * T], BF16, "Internal")

    es = ExitStack()
    sb = lambda n, shape, dt, **kw: es.enter_context(nc.sbuf_tensor(n, shape, dt, **kw))
    kb = KB(nc, es)

    t_id = sb("ident", [128, 128], F32)
    t_idb = sb("identb", [128, 128], BF16)
    t_ones = sb("onesr", [1, 128], BF16)
    t_onesF = sb("onesrF", [1, 128], F32)
    t_bvrow = sb("bvrow", [1, UP], BF16)
    t_b1 = sb("b1", [128, 20], F32)
    t_bq = sb("bq", [128, 16], F32)
    t_didx = sb("didxs", [128, NIC], I32)
    t_xam = sb("xam", [128, NTC], F32)
    t_gfac = sb("gfac", [128, NIC], F32)
    t_ufac = sb("ufac", [128, NIC], F32)
    t_dfac = sb("dfac", [128, NHC], F32)
    t_tamr = sb("tamr", [128, NTC], F32)
    t_row1 = sb("row1", [1, 512], F32)
    t_rtmp = sb("rtmp", [128, 6], F32)
    t_pwf = [sb("pwf0", [128, 1024], F32), sb("pwf1", [128, 1024], F32)]
    t_pw8 = [sb("pw8a", [128, 1024], FP8)] * 2
    t_pwb = sb("pwba", [128, 1024], BF16)
    t_pwt = sb("pwt", [128, 6], F32)

    p_es = ExitStack()
    p_a = p_es.enter_context(nc.psum_tensor("p_a", [128, 512], F32))
    p_b = p_es.enter_context(nc.psum_tensor("p_b", [128, 512], F32))
    p_c = p_es.enter_context(nc.psum_tensor("p_c", [128, 512], F32))
    p_d = p_es.enter_context(nc.psum_tensor("p_d", [128, 512], F32))
    p_e = p_es.enter_context(nc.psum_tensor("p_e", [128, 1024], F32))
    p_f = p_es.enter_context(nc.psum_tensor("p_f", [128, 1024], F32))

    s_tab = kb.sem("s_tab")
    s_h = [kb.sem("s_h0"), kb.sem("s_h1")]
    s_xw = [kb.sem("s_xw0"), kb.sem("s_xw1")]
    s_xt = kb.sem("s_xt")
    s_wa = kb.sem("s_wa")
    s_tb = kb.sem("s_tb")
    s_wf = [kb.sem("s_wf0"), kb.sem("s_wf1")]
    s_pwv = kb.sem("s_pwv")
    s_pwg = kb.sem("s_pwg")
    s_wqw = kb.sem("s_wqw")
    s_wt = [kb.sem(f"s_wt{i}") for i in range(4)]
    s_wdt = [kb.sem("s_wdt0"), kb.sem("s_wdt1")]
    s_qtw = kb.sem("s_qtw")
    s_tw = kb.sem("s_tw")
    s_tr = [kb.sem("s_tr0"), kb.sem("s_tr1")]
    s_qxw = kb.sem("s_qxw")
    s_qxl = kb.sem("s_qxl")
    s_qxst = [kb.sem("s_qxst0"), kb.sem("s_qxst1")]
    s_qtg = [kb.sem("s_qtg0"), kb.sem("s_qtg1")]
    s_fcw = kb.sem("s_fcw")
    s_fcr = kb.sem("s_fcr")
    s_hor = kb.sem("s_hor")
    s_out = kb.sem("s_out")
    s_mv = kb.sem("s_mv")
    s_rsw = kb.sem("s_rsw")
    s_kr2 = kb.sem("s_kr2")
    s_ash = kb.sem("s_ash")
    s_v = kb.sem("s_v")
    s_s = kb.sem("s_s")
    s_t = kb.sem("s_t")
    s_g = kb.sem("s_g")

    W = kb.wait
    OP = kb.op

    _stacks = []

    def reg(st):
        _stacks.append(st)
        return st

    def unreg(st):
        st.close()
        if st in _stacks:
            _stacks.remove(st)

    def cut(k):
        if PH_LIM <= k:
            raise _Cut()

    try:
        WD_SEG = (2048, 2048, 1536)
        pw_state = {"q": 0, "done": False, "wd_scale_v": 0}
        pw_wq_gate = {}
        pw_wq_up = {}
        pw_wq_wd = {}
        pw_wqw = [kb.sem("s_pww0"), kb.sem("s_pww1")]
        pw_fac_v_slot = [0, 0]

        NQ_PW = 184
        pw_rd = [0, 0]
        pw_rdv = [0, 0]
        pw_ld = [0, 0]
        pw_tpwt_rd = [0]
        pw_store = [None, 0]

        def pw_geom(q):
            if q < 88:
                c = q // 2
                dram_in = d_wg if q % 2 == 0 else d_wu
                return [(dram_in[c * 128:(c + 1) * 128, 0:1024], 1024),
                        (dram_in[c * 128:(c + 1) * 128, 1024:2048], 1024)]
            qq = q - 88
            c = qq // 6
            step = qq % 6
            o0 = sum(WD_SEG[:step % 3])
            w = WD_SEG[step % 3]
            return [(d_wd[c * 128:(c + 1) * 128, o0:o0 + 1024], 1024),
                    (d_wd[c * 128:(c + 1) * 128, o0 + 1024:o0 + w], w - 1024)]

        def pw_issue(q):
            for sl, (src, w) in enumerate(pw_geom(q)):
                if pw_rd[sl]:
                    kb.wait("scalar", s_pwg, pw_rd[sl])
                if pw_rdv[sl]:
                    kb.wait("scalar", s_pwv, pw_rdv[sl])
                kb.dma("scalar", t_pwf[sl][:, 0:w], src, s_wf[sl])
                pw_ld[sl] = s_wf[sl].n

        def pw_cast_store(hf, w, dst_ap, scale_col, wsl):
            OP("scalar", "activation", t_pw8[0][:, 0:w], t_pwf[hf][:, 0:w],
               AF.Copy, scale=scale_col, sem=s_pwg)
            kb.wait("vector", s_pwg, s_pwg.n)
            if pw_store[0] is not None:
                kb.wait("vector", pw_store[0], pw_store[1])
            OP("vector", "tensor_copy", t_pwb[:, 0:w], t_pw8[0][:, 0:w], sem=s_pwv)
            kb.wait("scalar", s_pwv, s_pwv.n)
            kb.dma("scalar", dst_ap, t_pwb[:, 0:w], pw_wqw[wsl])
            pw_store[0] = pw_wqw[wsl]
            pw_store[1] = pw_wqw[wsl].n

        def pw_quantum():
            q = pw_state["q"]
            if q >= NQ_PW:
                pw_state["done"] = True
                return
            pw_state["q"] += 1
            wsl = q % 2
            if q == 0:
                pw_issue(0)
            halves = pw_geom(q)
            if q < 88:
                mat = "g" if q % 2 == 0 else "u"
                c = q // 2
                dram_sc = d_wg_sc if mat == "g" else d_wu_sc
                fac = t_gfac if mat == "g" else t_ufac
                kb.wait("vector", s_wf[0], pw_ld[0])
                if pw_tpwt_rd[0]:
                    kb.wait("vector", s_pwg, pw_tpwt_rd[0])
                OP("vector", "tensor_reduce", t_pwt[:, 0:1], t_pwf[0][:, :], AX.X, ALU.max,
                   apply_absolute_value=True)
                kb.wait("vector", s_wf[1], pw_ld[1])
                OP("vector", "tensor_reduce", t_pwt[:, 5:6], t_pwf[1][:, :], AX.X, ALU.max,
                   apply_absolute_value=True)
                OP("vector", "tensor_tensor", t_pwt[:, 0:1], t_pwt[:, 0:1], t_pwt[:, 5:6],
                   ALU.max)
                OP("vector", "tensor_scalar", fac[:, c:c + 1], t_pwt[:, 0:1], 1.0 / 448.0,
                   None, ALU.mult)
                OP("vector", "reciprocal", t_pwt[:, 1:2], t_pwt[:, 0:1])
                OP("vector", "tensor_scalar", t_pwt[:, 2:3], t_pwt[:, 1:2], 224.0,
                   None, ALU.mult, sem=s_pwv)
                pw_rdv[0] = pw_rdv[1] = s_pwv.n
                kb.wait("scalar", s_pwv, s_pwv.n)
                for hf in range(2):
                    pw_cast_store(hf, 1024,
                                  dram_sc[c * 128:(c + 1) * 128, hf * 1024:(hf + 1) * 1024],
                                  t_pwt[:, 2:3], wsl)
                pw_rd[0] = pw_rd[1] = s_pwg.n
                pw_tpwt_rd[0] = s_pwg.n
                (pw_wq_gate if mat == "g" else pw_wq_up)[c] = (wsl, pw_wqw[wsl].n)
            else:
                qq = q - 88
                c = qq // 6
                step = qq % 6
                o0 = sum(WD_SEG[:step % 3])
                w = WD_SEG[step % 3]
                if step < 3:
                    kb.wait("vector", s_wf[0], pw_ld[0])
                    if pw_tpwt_rd[0]:
                        kb.wait("vector", s_pwg, pw_tpwt_rd[0])
                    OP("vector", "tensor_reduce", t_pwt[:, 0:1], t_pwf[0][:, :], AX.X,
                       ALU.max, apply_absolute_value=True)
                    kb.wait("vector", s_wf[1], pw_ld[1])
                    OP("vector", "tensor_reduce", t_pwt[:, 5:6], t_pwf[1][:, 0:w - 1024],
                       AX.X, ALU.max, apply_absolute_value=True)
                    OP("vector", "tensor_tensor", t_pwt[:, 0:1], t_pwt[:, 0:1],
                       t_pwt[:, 5:6], ALU.max)
                    if step == 0:
                        OP("vector", "tensor_copy", t_pwt[:, 1:2], t_pwt[:, 0:1], sem=s_pwv)
                    else:
                        OP("vector", "tensor_tensor", t_pwt[:, 1:2], t_pwt[:, 1:2],
                           t_pwt[:, 0:1], ALU.max, sem=s_pwv)
                    if step == 2:
                        OP("vector", "tensor_scalar", t_dfac[:, c:c + 1], t_pwt[:, 1:2],
                           1.0 / 448.0, None, ALU.mult)
                        OP("vector", "reciprocal", t_pwt[:, 1:2], t_pwt[:, 1:2])
                        OP("vector", "tensor_scalar", t_pwt[:, 4:5], t_pwt[:, 1:2], 224.0,
                           None, ALU.mult, sem=s_pwv)
                        pw_state["wd_scale_v"] = s_pwv.n
                    pw_rdv[0] = pw_rdv[1] = s_pwv.n
                else:
                    kb.wait("scalar", s_wf[0], pw_ld[0])
                    kb.wait("scalar", s_wf[1], pw_ld[1])
                    kb.wait("scalar", s_pwv, pw_state["wd_scale_v"])
                    for hf, hw_ in ((0, 1024), (1, w - 1024)):
                        pw_cast_store(hf, hw_,
                                      d_wd_sc[c * 128:(c + 1) * 128,
                                              o0 + hf * 1024:o0 + hf * 1024 + hw_],
                                      t_pwt[:, 4:5], wsl)
                    pw_rd[0] = pw_rd[1] = s_pwg.n
                    pw_tpwt_rd[0] = s_pwg.n
                    if step == 5:
                        pw_wq_wd[c] = (wsl, pw_wqw[wsl].n)
            if q + 1 < NQ_PW:
                pw_issue(q + 1)

        def pw_steps(n):
            for _ in range(n):
                if not pw_state["done"]:
                    pw_quantum()

        kb.dma("sync", t_b1[:, 0:4], d_bdkv.rearrange("(c p) -> p c", p=128), s_tab, allow_slow_non_contiguous=True)
        kb.dma("sync", t_b1[:, 4:8], d_bdq.rearrange("(c p) -> p c", p=128), s_tab, allow_slow_non_contiguous=True)
        kb.dma("sync", t_b1[:, 8:16], d_bupk.rearrange("(c p) -> p c", p=128), s_tab, allow_slow_non_contiguous=True)
        kb.dma("sync", t_b1[0:64, 16:17], d_bkr.rearrange("(p c) -> p c", c=1), s_tab)
        kb.dma("sync", t_bq[:, :], d_bqcomb.rearrange("(c p) -> p c", p=128), s_tab, allow_slow_non_contiguous=True)
        kb.dma("sync", t_didx[:, :], d_didx.rearrange("(c p) -> p c", p=128), s_tab, allow_slow_non_contiguous=True)
        kb.dma("gpsimd", t_bvrow[0:1, :], d_bupv[0:1, :], s_tb)
        TAB0 = s_tab.n
        OP("vector", "memset", t_id[:, :], 1.0, sem=s_v)
        OP("vector", "memset", t_ones[:, :], 1.0, sem=s_v)
        OP("vector", "memset", t_onesF[:, :], 1.0, sem=s_v)
        OP("vector", "memset", t_rtmp[:, 5:6], SC_ATT, sem=s_v)
        W("gpsimd", s_v)
        OP("gpsimd", "affine_select", t_id[:, :], t_id[:, :], pattern=[[1, 128]],
           compare_op=ALU.is_equal, fill=0.0, base=0, channel_multiplier=-1, sem=s_g)
        G_ID = s_g.n
        W("vector", s_g, G_ID)
        OP("vector", "tensor_copy", t_idb[:, :], t_id[:, :], sem=s_v)
        IDB_V = s_v.n


        ph_qx = reg(ExitStack())
        ph2 = reg(ExitStack())
        sb2 = lambda n, shape, dt, **kw: ph2.enter_context(nc.sbuf_tensor(n, shape, dt, **kw))
        t_attn = sb2("attn_fm", [128, 8 * T], BF16)
        t_aren = sb2("arena", [128, 4 * 2048], BF16)
        t_ckv = sb2("ckv_fm", [128, NDC * S], BF16)
        t_cq = sb2("cq_fm", [128, NDC * T], BF16)
        t_tabS = sb2("tabSs", [128, S], BF16)
        t_tabC = sb2("tabCs", [128, S], BF16)
        t_rop = sb2("ropeA", [128, S], BF16)
        t_rsw = sb2("ropeB", [128, T], BF16)
        ph_xn = reg(ExitStack())
        t_xnfm = ph_xn.enter_context(nc.sbuf_tensor("xn_fm", [128, NHC * S], BF16))
        ph1 = reg(ExitStack())
        sb1 = lambda n, shape, dt, **kw: ph1.enter_context(nc.sbuf_tensor(n, shape, dt, **kw))
        t_h = [sb1(f"h{i}", [128, H], F32) for i in range(2)]
        t_h2 = sb1("h2", [128, H], F32)
        t_xns = [sb1(f"xnb{i}", [128, H], BF16) for i in range(2)]
        t_q8tm = [sb1("q8tm0", [128, H], FP8)]
        t_qbtm = [sb1(f"qbtm{i}", [128, H], BF16) for i in range(4)]
        t_qxs = [sb1(f"qxs{i}", [128, 512], FP8) for i in range(2)]
        t_ss = sb1("lnss", [128, 2 * NKC], F32)
        t_rstd = sb1("rstd", [128, NKC], F32)
        t_nb = sb1("nbb", [128, NKC], F32)
        t_nm = sb1("negm", [128, NKC], F32)
        t_lt = sb1("lntmp", [128, NKC], F32)
        t_xmx = sb1("xmx", [128, NTC], F32)
        t_xmn = sb1("xmn", [128, NTC], F32)
        t_qsc = sb1("qsc", [128, NTC], F32)
        t_qbi = sb1("qbi", [128, NTC], F32)

        kb.dma("gpsimd", t_tabS[:, :], d_ts[:, :], s_tb)
        kb.dma("gpsimd", t_tabC[:, :], d_tcs[:, :], s_tb)
        TAB_ROPE = s_tb.n

        def h_src(kc):
            return (d_h_own if kc < NTC else d_h_oth)[(kc % NTC) * 128:(kc % NTC) * 128 + 128, :]

        vdone = {}
        for kc in range(NKC):
            sl = kc % 2
            if kc >= 2:
                W("sync", s_v, vdone[kc - 2])
            kb.dma("sync", t_h[sl][:, :], h_src(kc), s_h[sl])
            W("vector", s_h[sl], s_h[sl].n)
            OP("vector", "tensor_reduce", t_ss[:, 2 * kc:2 * kc + 1], t_h[sl][:, :], AX.X, ALU.add)
            OP("vector", "tensor_tensor", t_h2[:, :], t_h[sl][:, :], t_h[sl][:, :], ALU.mult)
            if kc < NTC:
                OP("vector", "tensor_reduce", t_xmx[:, kc:kc + 1], t_h[sl][:, :], AX.X, ALU.max)
                OP("vector", "tensor_reduce", t_xmn[:, kc:kc + 1], t_h[sl][:, :], AX.X, ALU.min)
            OP("vector", "tensor_reduce", t_ss[:, 2 * kc + 1:2 * kc + 2], t_h2[:, :], AX.X, ALU.add,
               sem=s_v)
            vdone[kc] = s_v.n

        OP("vector", "tensor_scalar", t_nm[:, :], t_ss[:, 0:2 * NKC:2], 1.0 / H, None, ALU.mult)
        OP("vector", "tensor_scalar", t_rstd[:, :], t_ss[:, 1:2 * NKC:2], 1.0 / H, None, ALU.mult)
        OP("vector", "tensor_tensor", t_lt[:, :], t_nm[:, :], t_nm[:, :], ALU.mult)
        OP("vector", "tensor_tensor", t_rstd[:, :], t_rstd[:, :], t_lt[:, :], ALU.subtract)
        OP("vector", "tensor_scalar", t_rstd[:, :], t_rstd[:, :], EPS, None, ALU.add)
        OP("vector", "tensor_copy", t_ss[:, 0:NKC], t_rstd[:, :])
        OP("vector", "reciprocal", t_rstd[:, :], t_rstd[:, :], sem=s_v)
        W("scalar", s_v)
        OP("scalar", "activation", t_rstd[:, :], t_rstd[:, :], AF.Sqrt, sem=s_s)
        W("vector", s_s)
        OP("vector", "tensor_tensor", t_lt[:, :], t_rstd[:, :], t_rstd[:, :], ALU.mult)
        OP("vector", "tensor_tensor", t_lt[:, :], t_lt[:, :], t_ss[:, 0:NKC], ALU.mult)
        OP("vector", "tensor_scalar", t_lt[:, :], t_lt[:, :], -0.5, 1.5, ALU.mult, ALU.add)
        OP("vector", "tensor_tensor", t_rstd[:, :], t_rstd[:, :], t_lt[:, :], ALU.mult)
        OP("vector", "tensor_tensor", t_nb[:, :], t_nm[:, :], t_rstd[:, :], ALU.mult)
        OP("vector", "tensor_scalar", t_nb[:, :], t_nb[:, :], -1.0, None, ALU.mult)
        OP("vector", "tensor_scalar", t_nm[:, :], t_nm[:, :], -1.0, None, ALU.mult, sem=s_v)
        OP("vector", "tensor_tensor", t_xmx[:, :], t_xmx[:, :], t_nm[:, 0:NTC], ALU.add)
        OP("vector", "tensor_tensor", t_xmn[:, :], t_xmn[:, :], t_nm[:, 0:NTC], ALU.add)
        OP("vector", "tensor_scalar", t_xmn[:, :], t_xmn[:, :], -1.0, None, ALU.mult)
        OP("vector", "tensor_tensor", t_xmx[:, :], t_xmx[:, :], t_xmn[:, :], ALU.max)
        OP("vector", "tensor_tensor", t_xam[:, 0:NTC], t_xmx[:, :], t_rstd[:, 0:NTC], ALU.mult)
        OP("vector", "reciprocal", t_xmn[:, :], t_xam[:, 0:NTC])
        OP("vector", "tensor_scalar", t_xmn[:, :], t_xmn[:, :], 224.0, None, ALU.mult)
        OP("vector", "tensor_tensor", t_qsc[:, :], t_rstd[:, 0:NTC], t_xmn[:, :], ALU.mult)
        OP("vector", "tensor_tensor", t_qbi[:, :], t_nb[:, 0:NTC], t_xmn[:, :], ALU.mult,
           sem=s_v)
        V1B = s_v.n

        psum_ro = [p_a, p_b, p_c, p_d]
        pidx = [0]
        W("scalar", s_v, V1B)
        W("tensor", s_v, IDB_V)
        sdone = {}
        xn_done = {}
        qb_copy = {}
        trp_free = [0, 0]
        qtrp_free = [0, 0, 0, 0]
        qxs_st = [0, 0]
        for kc in range(NKC):
            sl = kc % 2
            sx = kc % 4
            if kc >= 2:
                W("sync", s_s, sdone[kc - 2])
            else:
                W("sync", s_v, vdone[NKC - 2 + sl])
            kb.dma("sync", t_h[sl][:, :], h_src(kc), s_h[sl])
            W("scalar", s_h[sl], s_h[sl].n)
            if trp_free[sl]:
                W("scalar", s_t, trp_free[sl])
            OP("scalar", "activation", t_xns[sl][:, :], t_h[sl][:, :], AF.Identity,
               bias=t_nb[:, kc:kc + 1], scale=t_rstd[:, kc:kc + 1], sem=s_s)
            xn_done[kc] = s_s.n
            if kc < NTC:
                if kc >= 1:
                    W("scalar", s_v, qb_copy[kc - 1])
                OP("scalar", "activation", t_q8tm[0][:, :], t_h[sl][:, :], AF.Identity,
                   bias=t_qbi[:, kc:kc + 1], scale=t_qsc[:, kc:kc + 1], sem=s_s)
                W("vector", s_s, s_s.n)
                if qtrp_free[sx]:
                    W("vector", s_t, qtrp_free[sx])
                OP("vector", "tensor_copy", t_qbtm[sx][:, :], t_q8tm[0][:, :], sem=s_v)
                qb_copy[kc] = s_v.n
            sdone[kc] = s_s.n
            if sx % 2 == 1:
                g0 = kc - 1
                W("tensor", s_s, xn_done[kc])
                for f in range(NHC):
                    pg = psum_ro[pidx[0] % 4]
                    pidx[0] += 1
                    kb.wait_bank("tensor", pg)
                    for j in range(2):
                        OP("tensor", "matmul", pg[:, j * 128:(j + 1) * 128],
                           t_xns[j][:, f * 128:(f + 1) * 128], t_idb[:, :],
                           start=True, stop=True, sem=(s_t if j == 1 else None))
                    W("scalar", s_t, s_t.n)
                    OP("scalar", "copy", t_xnfm[:, f * S + g0 * 128:f * S + g0 * 128 + 256],
                       pg[:, 0:256], sem=s_s)
                    kb.set_bank(pg, s_s)
                trp_free[0] = trp_free[1] = s_t.n
            if sx == 3 and kc < NTC:
                g0 = kc - 3
                W("tensor", s_v, qb_copy[kc])
                for f in range(NHC):
                    sq = f % 2
                    pg = psum_ro[pidx[0] % 4]
                    pidx[0] += 1
                    kb.wait_bank("tensor", pg)
                    for j in range(4):
                        OP("tensor", "matmul", pg[:, j * 128:(j + 1) * 128],
                           t_qbtm[j][:, f * 128:(f + 1) * 128], t_idb[:, :],
                           start=True, stop=True, sem=(s_t if j == 3 else None))
                    W("vector", s_t, s_t.n)
                    if qxs_st[sq]:
                        W("vector", s_qxst[sq], qxs_st[sq])
                    OP("vector", "tensor_copy", t_qxs[sq][:, :], pg[:, :], sem=s_v)
                    kb.set_bank(pg, s_v)
                    W("scalar", s_v, s_v.n)
                    kb.dma("scalar", d_qxf_sc[f * 128:(f + 1) * 128, g0 * 128:g0 * 128 + 512],
                           t_qxs[sq][:, :], s_qxst[sq])
                    qxs_st[sq] = s_qxst[sq].n
                for j in range(4):
                    qtrp_free[j] = s_t.n
        XNFM_S = s_s.n
        V_QX = s_v.n
        QXW_ALL = [s.n for s in s_qxst]
        V_XAM = V1B

        cut(1)
        unreg(ph1)
        W("sync", s_v, V_XAM)
        kb.dma("sync", d_rs1[:, 0:NTC], t_xam[:, 0:NTC], s_mv)
        RS1_ST = s_mv.n

        cut(2)

        def arena_load(dram_ap, nchunks, ncols, after_t):
            W("gpsimd", s_t, after_t)
            kb.dma("gpsimd", t_aren[:, 0:nchunks * ncols].rearrange("p (f d) -> p f d", d=ncols),
                   dram_ap.rearrange("(f p) d -> p f d", p=128), s_wa)
            return s_wa.n

        def mm_group(dst_ap_fn, lhs_list, bias_col, out_width=512):
            pp = psum_ro[pidx[0] % 4]
            pidx[0] += 1
            kb.wait_bank("tensor", pp)
            n = len(lhs_list)
            for i, (lhsT, rhs) in enumerate(lhs_list):
                OP("tensor", "matmul", pp[:, 0:out_width] if out_width != 512 else pp[:, :],
                   lhsT, rhs, start=(i == 0), stop=(i == n - 1),
                   sem=(s_t if i == n - 1 else None))
            W("scalar", s_t, s_t.n)
            dst = dst_ap_fn(pp)
            OP("scalar", "activation", dst, pp[:, :] if out_width == 512 else pp[:, 0:out_width],
               AF.Identity, bias=bias_col, scale=1.0, sem=s_s)
            kb.set_bank(pp, s_s)
            pw_steps(1)
            return s_s.n

        W("scalar", s_tab, TAB0)
        W("tensor", s_tab, TAB0)

        W("gpsimd", s_xw[0], s_xw[0].n)
        W("gpsimd", s_xw[1], s_xw[1].n)
        W("gpsimd", s_qxw, s_qxw.n)
        W("gpsimd", s_s, s_s.n)
        W("gpsimd", s_v, s_v.n)
        AW = arena_load(d_wdkvT, NHC, DOWN, after_t=0)
        W("tensor", s_wa, AW)
        W("tensor", s_s, XNFM_S)
        for m in range(NDC):
            for blk in range(4):
                mm_group(
                    lambda pp, m=m, blk=blk: t_ckv[:, m * S + blk * 512:m * S + (blk + 1) * 512],
                    [(t_aren[:, f * DOWN + m * 128:f * DOWN + (m + 1) * 128],
                      t_xnfm[:, f * S + blk * 512:f * S + (blk + 1) * 512]) for f in range(NHC)],
                    t_b1[:, m:m + 1])
        T_CKV = s_t.n
        S_CKV = s_s.n

        AW = arena_load(d_wdqT, NHC, DOWN, after_t=T_CKV)
        W("tensor", s_wa, AW)
        for m in range(NDC):
            for blk in range(2):
                mm_group(
                    lambda pp, m=m, blk=blk: t_cq[:, m * T + blk * 512:m * T + (blk + 1) * 512],
                    [(t_aren[:, f * DOWN + m * 128:f * DOWN + (m + 1) * 128],
                      t_xnfm[:, f * S + blk * 512:f * S + (blk + 1) * 512]) for f in range(NHC)],
                    t_b1[:, 4 + m:5 + m])
        T_CQ = s_t.n
        S_CQ = s_s.n

        AW = arena_load(d_wkrT, NHC, RD, after_t=T_CQ)
        W("tensor", s_wa, AW)
        for blk in range(4):
            pw_steps(1)
            pp = psum_ro[pidx[0] % 4]
            pidx[0] += 1
            kb.wait_bank("tensor", pp)
            for f in range(NHC):
                OP("tensor", "matmul", pp[0:64, :],
                   t_aren[:, f * RD:(f + 1) * RD],
                   t_xnfm[:, f * S + blk * 512:f * S + (blk + 1) * 512],
                   start=(f == 0), stop=(f == NHC - 1), sem=(s_t if f == NHC - 1 else None))
            W("scalar", s_t, s_t.n)
            OP("scalar", "activation", t_rop[0:64, blk * 512:(blk + 1) * 512], pp[0:64, :],
               AF.Identity, bias=t_b1[0:64, 16:17], scale=1.0, sem=s_s)
            kb.set_bank(pp, s_s)
        S_KRP = s_s.n
        T_KRP = s_t.n
        unreg(ph_xn)
        phB = reg(ExitStack())
        sbB = lambda n, shape, dt, **kw: phB.enter_context(nc.sbuf_tensor(n, shape, dt, **kw))
        t_kall = sbB("K_all", [128, 8 * S], BF16)
        t_qall = sbB("Q_all", [128, NH * T], BF16)
        t_vaug = sbB("V_aug", [128, NKC * VSTR], BF16)
        t_kr2 = sbB("Kr2", [128, S], BF16)

        W("vector", s_t, T_KRP)
        OP("vector", "memset", t_vaug[:, :], 1.0, sem=s_v)
        V_VAUG1 = s_v.n

        rope_done = {0: 0, 64: 0}

        def rope_apply(src, dst, qr_base, width, tab0=0):
            W("sync", s_s, s_s.n)
            W("sync", s_v, s_v.n)
            kb.dma("sync", t_rsw[qr_base:qr_base + 32, 0:width],
                   src[qr_base + 32:qr_base + 64, 0:width], s_rsw)
            kb.dma("sync", t_rsw[qr_base + 32:qr_base + 64, 0:width],
                   src[qr_base:qr_base + 32, 0:width], s_rsw)
            W("vector", s_rsw, s_rsw.n)
            W("vector", s_tb, TAB_ROPE)
            OP("vector", "tensor_tensor", src[qr_base:qr_base + 64, 0:width],
               src[qr_base:qr_base + 64, 0:width],
               t_tabS[qr_base:qr_base + 64, tab0:tab0 + width], ALU.mult)
            OP("vector", "tensor_tensor", t_rsw[qr_base:qr_base + 64, 0:width],
               t_rsw[qr_base:qr_base + 64, 0:width],
               t_tabC[qr_base:qr_base + 64, tab0:tab0 + width], ALU.mult)
            OP("vector", "tensor_tensor", dst,
               src[qr_base:qr_base + 64, 0:width], t_rsw[qr_base:qr_base + 64, 0:width], ALU.add,
               sem=s_v)
            rope_done[qr_base] = s_v.n
            return s_v.n

        W("vector", s_s, S_KRP)
        rope_apply(t_rop[:, 0:T], t_kr2[0:64, 0:T], 0, T, tab0=0)
        V_KR = rope_apply(t_rop[:, T:S], t_kr2[0:64, T:S], 0, T, tab0=T)
        W("sync", s_v, V_KR)
        kb.dma("sync", t_kr2[64:128, :], t_kr2[0:64, :], s_kr2)
        MV_KR = s_kr2.n

        AW = arena_load(d_wupkT, NDC, UP, after_t=T_KRP)
        W("tensor", s_wa, AW)
        W("tensor", s_s, S_CKV)
        W("scalar", s_v, V_QX)
        for _i in range(2):
            W("scalar", s_qxst[_i], QXW_ALL[_i])
        for kd in range(8):
            for blk in range(4):
                mm_group(
                    lambda pp, kd=kd, blk=blk: t_kall[:, kd * S + blk * 512:kd * S + (blk + 1) * 512],
                    [(t_aren[:, dc * UP + kd * 128:dc * UP + (kd + 1) * 128],
                      t_ckv[:, dc * S + blk * 512:dc * S + (blk + 1) * 512]) for dc in range(NDC)],
                    t_b1[:, 8 + kd:9 + kd])
        T_KC = s_t.n

        AW = arena_load(d_wqcomb, NDC, 2048, after_t=T_KC)
        W("tensor", s_wa, AW)
        W("tensor", s_s, S_CQ)
        for hh in range(NH):
            par = hh % 2
            qc_base = 64 * par
            qr_base = 64 * (1 - par)
            for blk in range(2):
                pw_steps(1)
                pp = psum_ro[pidx[0] % 4]
                pidx[0] += 1
                kb.wait_bank("tensor", pp)
                for dc in range(NDC):
                    OP("tensor", "matmul", pp[:, :],
                       t_aren[:, dc * 2048 + hh * 128:dc * 2048 + (hh + 1) * 128],
                       t_cq[:, dc * T + blk * 512:dc * T + (blk + 1) * 512],
                       start=(dc == 0), stop=(dc == NDC - 1), sem=(s_t if dc == NDC - 1 else None))
                W("scalar", s_t, s_t.n)
                if rope_done[qr_base]:
                    W("scalar", s_v, rope_done[qr_base])
                OP("scalar", "activation",
                   t_qall[qc_base:qc_base + 64, hh * T + blk * 512:hh * T + (blk + 1) * 512],
                   pp[qc_base:qc_base + 64, :], AF.Identity, bias=t_bq[qc_base:qc_base + 64, hh:hh + 1],
                   scale=1.0)
                OP("scalar", "activation",
                   t_rop[qr_base:qr_base + 64, blk * 512:(blk + 1) * 512],
                   pp[qr_base:qr_base + 64, :], AF.Identity,
                   bias=t_bq[qr_base:qr_base + 64, hh:hh + 1], scale=1.0, sem=s_s)
                kb.set_bank(pp, s_s)
            W("vector", s_s, s_s.n)
            rope_apply(t_rop, t_qall[qr_base:qr_base + 64, hh * T:(hh + 1) * T], qr_base, T)
        T_Q = s_t.n
        V_Q = s_v.n

        AW = arena_load(d_wupvT, NDC, UP, after_t=T_Q)
        W("tensor", s_wa, AW)
        W("tensor", s_tb, s_tb.n)
        W("tensor", s_v, V_VAUG1)
        for kc in range(NKC):
            pw_steps(1)
            pp = p_e
            kb.wait_bank("tensor", pp)
            for half in range(2):
                for dc in range(NDC):
                    OP("tensor", "matmul", pp[:, half * 512:(half + 1) * 512],
                       t_ckv[:, dc * S + kc * 128:dc * S + (kc + 1) * 128],
                       t_aren[:, dc * UP + half * 512:dc * UP + (half + 1) * 512],
                       start=(dc == 0), stop=False)
                OP("tensor", "matmul", pp[:, half * 512:(half + 1) * 512],
                   t_ones[0:1, 0:128], t_bvrow[0:1, half * 512:(half + 1) * 512],
                   start=False, stop=True, sem=s_t)
            W("scalar", s_t, s_t.n)
            for hh in range(NH):
                OP("scalar", "copy",
                   t_vaug[:, kc * VSTR + hh * 65:kc * VSTR + hh * 65 + 64],
                   pp[:, hh * 64:(hh + 1) * 64], sem=(s_s if hh == NH - 1 else None))
            kb.set_bank(pp, s_s)
        T_V = s_t.n
        S_VE = s_s.n

        cut(3)
        ph3 = reg(ExitStack())
        t_kst = t_rop
        t_kst2 = t_ckv[:, 3072:5120]
        kstb = [t_kst, t_kst2]
        t_exp2 = [t_ckv[:, 1024:2048], t_ckv[:, 2048:3072],
                  t_ckv[:, 5120:6144], t_ckv[:, 6144:7168]]
        t_nrm = t_ckv[:, 0:512]
        t_atmp = t_ckv[:, 512:1024]

        W("tensor", s_v, V_Q)
        W("tensor", s_s, S_VE)
        W("tensor", s_kr2, MV_KR)
        exp_done = [kb.bank.get(id(p_e), (s_s, 0))[1], kb.bank.get(id(p_f), (s_s, 0))[1]]
        nrm_used = 0
        expmm_done = [0, 0, 0, 0]
        attn_ev = kb.bank.get(id(p_c), (s_v, 0))
        nrm_ev = 0
        W("vector", s_kr2, MV_KR)
        W("vector", s_s, S_VE)
        kst_free = [0, 0]
        kst_ready = [0, 0]

        def stage_kst(hh):
            par = hh % 2
            kd = hh // 2
            qc_base = 64 * par
            qr_base = 64 * (1 - par)
            kb_ = kstb[hh % 2]
            if kst_free[hh % 2]:
                W("vector", s_t, kst_free[hh % 2])
            OP("vector", "tensor_copy", kb_[qc_base:qc_base + 64, :],
               t_kall[qc_base:qc_base + 64, kd * S:(kd + 1) * S])
            OP("vector", "tensor_copy", kb_[qr_base:qr_base + 64, :],
               t_kr2[qr_base:qr_base + 64, :], sem=s_v)
            kst_ready[hh % 2] = s_v.n

        stage_kst(0)
        for hh in range(NH):
            par = hh % 2
            kd = hh // 2
            qc_base = 64 * par
            qr_base = 64 * (1 - par)
            if hh + 1 < NH:
                stage_kst(hh + 1)
            t_ksc = kstb[hh % 2]
            W("tensor", s_v, kst_ready[hh % 2])
            for qb in range(2):
                pv = [p_c, p_d][(hh * 2 + qb) % 2]
                for kk in range(NKC // 2):
                    if kk in (2, 5):
                        pw_steps(1)
                    sl = kk % 2
                    s4 = (qb * 8 + kk) % 4
                    pp = [p_e, p_f][sl]
                    W("tensor", s_s, exp_done[sl])
                    for j in range(2):
                        kc = 2 * kk + j
                        OP("tensor", "matmul", pp[:, j * 512:(j + 1) * 512],
                           t_ksc[:, kc * 128:(kc + 1) * 128],
                           t_qall[:, hh * T + qb * 512:hh * T + (qb + 1) * 512],
                           start=True, stop=True, sem=(s_t if j == 1 else None))
                    sc_val = s_t.n
                    W("scalar", s_t, sc_val)
                    if expmm_done[s4]:
                        W("scalar", s_t, expmm_done[s4])
                    OP("scalar", "activation", t_exp2[s4], pp[:, :], AF.Exp,
                       scale=t_rtmp[:, 5:6], sem=s_s)
                    exp_done[sl] = s_s.n
                    kb.set_bank(pp, s_s)
                    W("tensor", s_s, exp_done[sl])
                    if kk == 0:
                        kb.wait_bank("tensor", pv)
                    for j in range(2):
                        kc = 2 * kk + j
                        OP("tensor", "matmul", pv[0:65, :],
                           t_vaug[:, kc * VSTR + hh * 65:kc * VSTR + (hh + 1) * 65],
                           t_exp2[s4][:, j * 512:(j + 1) * 512],
                           start=(kc == 0), stop=(kc == NKC - 1),
                           sem=(s_t if kc == NKC - 1 else None))
                    expmm_done[s4] = s_t.n
                T_ATT = s_t.n
                kst_free[hh % 2] = sc_val
                pw_steps(1)
                W("vector", s_t, T_ATT)
                OP("vector", "reciprocal", t_row1[0:1, 0:512], pv[64:65, :], sem=s_v)
                W("tensor", s_v, s_v.n)
                if nrm_used:
                    W("tensor", s_s, nrm_used)
                else:
                    kb.wait_bank("tensor", p_a)
                OP("tensor", "matmul", p_a[0:64, 0:512], t_onesF[0:1, 0:64],
                   t_row1[0:1, 0:512], start=True, stop=True, sem=s_t)
                W("scalar", s_t, s_t.n)
                if nrm_ev:
                    W("scalar", s_v, nrm_ev)
                OP("scalar", "copy", t_nrm[0:64, 0:512], p_a[0:64, 0:512], sem=s_s)
                nrm_used = s_s.n
                W("vector", s_s, s_s.n)
                if par == 0:
                    OP("vector", "tensor_tensor",
                       t_attn[0:64, kd * T + qb * 512:kd * T + (qb + 1) * 512],
                       pv[0:64, :], t_nrm[0:64, 0:512], ALU.mult, sem=s_v)
                else:
                    W("vector", s_ash, s_ash.n)
                    OP("vector", "tensor_tensor", t_atmp[0:64, 0:512],
                       pv[0:64, :], t_nrm[0:64, 0:512], ALU.mult, sem=s_v)
                    W("sync", s_v, s_v.n)
                    kb.dma("sync", t_attn[64:128, kd * T + qb * 512:kd * T + (qb + 1) * 512],
                           t_atmp[0:64, 0:512], s_ash)
                nrm_ev = s_v.n
                kb.set_bank(pv, s_v)
        T_ATT_ALL = s_t.n
        V_ATT_ALL = s_v.n
        MV_ATT = s_ash.n
        kb.set_bank(p_a, s_s)

        cut(4)
        if DEBUG:
            W("sync", s_v, V_ATT_ALL)
            W("sync", s_ash, MV_ATT)
            W("sync", s_t, T_ATT_ALL)
            kb.dma("sync", d_dbg_attn[:, :], t_attn[:, :], s_mv)
            kb.dma("sync", d_dbg_q[:, :], t_qall[:, :], s_mv)
            kb.dma("sync", d_dbg_k[:, :], t_kall[:, :], s_mv)
            kb.dma("sync", d_dbg_kr[:, :], t_kr2[:, :], s_mv)
            kb.dma("sync", d_dbg_v[:, :], t_vaug[:, :], s_mv)
            kb.dma("sync", d_dbg_ckv[:, :], t_ckv[:, :], s_mv)
            kb.dma("sync", d_dbg_cq[:, :], t_cq[:, :], s_mv)
            W("tensor", s_mv, s_mv.n)
            W("gpsimd", s_mv, s_mv.n)
            W("scalar", s_mv, s_mv.n)
            W("vector", s_mv, s_mv.n)

        unreg(ph3)
        unreg(phB)
        ph35 = reg(ExitStack())
        t_wfc = ph35.enter_context(nc.sbuf_tensor("wfcA", [128, 8 * 2048], BF16))
        t_fcst = [ph35.enter_context(nc.sbuf_tensor(f"fcst{i}", [128, 512], F32)) for i in range(2)]
        t_bfc = ph35.enter_context(nc.sbuf_tensor("bfcr", [1, H], BF16))
        W("gpsimd", s_t, T_ATT_ALL)
        kb.dma("gpsimd", t_bfc[0:1, :], d_bfc[0:1, :], s_tb)
        W("tensor", s_tb, s_tb.n)
        AW = kb.dma("gpsimd", t_wfc[:, :].rearrange("p (f d) -> p f d", d=2048),
                    d_wfcT.rearrange("(f p) d -> p f d", p=128), s_wa)
        W("tensor", s_wa, AW)
        W("tensor", s_v, V_ATT_ALL)
        W("tensor", s_ash, MV_ATT)
        for tc in range(NTC):
            for nb in range(4):
                pp = [p_a, p_b][nb % 2]
                kb.wait_bank("tensor", pp)
                for vc in range(8):
                    OP("tensor", "matmul", pp[:, :],
                       t_attn[:, vc * T + tc * 128:vc * T + (tc + 1) * 128],
                       t_wfc[:, vc * 2048 + nb * 512:vc * 2048 + (nb + 1) * 512],
                       start=(vc == 0), stop=False)
                OP("tensor", "matmul", pp[:, :], t_ones[0:1, 0:128],
                   t_bfc[0:1, nb * 512:(nb + 1) * 512], start=False, stop=True, sem=s_t)
                W("scalar", s_t, s_t.n)
                fsl = nb % 2
                pw_steps(1)
                if tc * 4 + nb >= 2:
                    W("scalar", s_fcw, s_fcw.n)
                OP("scalar", "copy", t_fcst[fsl][:, :], pp[:, :], sem=s_s)
                kb.set_bank(pp, s_s)
                W("scalar", s_s, s_s.n)
                kb.dma("scalar", d_fc_sc[tc * 128:(tc + 1) * 128, nb * 512:(nb + 1) * 512],
                       t_fcst[fsl][:, :], s_fcw, drain=True)
        FCW_ALL = s_fcw.n
        S_FC = s_s.n

        unreg(ph35)
        unreg(ph2)

        cut(5)
        pw_steps(400)
        WQW_ALL = 0


        cut(6)
        ph45 = reg(ExitStack())
        t_wd8 = ph45.enter_context(nc.sbuf_tensor("wd8", [128, NIC * 2048], FP8))
        t_wdT = [ph45.enter_context(nc.sbuf_tensor(f"wdT{i}", [128, H], BF16,
                                                   is_transpose_dest=True)) for i in range(2)]
        ph4 = reg(ExitStack())
        sb4 = lambda n, shape, dt, **kw: ph4.enter_context(nc.sbuf_tensor(n, shape, dt, **kw))
        ISB = 4
        NSB = NIC // ISB
        t_wsbB = [sb4(f"wsbB{i}", [128, 512], BF16, is_transpose_dest=True) for i in range(4)]
        t_w8g = [sb4(f"w8g{i}", [128, NHC * 512], FP8) for i in range(2)]
        t_w8u = [sb4(f"w8u{i}", [128, NHC * 512], FP8) for i in range(2)]
        t_sg = sb4("sgb", [128, T], F32)
        t_su = sb4("sub", [128, T], F32)
        t_tb = [sb4(f"t_st{i}", [128, T], F32) for i in range(2)]
        t_tmx = sb4("tmx", [128, T], F32)
        t_tmn = sb4("tmn", [128, T], F32)
        t_qxfm = sb4("qx_fm", [128, NHC * T], FP8)
        t_tfacb = sb4("tfacb", [128, T], F32)

        W("gpsimd", s_s, S_FC)
        for _i in range(2):
            W("gpsimd", s_qxst[_i], QXW_ALL[_i])
        kb.dma("gpsimd", t_qxfm[:, :].rearrange("p (f d) -> p f d", d=T),
               d_qxf_sc.rearrange("(f p) d -> p f d", p=128), s_qxl)
        QXL = s_qxl.n

        W("sync", s_t, T_ATT_ALL)
        W("sync", s_mv, RS1_ST)
        for half in range(2):
            kb.dma("sync", t_row1[0:1, :],
                   d_rs1[:, half * 4:half * 4 + 4].rearrange("p c -> c p"), s_mv,
                   allow_slow_non_contiguous=True)
            W("vector", s_mv, s_mv.n)
            W("vector", s_t, T_ATT_ALL)
            OP("vector", "tensor_scalar", t_row1[0:1, :], t_row1[0:1, :], 4.0 / 448.0, None,
               ALU.mult, sem=s_v)
            W("tensor", s_v, s_v.n)
            pp = psum_ro[pidx[0] % 4]
            pidx[0] += 1
            kb.wait_bank("tensor", pp)
            OP("tensor", "matmul", pp[:, :], t_onesF[0:1, 0:128],
               t_row1[0:1, :], start=True, stop=True, sem=s_t)
            W("scalar", s_t, s_t.n)
            OP("scalar", "copy", t_tfacb[:, half * 512:(half + 1) * 512], pp[:, :], sem=s_s)
            kb.set_bank(pp, s_s)
            W("sync", s_t, s_t.n)
        V_TFACB = s_s.n

        W("sync", s_fcw, FCW_ALL)
        W("gpsimd", s_fcw, FCW_ALL)
        W("vector", s_fcw, FCW_ALL)
        W("tensor", s_qxl, QXL)
        W("tensor", s_s, V_TFACB)

        vqt_done = {}
        wdT_done = [0, 0]
        cast_done = [0, 0, 0, 0]
        w8_free = [0, 0]
        tmax_init = [False]
        w8_ready = [0, 0]

        def stage_w8(sbi):
            sl = sbi % 2
            need = [0, 0]
            for cc in range(sbi * ISB, (sbi + 1) * ISB):
                for tbl in (pw_wq_gate, pw_wq_up):
                    wsl, wv = tbl[cc]
                    need[wsl] = max(need[wsl], wv)
            for wsl in range(2):
                if need[wsl]:
                    W("sync", pw_wqw[wsl], need[wsl])
            if w8_free[sl]:
                W("scalar", s_t, w8_free[sl])
            for mat, w8 in ((0, t_w8g), (1, t_w8u)):
                dsc = (d_wg_sc, d_wu_sc)[mat]
                for hc in range(NHC):
                    tsl = (mat * NHC + hc) % 4
                    if cast_done[tsl]:
                        W("sync", s_s, cast_done[tsl])
                    kb.dmaT("sync", t_wsbB[tsl][:, :],
                            dsc[sbi * 512:(sbi + 1) * 512, hc * 128:(hc + 1) * 128], s_wt[tsl])
                    W("scalar", s_wt[tsl], s_wt[tsl].n)
                    OP("scalar", "copy", w8[sl][:, hc * 512:(hc + 1) * 512],
                       t_wsbB[tsl][:, :], sem=s_s)
                    cast_done[tsl] = s_s.n
            w8_ready[sl] = s_s.n

        stage_w8(0)
        for sbi in range(NSB):
            sl = sbi % 2
            if sbi + 1 < NSB:
                stage_w8(sbi + 1)
            W("tensor", s_s, w8_ready[sl])
            for ci in range(ISB):
                c = sbi * ISB + ci
                for pp, w8 in ((p_e, t_w8g), (p_f, t_w8u)):
                    kb.wait_bank("tensor", pp)
                    for half in range(2):
                        for hc in range(NHC):
                            OP("tensor", "matmul", pp[:, half * 512:(half + 1) * 512],
                               w8[sl][:, hc * 512 + ci * 128:hc * 512 + (ci + 1) * 128],
                               t_qxfm[:, hc * T + half * 512:hc * T + (half + 1) * 512],
                               start=(hc == 0), stop=(hc == NHC - 1),
                               sem=(s_t if (hc == NHC - 1 and half == 1) else None))
                T_GU = s_t.n
                W("vector", s_t, T_GU)
                OP("vector", "tensor_tensor", t_sg[:, :], p_e[:, :], t_tfacb[:, :], ALU.mult,
                   sem=s_v)
                W("scalar", s_v, s_v.n)
                OP("scalar", "activation", t_sg[:, :], t_sg[:, :], AF.Silu,
                   scale=t_gfac[:, c:c + 1], sem=s_s)
                W("vector", s_s, s_s.n)
                tsl = c % 2
                if c >= 2:
                    W("vector", s_tw, s_tw.n)
                OP("vector", "tensor_tensor", t_su[:, :], p_f[:, :], t_tfacb[:, :], ALU.mult)
                OP("vector", "scalar_tensor_tensor", t_tb[tsl][:, :],
                   t_su[:, :], t_ufac[:, c:c + 1], t_sg[:, :], ALU.mult, ALU.mult)
                if not tmax_init[0]:
                    OP("vector", "tensor_copy", t_tmx[:, :], t_tb[tsl][:, :])
                    OP("vector", "tensor_copy", t_tmn[:, :], t_tb[tsl][:, :], sem=s_v)
                    tmax_init[0] = True
                else:
                    OP("vector", "tensor_tensor", t_tmx[:, :], t_tmx[:, :],
                       t_tb[tsl][:, :], ALU.max)
                    OP("vector", "tensor_tensor", t_tmn[:, :], t_tmn[:, :],
                       t_tb[tsl][:, :], ALU.min, sem=s_v)
                kb.set_bank(p_e, s_v)
                kb.set_bank(p_f, s_v)
                W("scalar", s_v, s_v.n)
                kb.dma("scalar", d_t_sc[c * 128:(c + 1) * 128, :], t_tb[tsl][:, :], s_tw)
            w8_free[sl] = s_t.n
            if sbi == 0:
                need_wd = [0, 0]
                for cc in range(NHC):
                    wsl_c, wv_c = pw_wq_wd[cc]
                    need_wd[wsl_c] = max(need_wd[wsl_c], wv_c)
                for wsl_c in range(2):
                    if need_wd[wsl_c]:
                        W("sync", pw_wqw[wsl_c], need_wd[wsl_c])
                W("sync", s_s, S_FC)
            for c in range(sbi * 4, sbi * 4 + 4):
                dsl = c % 2
                if wdT_done[dsl]:
                    W("sync", s_s, wdT_done[dsl])
                kb.dmaT("sync", t_wdT[dsl][:, :], d_wd_sc[:, c * 128:(c + 1) * 128], s_wdt[dsl])
                W("scalar", s_wdt[dsl], s_wdt[dsl].n)
                OP("scalar", "copy", t_wd8[:, c * 2048:(c + 1) * 2048], t_wdT[dsl][:, :],
                   sem=s_s)
                wdT_done[dsl] = s_s.n
        G_WD8 = s_s.n
        V_T_ALL = s_v.n
        TW_ALL = s_tw.n

        W("tensor", s_g, G_ID)
        OP("vector", "tensor_scalar", t_tmn[:, :], t_tmn[:, :], -1.0, None, ALU.mult)
        OP("vector", "tensor_tensor", t_tmx[:, :], t_tmx[:, :], t_tmn[:, :], ALU.max, sem=s_v)
        V_TMX = s_v.n
        W("tensor", s_v, V_TMX)
        for tb in range(NTC):
            pp = [p_a, p_b][tb % 2]
            kb.wait_bank("tensor", pp)
            OP("tensor", "transpose", pp[:, 0:128], t_tmx[:, tb * 128:(tb + 1) * 128], t_id[:, :],
               sem=s_t)
            W("vector", s_t, s_t.n)
            OP("vector", "tensor_reduce", t_tamr[:, tb:tb + 1], pp[:, 0:128], AX.X, ALU.max, sem=s_v)
            kb.set_bank(pp, s_v)
        V_TAMR = s_v.n
        T_AMX = s_t.n
        W("sync", s_v, V_TAMR)
        kb.dma("sync", d_rs1[:, 0:NTC], t_tamr[:, 0:NTC], s_mv)
        W("sync", s_mv, s_mv.n)
        unreg(ph4)
        unreg(ph_qx)

        cut(7)
        ph5 = reg(ExitStack())
        sb5 = lambda n, shape, dt, **kw: ph5.enter_context(nc.sbuf_tensor(n, shape, dt, **kw))
        t_qsclb = sb5("qsclb", [128, T], F32)
        t_qgf = [sb5(f"qgf{i}", [128, T], F32) for i in range(2)]
        t_qperm = sb5("qperm", [128, NIC * T], FP8)
        t_swdb = sb5("swdb", [128, H], F32)
        t_ho = sb5("hor", [128, H], F32)
        t_fcb = sb5("fcbr", [128, H], F32)
        t_outb = sb5("outbr", [128, H], F32)

        for half in range(2):
            kb.dma("sync", t_row1[0:1, :],
                   d_rs1[:, half * 4:half * 4 + 4].rearrange("p c -> c p"), s_mv,
                   allow_slow_non_contiguous=True)
            W("vector", s_mv, s_mv.n)
            OP("vector", "reciprocal", t_row1[0:1, :], t_row1[0:1, :])
            OP("vector", "tensor_scalar", t_row1[0:1, :], t_row1[0:1, :], 224.0, None,
               ALU.mult, sem=s_v)
            W("tensor", s_v, s_v.n)
            pp = [p_a, p_b][half]
            kb.wait_bank("tensor", pp)
            OP("tensor", "matmul", pp[:, :], t_onesF[0:1, 0:128],
               t_row1[0:1, :], start=True, stop=True, sem=s_t)
            W("vector", s_t, s_t.n)
            OP("vector", "tensor_copy", t_qsclb[:, half * 512:(half + 1) * 512], pp[:, :], sem=s_v)
            kb.set_bank(pp, s_v)
            W("sync", s_t, s_t.n)
        V_QSCLB = s_v.n

        kb.dma("sync", d_rs1[:, 0:NHC], t_dfac[:, 0:NHC], s_mv)
        W("sync", s_mv, s_mv.n)
        for q4 in range(4):
            kb.dma("sync", t_row1[0:1, :],
                   d_rs1[:, q4 * 4:q4 * 4 + 4].rearrange("p c -> c p"), s_mv,
                   allow_slow_non_contiguous=True)
            W("tensor", s_mv, s_mv.n)
            pp = [p_a, p_b][q4 % 2]
            kb.wait_bank("tensor", pp)
            OP("tensor", "matmul", pp[:, :], t_onesF[0:1, 0:128],
               t_row1[0:1, :], start=True, stop=True, sem=s_t)
            W("vector", s_t, s_t.n)
            OP("vector", "tensor_copy", t_swdb[:, q4 * 512:(q4 + 1) * 512], pp[:, :], sem=s_v)
            kb.set_bank(pp, s_v)
            W("sync", s_t, s_t.n)
        V_SWDB = s_v.n

        W("gpsimd", s_tw, TW_ALL)
        W("gpsimd", s_t, T_AMX)
        W("gpsimd", s_v, V_TAMR)
        W("vector", s_tw, TW_ALL)
        W("vector", s_t, T_AMX)
        vqt = {}
        gat = {}
        for c in range(NIC):
            sl = c % 2
            if c >= 2:
                W("gpsimd", s_v, vqt[c - 2])
            kb.gather(t_qgf[sl][:, :], d_t_sc[:, :], t_didx[:, c:c + 1], s_qtg[sl])
            gat[c] = s_qtg[sl].n
            W("vector", s_qtg[sl], gat[c])
            OP("vector", "tensor_tensor", t_qperm[:, c * T:(c + 1) * T],
               t_qgf[sl][:, :], t_qsclb[:, :], ALU.mult, sem=s_v)
            vqt[c] = s_v.n
        V_QTT_ALL = s_v.n

        W("tensor", s_s, G_WD8)
        W("tensor", s_v, V_SWDB)
        W("scalar", s_v, V_QTT_ALL)
        for tc in range(NTC):
            if tc >= 1:
                W("scalar", s_out, s_out.n)
            W("scalar", s_fcw, FCW_ALL)
            kb.dma("scalar", t_ho[:, :], d_h_own[tc * 128:(tc + 1) * 128, :], s_hor)
            kb.dma("scalar", t_fcb[:, :], d_fc_sc[tc * 128:(tc + 1) * 128, :], s_fcr)
            for nb in range(4):
                pp = [p_c, p_d][nb % 2]
                kb.wait_bank("tensor", pp)
                for c in range(NIC):
                    if tc == 0 and nb == 0:
                        W("tensor", s_v, vqt[c])
                    OP("tensor", "matmul", pp[:, :],
                       t_qperm[:, c * T + tc * 128:c * T + (tc + 1) * 128],
                       t_wd8[:, c * 2048 + nb * 512:c * 2048 + (nb + 1) * 512],
                       start=(c == 0), stop=(c == NIC - 1), sem=(s_t if c == NIC - 1 else None))
                T_DN = s_t.n
                W("vector", s_t, T_DN)
                W("vector", s_hor, s_hor.n)
                W("vector", s_fcr, s_fcr.n)
                if nb == 0:
                    OP("vector", "tensor_scalar", t_rtmp[:, 2:3], t_tamr[:, tc:tc + 1],
                       4.0 / 448.0, None, ALU.mult)
                OP("vector", "tensor_scalar", t_outb[:, nb * 512:(nb + 1) * 512], pp[:, :],
                   t_rtmp[:, 2:3], None, ALU.mult)
                OP("vector", "tensor_tensor", t_outb[:, nb * 512:(nb + 1) * 512],
                   t_outb[:, nb * 512:(nb + 1) * 512], t_swdb[:, nb * 512:(nb + 1) * 512], ALU.mult)
                OP("vector", "tensor_tensor", t_outb[:, nb * 512:(nb + 1) * 512],
                   t_outb[:, nb * 512:(nb + 1) * 512], t_fcb[:, nb * 512:(nb + 1) * 512], ALU.add)
                OP("vector", "tensor_tensor", t_outb[:, nb * 512:(nb + 1) * 512],
                   t_outb[:, nb * 512:(nb + 1) * 512], t_ho[:, nb * 512:(nb + 1) * 512], ALU.add,
                   sem=s_v)
                kb.set_bank(pp, s_v)
            W("scalar", s_v, s_v.n)
            kb.dma("scalar", d_out[tc * 128:(tc + 1) * 128, :], t_outb[:, :], s_out)
        OUT_ALL = s_out.n
        unreg(ph5)
        unreg(ph45)

    except _Cut:
        pass
    for _st in reversed(_stacks):
        _st.close()
    if s_out.n:
        W("gpsimd", s_out, s_out.n)

    blk = nc.Block()
    with blk:
        kb.flush(blk)
    es.close()
    p_es.close()
    return nc


_NC_CACHE = {}


def _get_nc():
    if "nc" not in _NC_CACHE:
        _NC_CACHE["nc"] = build_nc()
    return _NC_CACHE["nc"]


def _host_prep(inputs):
    x = np.ascontiguousarray(np.asarray(inputs["L0_attn_h"], dtype=np.float32))

    wupq = np.asarray(inputs["L0_attn_W_up_q"], dtype=np.float32)
    wqr = np.asarray(inputs["L0_attn_W_qr"], dtype=np.float32)
    bupq = np.asarray(inputs["L0_attn_b_up_q"], dtype=np.float32)
    bqr = np.asarray(inputs["L0_attn_b_qr"], dtype=np.float32)

    deint = np.concatenate([np.arange(0, RD, 2), np.arange(1, RD, 2)])

    wq_cols, bq_cols = [], []
    for h in range(NH):
        qc = wupq[h * 64:(h + 1) * 64, :].T
        qr = wqr[h * 64 + deint, :].T
        bc = bupq[h * 64:(h + 1) * 64]
        br = bqr[h * 64 + deint]
        if h % 2 == 0:
            wq_cols += [qc, qr]
            bq_cols += [bc, br]
        else:
            wq_cols += [qr, qc]
            bq_cols += [br, bc]
    wqcomb = np.ascontiguousarray(np.concatenate(wq_cols, axis=1), dtype=np.float32)
    bqcomb = np.ascontiguousarray(np.concatenate(bq_cols), dtype=np.float32)

    wkr = np.asarray(inputs["L0_attn_W_kr"], dtype=np.float32)
    wkrT = np.ascontiguousarray(wkr[deint, :].T)
    bkr = np.ascontiguousarray(np.asarray(inputs["L0_attn_b_kr"], dtype=np.float32)[deint])

    common = {
        "wdkvT": np.ascontiguousarray(np.asarray(inputs["L0_attn_W_down_kv"], np.float32).T),
        "wdqT": np.ascontiguousarray(np.asarray(inputs["L0_attn_W_down_q"], np.float32).T),
        "wupkT": np.ascontiguousarray(np.asarray(inputs["L0_attn_W_up_k"], np.float32).T),
        "wupvT": np.ascontiguousarray(np.asarray(inputs["L0_attn_W_up_v"], np.float32).T),
        "wqcomb": wqcomb,
        "wkrT": wkrT,
        "wfcT": np.ascontiguousarray(np.asarray(inputs["L0_attn_W_fc"], np.float32).T),
        "bdkv": np.ascontiguousarray(np.asarray(inputs["L0_attn_b_down_kv"], np.float32)),
        "bupk": np.ascontiguousarray(np.asarray(inputs["L0_attn_b_up_k"], np.float32)),
        "bupv": np.ascontiguousarray(np.asarray(inputs["L0_attn_b_up_v"], np.float32)[None, :]),
        "bdq": np.ascontiguousarray(np.asarray(inputs["L0_attn_b_down_q"], np.float32)),
        "bqcomb": bqcomb,
        "bkr": bkr,
        "bfc": np.ascontiguousarray(np.asarray(inputs["L0_attn_b_fc"], np.float32)[None, :]),
        "wg_nat": np.ascontiguousarray(np.asarray(inputs["L0_ffn_w_gate"], np.float32)),
        "wu_nat": np.ascontiguousarray(np.asarray(inputs["L0_ffn_w_up"], np.float32)),
        "wd_nat": np.ascontiguousarray(np.asarray(inputs["L0_ffn_w_down"], np.float32)),
        "didx": np.ascontiguousarray(np.asarray(inputs["L0_ffn_down_reorder_index"], np.int32)),
    }

    theta = (1.0 / (10000.0 ** (np.arange(0, RD, 2, dtype=np.float32) / np.float32(RD)))).astype(np.float32)

    in_maps = []
    for core in range(8):
        b, half = divmod(core, 2)
        own = slice(half * T, (half + 1) * T)
        oth = slice((1 - half) * T, (2 - half) * T)
        pos = np.concatenate([np.arange(half * T, (half + 1) * T, dtype=np.float32),
                              np.arange((1 - half) * T, (2 - half) * T, dtype=np.float32)])
        ang = (pos[:, None] * theta[None, :]).astype(np.float32)
        sin_t = np.sin(ang).astype(np.float32).T
        cos_t = np.cos(ang).astype(np.float32).T
        tabS = np.ascontiguousarray(np.tile(sin_t, (4, 1)).astype(np.float32))
        tabC = np.ascontiguousarray(
            np.concatenate([-cos_t, cos_t, -cos_t, cos_t], axis=0).astype(np.float32))
        m = dict(common)
        m["h_own"] = np.ascontiguousarray(x[b, own, :])
        m["h_oth"] = np.ascontiguousarray(x[b, oth, :])
        m["tabS"] = tabS
        m["tabCsgn"] = tabC
        in_maps.append(m)
    return in_maps


def kernel(**inputs):
    in_maps = _host_prep(inputs)
    nc = _get_nc()
    res = run_bass_kernel_spmd(nc, in_maps, core_ids=list(range(8)))
    out = np.empty((4, 2048, 2048), dtype=np.float32)
    for core in range(8):
        b, half = divmod(core, 2)
        out[b, half * T:(half + 1) * T, :] = res.results[core]["y"]
    return out
